# version 05
# baseline (speedup 1.0000x reference)
import numpy as np
import concourse.bass as bass
import concourse.mybir as mybir
from concourse.bass_utils import run_bass_kernel_spmd

F32 = mybir.dt.float32
BF16 = mybir.dt.bfloat16
I32 = mybir.dt.int32
AF = mybir.ActivationFunctionType
ALU = mybir.AluOpType
AX = mybir.AxisListType

B, L, HID = 2, 4096, 4096
NCORES = 8
T = 1024
NT = 8
NCH = 8
NK = 32
H, KVH, D = 32, 8, 128
ROPE_THETA = 10000.0
RMS_EPS = 1e-5
LN_EPS = 1e-5
FFN_T = 0.1
FFN_TH = 0.01
ALLT = B * L
NEG_BIG = -1.0e30
DEBUG = False


class Prog:
    ENGINES = ("t", "v", "s", "g", "q")
    WAIT_ALL = ("CONSTg", "CONSTq")

    def __init__(self):
        self.ops = {e: [] for e in self.ENGINES}
        self.cnt = {e: 0 for e in self.ENGINES}
        self.bufs = {}

    def _bump(self, key, inc):
        self.cnt[key] = self.cnt.get(key, 0) + inc

    def _deps(self, reads, writes, extra):
        deps = []
        for b in reads:
            st = self.bufs.get(b)
            if st is not None and st[0] is not None:
                deps.append(st[0])
        for b in writes:
            st = self.bufs.get(b)
            if st is not None:
                if st[0] is not None:
                    deps.append(st[0])
                deps.extend(st[1])
        for d in extra:
            if d is not None:
                deps.append(d)
        best = {}
        for (e, v) in deps:
            if best.get(e, -1) < v:
                best[e] = v
        return list(best.items())

    def emit(self, eng, fn, reads=(), writes=(), extra_deps=(), inc=None,
             sem=None):
        key = sem if sem is not None else eng
        if inc is None:
            inc = 16 if key not in self.ENGINES else (
                0 if eng in ("g", "q") else 1)
            if eng not in ("g", "q") and key in self.ENGINES:
                inc = 1
        deps = self._deps(reads, writes, extra_deps)
        self._bump(key, inc)
        tok = (key, self.cnt[key])
        self.ops[eng].append((fn, deps, inc, key))
        for b in writes:
            self.bufs[b] = [tok, []]
        for b in reads:
            self.bufs.setdefault(b, [None, []])[1].append(tok)
        return tok

    def group(self, eng, fns, reads=(), writes=(), extra_deps=()):
        deps = self._deps(reads, writes, extra_deps)
        inc = 1
        for i, fn in enumerate(fns):
            self.ops[eng].append((fn, deps if i == 0 else [],
                                  inc if i == len(fns) - 1 else 0, eng))
        self._bump(eng, inc)
        tok = (eng, self.cnt[eng])
        for b in writes:
            self.bufs[b] = [tok, []]
        for b in reads:
            self.bufs.setdefault(b, [None, []])[1].append(tok)
        return tok

    def fence(self):
        snap = dict(self.cnt)
        for e in self.ENGINES:
            deps = [(k, v) for k, v in snap.items() if k != e and v > 0]
            self.emit(e, lambda en: en.nop(), extra_deps=deps, inc=0)

    def replay(self, nc, es, block):
        sems = {}
        for k in self.cnt:
            sems[k] = es.enter_context(nc.semaphore(f"sm_{k}"))
        totals = dict(self.cnt)

        def body(eng_name):
            def run(en):
                waited = {}
                for (fn, deps, inc, key) in self.ops[eng_name]:
                    for (dk, dv) in deps:
                        if dk == eng_name and eng_name == "t":
                            continue
                        if dk in self.WAIT_ALL:
                            dv = totals[dk]
                        if waited.get(dk, 0) >= dv:
                            continue
                        en.wait_ge(sems[dk], dv)
                        waited[dk] = dv
                    inst = fn(en)
                    if inc > 0:
                        inst.then_inc(sems[key], inc)
                if eng_name == "q":
                    for k, tv in totals.items():
                        if k not in self.ENGINES and tv > 0 and \
                                waited.get(k, 0) < tv:
                            en.wait_ge(sems[k], tv)
                elif totals.get(eng_name, 0) > 0:
                    en.wait_ge(sems[eng_name], totals[eng_name])
            return run

        block.tensor(body("t"))
        block.vector(body("v"))
        block.scalar(body("s"))
        block.gpsimd(body("g"))
        block.sync(body("q"))


PB = [f"pb{i}" for i in range(8)]


def build_program():
    nc = bass.Bass("TRN2", target_bir_lowering=False)

    dx = nc.dram_tensor("x_own", [T, HID], F32, kind="ExternalInput").ap()
    dxw = nc.dram_tensor("x_win", [T, HID], F32, kind="ExternalInput").ap()
    dxT = nc.dram_tensor("xT_own", [HID, T], F32, kind="ExternalInput").ap()
    dxTall = nc.dram_tensor("xT_all", [HID, ALLT], F32, kind="ExternalInput").ap()
    dWq = nc.dram_tensor("Wq_blk", [32, 128, HID], F32, kind="ExternalInput").ap()
    dWk = nc.dram_tensor("Wk_blk", [8, 128, HID], F32, kind="ExternalInput").ap()
    dWv = nc.dram_tensor("Wv_blk", [8, 128, HID], F32, kind="ExternalInput").ap()
    dWg = nc.dram_tensor("Wg_blk", [32, 128, HID], F32, kind="ExternalInput").ap()
    dWo = nc.dram_tensor("Wo_blk", [32, 128, HID], F32, kind="ExternalInput").ap()
    dWf = nc.dram_tensor("Wf_blk", [32, 128, HID], F32, kind="ExternalInput").ap()
    dWin = nc.dram_tensor("Win_t", [HID, 32], F32, kind="ExternalInput").ap()
    dCWO = nc.dram_tensor("CWO", [HID, 65], F32, kind="ExternalInput").ap()
    dfb = nc.dram_tensor("ffn_b", [128, 32], F32, kind="ExternalInput").ap()
    ddtb = nc.dram_tensor("dt_b", [32, 1], F32, kind="ExternalInput").ap()
    dAlog = nc.dram_tensor("A_log", [32, 1], F32, kind="ExternalInput").ap()
    dgn = nc.dram_tensor("gnorm", [128, 1], F32, kind="ExternalInput").ap()
    dassign = nc.dram_tensor("assign", [1, HID], I32, kind="ExternalInput").ap()
    dI = nc.dram_tensor("ident", [128, 128], F32, kind="ExternalInput").ap()
    dS_T = nc.dram_tensor("S_T", [128, 128], F32, kind="ExternalInput").ap()
    dmask = nc.dram_tensor("maskneg4", [128, 512], F32, kind="ExternalInput").ap()
    dcos_o = nc.dram_tensor("cos_own", [128, T], F32, kind="ExternalInput").ap()
    dsin_o = nc.dram_tensor("sin_own", [128, T], F32, kind="ExternalInput").ap()
    dcos_w = nc.dram_tensor("cos_win", [128, T], F32, kind="ExternalInput").ap()
    dsin_w = nc.dram_tensor("sin_win", [128, T], F32, kind="ExternalInput").ap()
    diota = nc.dram_tensor("iota64_in", [64, 1], F32, kind="ExternalInput").ap()
    doff = nc.dram_tensor("own_off", [1, 1], I32, kind="ExternalInput").ap()

    dout = nc.dram_tensor("out", [HID, T], F32, kind="ExternalOutput").ap()
    if DEBUG:
        ddbg = {n: nc.dram_tensor(f"dbg_{n}", sh, F32, kind="ExternalOutput").ap()
                for n, sh in [("dmax", [64, 1]), ("selc", [64, 1]),
                              ("rowselc", [128, 32]), ("qselB", [128, T]),
                              ("biasrs", [128, 32]), ("qrowf", [1, T]),
                              ("states", [128, H * 128]),
                              ("dtsb", [128, 256]), ("decB", [128, 256]),
                              ("negG", [128, 256]), ("ekd", [128, 256])]}

    sGd = nc.dram_tensor("Gdram", [32, T], F32)
    sEGb = nc.dram_tensor("EGbf", [32, T], BF16)
    sdec = nc.dram_tensor("decdram", [32, 8], BF16)
    sqsp = nc.dram_tensor("qspill", [NCH, 128, H, 128], BF16)
    ssgsp = nc.dram_tensor("sgspill", [NCH, 128, H, 128], BF16)
    sksp = nc.dram_tensor("kspill", [NCH, 128, KVH, 128], BF16)
    svsp = nc.dram_tensor("vspill", [NCH, 128, KVH, 128], BF16)
    sksp_w = nc.dram_tensor("kspill_w", [NCH, 128, KVH, 128], BF16)
    svsp_w = nc.dram_tensor("vspill_w", [NCH, 128, KVH, 128], BF16)
    spart = nc.dram_tensor("spartial", [HID, T], F32)
    sqsel = nc.dram_tensor("qseld", [128, 64], BF16)
    srowd = nc.dram_tensor("rowseld", [1, HID], F32)

    p = Prog()

    from contextlib import ExitStack
    es = ExitStack()
    SBt = lambda name, shape, dt: es.enter_context(nc.sbuf_tensor(name, shape, dt))
    PSt = lambda name, shape, dt: es.enter_context(nc.psum_tensor(name, shape, dt))

    with es:
        bigA = SBt("bigA", [128, NK * T], BF16)
        states = SBt("states", [128, H * 128], F32)
        dtsb = SBt("dtsb", [128, 256], F32)
        ekd = SBt("ekd", [128, 256], F32)
        dte = SBt("dte", [128, 256], F32)
        decB = SBt("decB", [128, 256], F32)
        dtsbW = SBt("dtsbW", [128, 256], F32)
        dteW = SBt("dteW", [128, 256], F32)
        decBW = SBt("decBW", [128, 256], F32)
        negG = SBt("negG", [128, 256], F32)
        wint = SBt("wint", [128, NK * 32], BF16)
        cwot = SBt("cwot", [128, NK * 65], BF16)
        I_bf = SBt("I_bf", [128, 128], BF16)
        I_f = SBt("I_f", [128, 128], F32)
        S_Tc = SBt("S_Tc", [128, 128], BF16)
        maskc = SBt("maskc", [128, 512], F32)
        cosO = SBt("cosO", [128, T], BF16)
        sinO = SBt("sinO", [128, T], BF16)
        cosW = SBt("cosW", [128, T], BF16)
        sinW = SBt("sinW", [128, T], BF16)
        onesr = SBt("onesr", [1, 128], BF16)
        onesc = SBt("onesc", [128, 1], BF16)
        c_eps = SBt("c_eps", [128, 1], F32)
        c_one = SBt("c_one", [128, 1], F32)
        iota64 = SBt("iota64", [64, 1], F32)
        alog = SBt("alog", [32, 1], F32)
        expA = SBt("expA", [32, 1], F32)
        dtbias = SBt("dtbias", [32, 1], F32)
        gnormc = SBt("gnormc", [128, 1], F32)
        ffnb = SBt("ffnb", [128, 32], F32)
        zeros32 = SBt("zeros32", [32, 128], F32)
        qselB = SBt("qselB", [128, T], BF16)
        rowselc = SBt("rowselc", [128, 32], F32)
        biasrs = SBt("biasrs", [128, 32], F32)
        dmax = SBt("dmax", [64, 1], F32)
        tmax = SBt("tmax", [64, 1], F32)
        selc = SBt("selc", [64, 1], F32)
        selbf = SBt("selbf", [64, 1], BF16)
        qrowf = SBt("qrowf", [1, T], BF16)
        Scol = SBt("Scol", [128, 1], F32)
        Scbf = SBt("Scbf", [1, 65], BF16)
        ScB = SBt("ScB", [128, 64], F32)
        offsb = SBt("offsb", [1, 1], I32)
        POOLN = 23808
        pool = SBt("pool", [128, POOLN], BF16)
        qcb = [SBt(f"qcb{i}", [128, H * 128], BF16) for i in range(2)]
        sgcb = [SBt(f"sgcb{i}", [128, H * 128], BF16) for i in range(2)]
        kcb = [SBt(f"kcb{i}", [128, KVH * 128], BF16) for i in range(2)]
        vcb = [SBt(f"vcb{i}", [128, KVH * 128], BF16) for i in range(2)]
        pb = [PSt(f"psb{i}", [128, 512], F32) for i in range(8)]

        roff = es.enter_context(nc.sync.register("roff"))

        def pv(off, part, free, dt=BF16):
            if dt == BF16:
                return pool[0:part, off:off + free]
            return pool[:, off:off + 2 * free].bitcast(dt)[0:part, 0:free]

        def dma(qe, out_ap, in_ap, reads=(), writes=(), deps=(), slow=False,
                sem=None):
            def fn(en, out_ap=out_ap, in_ap=in_ap, slow=slow):
                return en.dma_start(out=out_ap, in_=in_ap,
                                    allow_slow_non_contiguous=slow)
            if sem is None:
                sem = writes[0] if writes else "misc_dma"
            return p.emit(qe, fn, reads=reads, writes=writes, extra_deps=deps,
                          sem=sem, inc=16)
        
        dma("g", I_bf[:, :], dI, writes=["I_bf"], sem="CONSTg")
        dma("q", I_f[:, :], dI, writes=["I_f"], sem="CONSTq")
        dma("g", S_Tc[:, :], dS_T, writes=["S_T"], sem="CONSTg")
        dma("q", maskc[:, :], dmask, writes=["maskc"], sem="CONSTq")
        dma("g", cosO[:, :], dcos_o, writes=["cosO"], sem="CONSTg")
        dma("g", sinO[:, :], dsin_o, writes=["sinO"], sem="CONSTg")
        dma("g", cosW[:, :], dcos_w, writes=["cosW"], sem="CONSTg")
        dma("g", sinW[:, :], dsin_w, writes=["sinW"], sem="CONSTg")
        dma("q", iota64[:, :], diota, writes=["iota64"], sem="CONSTq")
        dma("q", alog[:, :], dAlog, writes=["alog"], sem="CONSTq")
        dma("q", dtbias[:, :], ddtb, writes=["dtbias"], sem="CONSTq")
        dma("q", gnormc[:, :], dgn, writes=["gnormc"], sem="CONSTq")
        dma("q", ffnb[:, :], dfb, writes=["ffnb"], sem="CONSTq")
        dma("q", offsb[:, :], doff, writes=["offsb"], sem="CONSTq")
        dma("g", wint[:, :].rearrange("p (k h) -> p k h", k=NK),
            dWin.rearrange("(k p) h -> p k h", p=128), writes=["wint"], sem="CONSTg")
        dma("g", cwot[:, :].rearrange("p (k c) -> p k c", k=NK),
            dCWO.rearrange("(k p) c -> p k c", p=128), writes=["cwot"], sem="CONSTg")
        p.emit("v", lambda en: en.memset(onesr[:, :], 1.0), writes=["onesr"])
        p.emit("v", lambda en: en.memset(c_eps[:, :], LN_EPS), writes=["c_eps"])
        p.emit("v", lambda en: en.memset(c_one[:, :], 1.0), writes=["c_one"])
        p.emit("v", lambda en: en.memset(onesc[:, :], 1.0), writes=["onesc"])
        p.emit("v", lambda en: en.memset(zeros32[:, :], 0.0), writes=["zeros32"])
        p.emit("v", lambda en: en.memset(states[:, :], 0.0), writes=["states"])
        p.emit("v", lambda en: en.memset(dmax[:, :], NEG_BIG), writes=["dmax"])
        p.emit("s", lambda en: en.activation(expA[:, :], alog[:, :], AF.Exp),
               reads=["alog"], writes=["expA"])
        p.emit("q", lambda en: en.reg_load(roff, offsb[0:1, 0:1]),
               reads=["offsb"], writes=["roff"], inc=0)

        def phase_p1():
            xt = [pv(11520, 128, 512), pv(12032, 128, 512)]
            xq = [pv(12544, 128, 512), pv(13056, 128, 512)]
            dots = pv(13568, 65, 512, F32)
            dotsB = pv(14592, 65, 512, F32)
            a_sb = pv(15616, 128, 64, F32)
            e_sb = pv(15744, 128, 64, F32)
            col = lambda i: pv(15872 + i * 2, 128, 1, F32)
            mcol, m2c, varc, rstdc, zmaxc, nzmaxc, Zc, rZc, qmx = (
                col(i) for i in range(9))
            dots_bf = pv(15936, 128, 64)
            qse = pv(16000, 128, 1, F32)
            Mblk = pv(16064, 64, 512)
            ai_blk = pv(16576, 1, 512, I32)
            ab_blk = pv(17600, 1, 512)
            rowblk = pv(18112, 1, 512, F32)

            fns = [(lambda en, k=k: en.matmul(
                pb[7][0:65, 0:1], cwot[:, k * 65:(k + 1) * 65], onesc[:, :],
                start=(k == 0), stop=(k == NK - 1))) for k in range(NK)]
            p.group("t", fns, reads=["cwot", "onesc"], writes=["pb7"])
            p.emit("s", lambda en: en.copy(Scol[0:65, 0:1], pb[7][0:65, 0:1]),
                   reads=["pb7"], writes=["Scol"])
            p.emit("t", lambda en: en.transpose(
                pb[6][0:1, 0:65], Scol[0:65, 0:1], I_f[0:65, 0:65]),
                reads=["Scol", "I_f"], writes=["pb6"])
            p.emit("s", lambda en: en.copy(Scbf[0:1, 0:65], pb[6][0:1, 0:65]),
                   reads=["pb6"], writes=["Scbf"])
            p.emit("t", lambda en: en.matmul(
                pb[6][:, 0:64], onesr[0:1, :], Scbf[0:1, 0:64],
                start=True, stop=True),
                reads=["onesr", "Scbf"], writes=["pb6"])
            p.emit("s", lambda en: en.copy(ScB[:, :], pb[6][:, 0:64]),
                   reads=["pb6"], writes=["ScB"])
            yield

            for tg in range(16):
                for k in range(NK):
                    par = k % 2
                    bx, bq = f"p1x{par}", f"p1q{par}"
                    dma("g", xt[par][:, :],
                        dxTall[k * 128:(k + 1) * 128, tg * 512:(tg + 1) * 512],
                        writes=[bx])
                    p.emit("s", lambda en, par=par: en.activation(
                        xq[par][:, :], xt[par][:, :], AF.Square),
                        reads=[bx], writes=[bq])
                    fns = [
                        (lambda en, k=k, par=par: en.matmul(
                            pb[6][0:65, :], cwot[:, k * 65:(k + 1) * 65],
                            xt[par][:, :],
                            start=(k == 0), stop=(k == NK - 1))),
                        (lambda en, k=k, par=par: en.matmul(
                            pb[7][0:65, :], cwot[:, k * 65:(k + 1) * 65],
                            xq[par][:, :],
                            start=(k == 0), stop=(k == NK - 1))),
                    ]
                    p.group("t", fns, reads=[bx, bq, "cwot"],
                            writes=["pb6", "pb7"])
                    if k % 8 == 7:
                        yield
                p.emit("s", lambda en: en.copy(dots[:, :], pb[6][0:65, :]),
                       reads=["pb6"], writes=["p1d"])
                p.emit("s", lambda en: en.copy(dotsB[:, :], pb[7][0:65, :]),
                       reads=["pb7"], writes=["p1dB"])
                for j in range(4):
                    tile = tg * 4 + j
                    p.emit("t", lambda en, j=j: en.transpose(
                        pb[6][:, 0:65], dots[:, j * 128:(j + 1) * 128],
                        I_f[0:65, 0:65]),
                        reads=["p1d", "I_f"], writes=["pb6"])
                    p.emit("t", lambda en, j=j: en.transpose(
                        pb[7][:, 0:65], dotsB[:, j * 128:(j + 1) * 128],
                        I_f[0:65, 0:65]),
                        reads=["p1dB", "I_f"], writes=["pb7"])
                    p.emit("v", lambda en: en.tensor_scalar(
                        mcol, pb[6][:, 64:65], 1.0 / HID, None, ALU.mult),
                        reads=["pb6"], writes=["p1m"])
                    p.emit("v", lambda en: en.tensor_tensor(
                        m2c, mcol, mcol, ALU.mult),
                        reads=["p1m"], writes=["p1m2"])
                    p.emit("v", lambda en: en.scalar_tensor_tensor(
                        varc, pb[7][:, 64:65], 1.0 / HID, m2c,
                        ALU.mult, ALU.subtract),
                        reads=["pb7", "p1m2"], writes=["p1var"])
                    p.emit("s", lambda en: en.activation(
                        rstdc, varc, AF.Ln, bias=c_eps[:, :]),
                        reads=["p1var", "c_eps"], writes=["p1rstd"])
                    p.emit("s", lambda en: en.activation(
                        rstdc, rstdc, AF.Exp, scale=-0.5),
                        reads=["p1rstd"], writes=["p1rstd"])
                    p.emit("v", lambda en: en.tensor_scalar(
                        m2c, mcol, -1.0, None, ALU.mult),
                        reads=["p1m"], writes=["p1m2"])
                    p.emit("v", lambda en: en.scalar_tensor_tensor(
                        a_sb, ScB[:, :], m2c, pb[6][:, 0:64],
                        ALU.mult, ALU.add),
                        reads=["ScB", "p1m2", "pb6"], writes=["p1a"])
                    p.emit("v", lambda en: en.tensor_scalar(
                        a_sb, a_sb, rstdc, 1.0 / FFN_T, ALU.mult, ALU.mult),
                        reads=["p1a", "p1rstd"], writes=["p1a"])
                    p.emit("v", lambda en: en.reduce_max(zmaxc, a_sb, axis=AX.X),
                           reads=["p1a"], writes=["p1zm"])
                    p.emit("v", lambda en: en.tensor_scalar(
                        nzmaxc, zmaxc, -1.0, None, ALU.mult),
                        reads=["p1zm"], writes=["p1nzm"])
                    p.emit("s", lambda en: en.activation(
                        e_sb, a_sb, AF.Exp, bias=nzmaxc),
                        reads=["p1a", "p1nzm"], writes=["p1e"])
                    p.emit("v", lambda en: en.reduce_sum(Zc, e_sb, axis=AX.X),
                           reads=["p1e"], writes=["p1Z"])
                    p.emit("v", lambda en: en.reciprocal(rZc, Zc),
                           reads=["p1Z"], writes=["p1rZ"])
                    p.emit("v", lambda en: en.tensor_scalar(
                        dots_bf, e_sb, rZc, None, ALU.mult),
                        reads=["p1e", "p1rZ"], writes=["p1dbf"])
                    p.emit("v", lambda en: en.reduce_max(qmx, dots_bf, axis=AX.X),
                           reads=["p1dbf"], writes=["p1qmx"])
                    p.emit("v", lambda en: en.tensor_scalar(
                        qse, qmx, FFN_TH, None, ALU.is_gt),
                        reads=["p1qmx"], writes=["p1qse"])
                    dma("g", bass.AP(sqsel, tile, [[64, 128], [1, 1]]), qse,
                        reads=["p1qse"], writes=["qseld"], slow=True)
                    p.emit("t", lambda en: en.transpose(
                        pb[7][:, :].bitcast(BF16)[0:64, 0:128],
                        dots_bf, I_bf[:, :]),
                        reads=["p1dbf", "I_bf"], writes=["pb7"])
                    p.emit("v", lambda en: en.reduce_max(
                        tmax[:, :], pb[7][:, :].bitcast(BF16)[0:64, 0:128],
                        axis=AX.X),
                        reads=["pb7"], writes=["tmax"])
                    p.emit("v", lambda en: en.tensor_tensor(
                        dmax[:, :], dmax[:, :], tmax[:, :], ALU.max),
                        reads=["tmax", "dmax"], writes=["dmax"])
                    yield

            p.emit("v", lambda en: en.tensor_scalar(
                selc[:, :], dmax[:, :], FFN_TH, None, ALU.is_gt),
                reads=["dmax"], writes=["selc"])
            p.emit("v", lambda en: en.tensor_copy(selbf[:, :], selc[:, :]),
                   reads=["selc"], writes=["selbf"])
            for och in range(8):
                dma("q", ai_blk, dassign[0:1, och * 512:(och + 1) * 512],
                    writes=["p1ai"])
                p.emit("v", lambda en: en.tensor_copy(ab_blk, ai_blk),
                       reads=["p1ai"], writes=["p1ab"])
                p.emit("t", lambda en: en.matmul(
                    pb[6][0:64, :], onesr[0:1, 0:64], ab_blk,
                    start=True, stop=True),
                    reads=["onesr", "p1ab"], writes=["pb6"])
                p.emit("v", lambda en: en.tensor_scalar(
                    Mblk, pb[6][0:64, :], iota64[:, :], None, ALU.is_equal),
                    reads=["pb6", "iota64"], writes=["p1M"])
                p.emit("t", lambda en: en.matmul(
                    pb[7][0:1, :], selbf[:, :], Mblk, start=True, stop=True),
                    reads=["selbf", "p1M"], writes=["pb7"])
                p.emit("v", lambda en: en.tensor_scalar(
                    rowblk, pb[7][0:1, :], 0.5, None, ALU.is_gt),
                    reads=["pb7"], writes=["p1row"])
                dma("q", srowd.ap()[0:1, och * 512:(och + 1) * 512], rowblk,
                    reads=["p1row"], writes=["rowd"])
                yield
            dma("g", rowselc[:, :],
                bass.AP(srowd, 0, [[1, 128], [128, 32]]),
                reads=["rowd"], writes=["rowselc"], slow=True)
            p.emit("v", lambda en: en.tensor_tensor(
                biasrs[:, :], ffnb[:, :], rowselc[:, :], ALU.mult),
                reads=["ffnb", "rowselc"], writes=["biasrs"])

            def rd_qrow(en):
                return en.dma_start(
                    out=qrowf[0:1, :].rearrange("o (tt pp) -> o tt pp", tt=8),
                    in_=bass.AP(sqsel, roff, [[0, 1], [1, 8], [64, 128]]),
                    allow_slow_non_contiguous=True)
            p.emit("q", rd_qrow, reads=["qseld", "roff"], writes=["qrowf"],
                   sem="qrowf", inc=16)
            p.emit("t", lambda en: en.matmul(
                pb[6][:, :], onesr[0:1, :], qrowf[0:1, 0:512],
                start=True, stop=True),
                reads=["onesr", "qrowf"], writes=["pb6"])
            p.emit("t", lambda en: en.matmul(
                pb[7][:, :], onesr[0:1, :], qrowf[0:1, 512:1024],
                start=True, stop=True),
                reads=["onesr", "qrowf"], writes=["pb7"])
            p.emit("s", lambda en: en.copy(qselB[:, 0:512], pb[6][:, :]),
                   reads=["pb6"], writes=["qselB"])
            p.emit("s", lambda en: en.copy(qselB[:, 512:1024], pb[7][:, :]),
                   reads=["pb7"], writes=["qselB"])
            yield

        def phase_ln(xdram, dst_h, dstname, tag):
            xt = pv(0, 128, 4096)
            xnt = pv(4096, 128, 4096)
            xsq = pv(8192, 128, 4096)
            col = lambda i: pv(12544 + i * 2, 128, 1, F32)
            scol, mcol, sscol, varc, rstdc, nmr = (col(i) for i in range(6))
            for tt in range(NT):
                dma("g", xt[:, :], xdram[tt * 128:(tt + 1) * 128, :],
                    writes=[f"{tag}x"])
                p.emit("v", lambda en: en.reduce_sum(scol, xt[:, :], axis=AX.X),
                       reads=[f"{tag}x"], writes=[f"{tag}s"])
                p.emit("v", lambda en: en.tensor_scalar(
                    mcol, scol, 1.0 / HID, None, ALU.mult),
                    reads=[f"{tag}s"], writes=[f"{tag}m"])
                p.emit("s", lambda en: en.activation(xsq[:, :], xt[:, :], AF.Square),
                       reads=[f"{tag}x"], writes=[f"{tag}xsq"])
                p.emit("v", lambda en: en.reduce_sum(sscol, xsq[:, :], axis=AX.X),
                       reads=[f"{tag}xsq"], writes=[f"{tag}ss"])
                p.emit("v", lambda en: en.tensor_tensor(varc, mcol, mcol, ALU.mult),
                       reads=[f"{tag}m"], writes=[f"{tag}var"])
                p.emit("v", lambda en: en.scalar_tensor_tensor(
                    varc, sscol, 1.0 / HID, varc, ALU.mult, ALU.subtract),
                    reads=[f"{tag}ss", f"{tag}var"], writes=[f"{tag}var"])
                p.emit("s", lambda en: en.activation(rstdc, varc, AF.Ln, bias=c_eps[:, :]),
                       reads=[f"{tag}var", "c_eps"], writes=[f"{tag}r"])
                p.emit("s", lambda en: en.activation(rstdc, rstdc, AF.Exp, scale=-0.5),
                       reads=[f"{tag}r"], writes=[f"{tag}r"])
                p.emit("v", lambda en: en.scalar_tensor_tensor(
                    nmr, mcol, -1.0, rstdc, ALU.mult, ALU.mult),
                    reads=[f"{tag}m", f"{tag}r"], writes=[f"{tag}nmr"])
                p.emit("v", lambda en: en.tensor_scalar(
                    xnt[:, :], xt[:, :], rstdc, nmr, ALU.mult, ALU.add),
                    reads=[f"{tag}x", f"{tag}r", f"{tag}nmr"], writes=[f"{tag}xn"])
                for kq in range(0, NK, 8):
                    par = (kq // 8) % 2
                    bank = pb[6 + par]
                    pbf = bank[:, :].bitcast(BF16)
                    fns = []
                    for j in range(8):
                        fns.append(lambda en, k=kq + j, j=j, pbf=pbf: en.transpose(
                            pbf[:, j * 128:(j + 1) * 128],
                            xnt[:, k * 128:(k + 1) * 128], I_bf[:, :]))
                    p.group("t", fns, reads=[f"{tag}xn", "I_bf"],
                            writes=[PB[6 + par]])
                    def evac(en, kq=kq, tt=tt, pbf=pbf):
                        dstap = bass.AP(dst_h, kq * T + tt * 128,
                                        [[NK * T, 128], [T, 8], [1, 128]])
                        return en.tensor_copy(
                            dstap, pbf[:, :].rearrange("p (j t) -> p j t", j=8))
                    p.emit("v", evac, reads=[PB[6 + par]], writes=[dstname])

        def phase_kvdt(xn_src, xnname, cosT, cname, sinT, sname, tag,
                       kdst, kdstn, vdst, vdstn, dtsbX, dtsbn, dteX, dten,
                       decBX, decBn):
            wstA = pv(0, 128, 4096)
            wstB = pv(4096, 128, 4096)
            sc1 = pv(8192, 128, 1024)
            sc2 = pv(9216, 128, 1024)
            sc3 = pv(10240, 128, 1024)
            wintv = wint
            GtT = pv(12288, 32, 1024, F32)
            dtT = pv(14336, 32, 1024, F32)
            gTt = pv(16384, 32, 1024, F32)
            eGtT = pv(18432, 32, 1024)
            zb = pv(19456, 32, 512, F32)
            absz = pv(20480, 32, 512, F32)
            easb = pv(21504, 32, 512, F32)
            rel = pv(22528, 32, 512, F32)
            ek_sb = pv(23552, 32, 128, F32)

            def proj8(wblk, dst, dstname, is_k):
                for m in range(KVH):
                    par = m % 2
                    stripe = wstA if par == 0 else wstB
                    wb = "wstA" if par == 0 else "wstB"
                    dma("g", stripe[:, :], wblk[m, :, :], writes=[wb], sem="wsA" if par == 0 else "wsB")
                    pjn = [PB[par * 2], PB[par * 2 + 1]]
                    fns = []
                    for k in range(NK):
                        for nh in range(2):
                            fns.append(lambda en, k=k, nh=nh, par=par,
                                       stripe=stripe: en.matmul(
                                pb[par * 2 + nh][:, :],
                                stripe[:, k * 128:(k + 1) * 128],
                                xn_src[:, k * T + nh * 512:k * T + (nh + 1) * 512],
                                start=(k == 0), stop=(k == NK - 1)))
                    p.group("t", fns, reads=[wb, xnname], writes=pjn)
                    p.emit("s", lambda en, par=par: en.copy(
                        sc1[:, 0:512], pb[par * 2][:, :]),
                        reads=[pjn[0]], writes=["prs1"])
                    p.emit("s", lambda en, par=par: en.copy(
                        sc1[:, 512:1024], pb[par * 2 + 1][:, :]),
                        reads=[pjn[1]], writes=["prs1"])
                    if is_k:
                        fns = [(lambda en, nh=nh: en.matmul(
                            pb[4 + nh][:, :], S_Tc[:, :],
                            sc1[:, nh * 512:(nh + 1) * 512],
                            start=True, stop=True)) for nh in range(2)]
                        p.group("t", fns, reads=["prs1", "S_T"],
                                writes=["pb4", "pb5"])
                        for nh in range(2):
                            p.emit("v", lambda en, nh=nh: en.tensor_tensor(
                                sc2[:, nh * 512:(nh + 1) * 512], pb[4 + nh][:, :],
                                sinT[:, nh * 512:(nh + 1) * 512], ALU.mult),
                                reads=[PB[4 + nh], sname], writes=["prs2"])
                            p.emit("v", lambda en, nh=nh, par=par: en.tensor_tensor(
                                sc3[:, nh * 512:(nh + 1) * 512],
                                pb[par * 2 + nh][:, :],
                                cosT[:, nh * 512:(nh + 1) * 512], ALU.mult),
                                reads=[pjn[nh], cname], writes=["prs3"])
                        p.emit("v", lambda en: en.tensor_tensor(
                            sc3[:, :], sc3[:, :], sc2[:, :], ALU.add),
                            reads=["prs3", "prs2"], writes=["prs3"])
                    else:
                        pbf4 = pb[4][:, :].bitcast(BF16)
                        pbf5 = pb[5][:, :].bitcast(BF16)
                        fns = []
                        for j in range(8):
                            tgt = pbf4 if j < 4 else pbf5
                            fns.append(lambda en, j=j, tgt=tgt: en.transpose(
                                tgt[:, (j % 4) * 128:(j % 4 + 1) * 128],
                                sc1[:, j * 128:(j + 1) * 128], I_bf[:, :]))
                        p.group("t", fns, reads=["prs1", "I_bf"],
                                writes=["pb4", "pb5"])
                        p.emit("v", lambda en, pbf4=pbf4: en.tensor_copy(
                            sc3[:, 0:512], pbf4[:, 0:512]),
                            reads=["pb4"], writes=["prs3"])
                        p.emit("v", lambda en, pbf5=pbf5: en.tensor_copy(
                            sc3[:, 512:1024], pbf5[:, 0:512]),
                            reads=["pb5"], writes=["prs3"])
                    def sp(en, m=m, dst=dst):
                        return en.dma_start(out=bass.AP(
                            dst, m * 128,
                            [[KVH * 128, 128], [128 * KVH * 128, NCH], [1, 128]]),
                            in_=sc3[:, :].rearrange("p (c t) -> p c t", c=NCH))
                    p.emit("q", sp, reads=["prs3"], writes=[dstname], sem=dstname, inc=16)

            proj8(dWk, kdst, kdstn, True)
            proj8(dWv, vdst, vdstn, False)

            fns = []
            for k in range(NK):
                for nh in range(2):
                    fns.append(lambda en, k=k, nh=nh: en.matmul(
                        pb[nh][0:32, :], wint[:, k * 32:(k + 1) * 32],
                        xn_src[:, k * T + nh * 512:k * T + (nh + 1) * 512],
                        start=(k == 0), stop=(k == NK - 1)))
            p.group("t", fns, reads=["wint", xnname], writes=["pb0", "pb1"])
            for nh in range(2):
                sl = slice(nh * 512, (nh + 1) * 512)
                p.emit("v", lambda en, nh=nh: en.tensor_scalar(
                    zb, pb[nh][0:32, :], dtbias[:, :], None, ALU.add),
                    reads=[PB[nh], "dtbias"], writes=[f"{tag}zb"])
                p.emit("v", lambda en: en.tensor_scalar(
                    rel, zb, -1.0, None, ALU.mult),
                    reads=[f"{tag}zb"], writes=[f"{tag}rel"])
                p.emit("v", lambda en: en.tensor_tensor(
                    absz, zb, rel, ALU.max),
                    reads=[f"{tag}zb", f"{tag}rel"], writes=[f"{tag}ab"])
                p.emit("s", lambda en: en.activation(easb, absz, AF.Exp, scale=-1.0),
                       reads=[f"{tag}ab"], writes=[f"{tag}ea"])
                p.emit("s", lambda en: en.activation(easb, easb, AF.Ln, bias=c_one[0:32, :]),
                       reads=[f"{tag}ea", "c_one"], writes=[f"{tag}ea"])
                p.emit("v", lambda en: en.tensor_relu(rel, zb),
                       reads=[f"{tag}zb"], writes=[f"{tag}rel"])
                p.emit("v", lambda en, sl=sl: en.tensor_tensor(
                    dtT[:, sl], rel, easb, ALU.add),
                    reads=[f"{tag}rel", f"{tag}ea"], writes=["dtT"])
            p.emit("v", lambda en: en.tensor_scalar(
                gTt[:, :], dtT[:, :], expA[:, :], -1.0, ALU.mult, ALU.mult),
                reads=["dtT", "expA"], writes=["gTt"])
            for c in range(NCH):
                p.emit("v", lambda en, c=c: en.tensor_tensor_scan(
                    GtT[:, c * 128:(c + 1) * 128], gTt[:, c * 128:(c + 1) * 128],
                    zeros32[:, :], 0.0, ALU.add, ALU.add),
                    reads=["gTt", "zeros32"], writes=["GtT"])
            p.emit("s", lambda en: en.activation(eGtT[:, :], GtT[:, :], AF.Exp),
                   reads=["GtT"], writes=["eGtT"])
            for c in range(NCH):
                sl = slice(c * 128, (c + 1) * 128)
                cs = slice(c * 32, (c + 1) * 32)
                p.emit("t", lambda en, sl=sl: en.transpose(
                    pb[6][:, 0:32], GtT[:, sl], I_f[0:32, 0:32]),
                    reads=["GtT", "I_f"], writes=["pb6"])
                p.emit("s", lambda en, cs=cs: en.activation(
                    negG[:, cs], pb[6][:, 0:32], AF.Copy, scale=-1.0),
                    reads=["pb6"], writes=["negG"])
                p.emit("s", lambda en, sl=sl, c=c: en.activation(
                    ek_sb, GtT[:, sl], AF.Exp, scale=-1.0,
                    bias=GtT[:, c * 128 + 127:c * 128 + 128]),
                    reads=["GtT"], writes=[f"{tag}ek"])
                p.emit("t", lambda en: en.transpose(
                    pb[7][:, 0:32], ek_sb, I_f[0:32, 0:32]),
                    reads=[f"{tag}ek", "I_f"], writes=["pb7"])
                p.emit("s", lambda en, cs=cs: en.copy(ekd[:, cs], pb[7][:, 0:32]),
                       reads=["pb7"], writes=["ekd"])
                p.emit("t", lambda en, sl=sl: en.transpose(
                    pb[6][:, 0:32], dtT[:, sl], I_f[0:32, 0:32]),
                    reads=["dtT", "I_f"], writes=["pb6"])
                p.emit("s", lambda en, cs=cs: en.copy(dtsbX[:, cs], pb[6][:, 0:32]),
                       reads=["pb6"], writes=[dtsbn])
            p.emit("v", lambda en: en.tensor_tensor(
                dteX[:, :], dtsbX[:, :], ekd[:, :], ALU.mult),
                reads=[dtsbn, "ekd"], writes=[dten])
            dma("q", sGd.ap(), GtT[:, :], reads=["GtT"], writes=["Gd"])
            dma("q", sEGb.ap(), eGtT[:, :], reads=["eGtT"], writes=["EGd"])
            dma("q", sdec.ap(),
                bass.AP(pool, eGtT.offset + 127, [[POOLN, 32], [128, 8]]),
                reads=["eGtT"], writes=["decd"], slow=True)
            dma("g", decBX[:, :],
                bass.AP(sdec, 0, [[0, 128], [8, 32], [1, 8]]),
                reads=["decd"], writes=[decBn], sem=decBn)

        def phase_attn(with_y, tag, kspill, vspill, kspn, vspn,
                       scb, csb, ybanks, ktb, dtsbX, dtsbn, dteX, dten,
                       decBX, decBn):
            ktr = [pv(11520, 128, 128), pv(11648, 128, 128)]
            xc4 = [pv(11776, 128, 512), pv(12288, 128, 512)]
            xc24 = [pv(12800, 128, 512), pv(13312, 128, 512)]
            stbf = pv(13824, 128, 512)
            qhat = [pv(14336, 128, 512), pv(14848, 128, 512)]
            Gb = [pv(15360, 128, 512, F32), pv(16384, 128, 512, F32)]
            E_sb = [pv(17408, 128, 512), pv(17920, 128, 512)]
            dct = [pv(18432, 128, 512), pv(18944, 128, 512)]
            STs = [pv(19456, 128, 512), pv(19968, 128, 512)]
            ysq = pv(20480, 128, 512)
            z_sb = pv(20480, 128, 512)
            ysb = [pv(20992, 128, 512), pv(21504, 128, 512)]
            r1 = pv(22016, 1, 512, F32)
            rsr = pv(23040, 1, 512)

            def load_chunk(c):
                par = c % 2
                dma("q", kcb[par][:, :].rearrange("p (m t) -> p m t", m=KVH),
                    bass.AP(kspill, c * 128 * KVH * 128,
                            [[KVH * 128, 128], [128, KVH], [1, 128]]),
                    reads=[kspn], writes=[f"kc{par}"], sem=f"kc{par}")
                dma("q", vcb[par][:, :].rearrange("p (m t) -> p m t", m=KVH),
                    bass.AP(vspill, c * 128 * KVH * 128,
                            [[KVH * 128, 128], [128, KVH], [1, 128]]),
                    reads=[vspn], writes=[f"vc{par}"], sem=f"vc{par}")
                if with_y:
                    dma("q", qcb[par][:, :].rearrange("p (m t) -> p m t", m=H),
                        bass.AP(sqsp, c * 128 * H * 128,
                                [[H * 128, 128], [128, H], [1, 128]]),
                        reads=["qspd"], writes=[f"qc{par}"], sem=f"qc{par}")
                    dma("q", sgcb[par][:, :].rearrange("p (m t) -> p m t", m=H),
                        bass.AP(ssgsp, c * 128 * H * 128,
                                [[H * 128, 128], [128, H], [1, 128]]),
                        reads=["sgspd"], writes=[f"sgc{par}"], sem=f"sgc{par}")

            def load_ge(c, g):
                if not with_y:
                    return
                gp = g % 2
                h0 = 4 * g
                def gb_dma(en, c=c, h0=h0, gp=gp):
                    return en.dma_start(
                        out=Gb[gp][:, :].rearrange("p (j t) -> p j t", j=4),
                        in_=bass.AP(sGd, h0 * T + c * 128,
                                    [[0, 128], [T, 4], [1, 128]]))
                p.emit("g", gb_dma, reads=["Gd"], writes=[f"Gb{gp}"],
                       sem=f"Gb{gp}", inc=16)
                def e_dma(en, c=c, h0=h0, gp=gp):
                    return en.dma_start(
                        out=E_sb[gp][:, :].rearrange("p (j t) -> p j t", j=4),
                        in_=bass.AP(sEGb, h0 * T + c * 128,
                                    [[0, 128], [T, 4], [1, 128]]))
                p.emit("g", e_dma, reads=["EGd"], writes=[f"E{gp}"],
                       sem=f"E{gp}", inc=16)

            load_chunk(0)
            load_ge(0, 0)
            for c in range(NCH):
                par = c % 2
                if c + 1 < NCH:
                    load_chunk(c + 1)
                k_c, v_c = kcb[par], vcb[par]
                q_c, sg_c = qcb[par], sgcb[par]
                kcn, vcn = f"kc{par}", f"vc{par}"
                qcn, sgcn = f"qc{par}", f"sgc{par}"
                for g in range(KVH):
                    if g + 1 < KVH:
                        load_ge(c, g + 1)
                    elif c + 1 < NCH:
                        load_ge(c + 1, 0)
                    gp = g % 2
                    h0 = 4 * g
                    sb = pb[scb[gp]]
                    cbk = pb[csb[gp]]
                    yb = pb[ybanks[gp]]
                    sbn = PB[scb[gp]]
                    cbn = PB[csb[gp]]
                    ybn = PB[ybanks[gp]]
                    ktpb = pb[ktb]
                    ktpbn = PB[ktb]
                    p.emit("t", lambda en, g=g, k_c=k_c, gp=gp, ktpb=ktpb:
                           en.transpose(
                        ktpb[:, :].bitcast(BF16)[:, gp * 128:gp * 128 + 128],
                        k_c[:, g * 128:(g + 1) * 128], I_bf[:, :]),
                        reads=[kcn, "I_bf"], writes=[ktpbn])
                    p.emit("s", lambda en, gp=gp, ktpb=ktpb: en.copy(
                        ktr[gp][:, :],
                        ktpb[:, :].bitcast(BF16)[:, gp * 128:gp * 128 + 128]),
                        reads=[ktpbn], writes=[f"ktr{gp}"])
                    vblk = bass.AP(vcb[par], g * 128,
                                   [[KVH * 128, 128], [0, 4], [1, 128]])
                    dtc = bass.AP(dtsbX, c * 32 + h0, [[256, 128], [1, 4], [0, 128]])
                    dtec = bass.AP(dteX, c * 32 + h0, [[256, 128], [1, 4], [0, 128]])
                    if with_y:
                        p.emit("v", lambda en, vblk=vblk, dtc=dtc, gp=gp:
                               en.tensor_tensor(
                            xc4[gp][:, :].rearrange("p (j t) -> p j t", j=4),
                            vblk, dtc, ALU.mult),
                            reads=[vcn, dtsbn], writes=[f"xc4{gp}"])
                    p.emit("v", lambda en, vblk=vblk, dtec=dtec, gp=gp:
                           en.tensor_tensor(
                        xc24[gp][:, :].rearrange("p (j t) -> p j t", j=4),
                        vblk, dtec, ALU.mult),
                        reads=[vcn, dten], writes=[f"xc24{gp}"])
                    if with_y:
                        fns = [(lambda en, hh=hh, g=g, sb=sb, h0=h0, k_c=k_c,
                                q_c=q_c: en.matmul(
                            sb[:, hh * 128:(hh + 1) * 128],
                            k_c[:, g * 128:(g + 1) * 128],
                            q_c[:, (h0 + hh) * 128:(h0 + hh + 1) * 128],
                            start=True, stop=True)) for hh in range(4)]
                        p.group("t", fns, reads=[kcn, qcn], writes=[sbn])
                        p.emit("v", lambda en, gp=gp: en.tensor_tensor(
                            Gb[gp][:, :], Gb[gp][:, :], maskc[:, :], ALU.add),
                            reads=[f"Gb{gp}", "maskc"], writes=[f"Gb{gp}"])
                        for hh in range(4):
                            p.emit("s", lambda en, hh=hh, c=c, h0=h0, gp=gp:
                                   en.activation(
                                dct[gp][:, hh * 128:(hh + 1) * 128],
                                Gb[gp][:, hh * 128:(hh + 1) * 128], AF.Exp,
                                bias=negG[:, c * 32 + h0 + hh:c * 32 + h0 + hh + 1]),
                                reads=[f"Gb{gp}", "negG"], writes=[f"dct{gp}"])
                        p.emit("v", lambda en, sb=sb, gp=gp: en.tensor_tensor(
                            STs[gp][:, :], sb[:, :], dct[gp][:, :], ALU.mult),
                            reads=[sbn, f"dct{gp}"], writes=[f"ST{gp}"])
                        p.emit("v", lambda en, h0=h0, gp=gp, q_c=q_c:
                               en.tensor_tensor(
                            qhat[gp][:, :], q_c[:, h0 * 128:(h0 + 4) * 128],
                            E_sb[gp][:, :], ALU.mult),
                            reads=[qcn, f"E{gp}"], writes=[f"qh{gp}"])
                    fns = [(lambda en, hh=hh, cbk=cbk, gp=gp: en.matmul(
                        cbk[:, hh * 128:(hh + 1) * 128], ktr[gp][:, :],
                        xc24[gp][:, hh * 128:(hh + 1) * 128],
                        start=True, stop=True)) for hh in range(4)]
                    p.group("t", fns, reads=[f"ktr{gp}", f"xc24{gp}"],
                            writes=[cbn])
                    if with_y:
                        p.emit("v", lambda en, h0=h0: en.tensor_copy(
                            stbf[:, :], states[:, h0 * 128:(h0 + 4) * 128]),
                            reads=["states"], writes=["stbf"])
                        for hh in range(4):
                            h = h0 + hh
                            fns = [
                                (lambda en, hh=hh, yb=yb, gp=gp: en.matmul(
                                    yb[:, hh * 128:(hh + 1) * 128],
                                    xc4[gp][:, hh * 128:(hh + 1) * 128],
                                    STs[gp][:, hh * 128:(hh + 1) * 128],
                                    start=True, stop=False)),
                                (lambda en, hh=hh, yb=yb, gp=gp: en.matmul(
                                    yb[:, hh * 128:(hh + 1) * 128],
                                    stbf[:, hh * 128:(hh + 1) * 128],
                                    qhat[gp][:, hh * 128:(hh + 1) * 128],
                                    start=False, stop=True)),
                            ]
                            p.group("t", fns,
                                    reads=[f"xc4{gp}", f"ST{gp}",
                                           "stbf", f"qh{gp}"],
                                    writes=[ybn])
                            p.emit("v", lambda en, h=h, c=c, cbk=cbk, hh=hh:
                                   en.scalar_tensor_tensor(
                                states[:, h * 128:(h + 1) * 128],
                                states[:, h * 128:(h + 1) * 128],
                                decBX[:, h * 8 + c:h * 8 + c + 1],
                                cbk[:, hh * 128:(hh + 1) * 128],
                                ALU.mult, ALU.add),
                                reads=[decBn, cbn, "stbf"],
                                writes=["states"])
                        p.emit("s", lambda en, yb=yb: en.activation(
                            ysq[:, :], yb[:, :], AF.Square),
                            reads=[ybn], writes=["ysq"])
                        p.emit("s", lambda en, yb=yb, gp=gp: en.activation(
                            ysb[gp][:, :], yb[:, :], AF.Copy, scale=gnormc[:, :]),
                            reads=[ybn, "gnormc"], writes=[f"ysb{gp}"])
                        p.emit("t", lambda en, sb=sb: en.matmul(
                            sb[0:1, :], onesc[:, :], ysq[:, :],
                            start=True, stop=True),
                            reads=["ysq", "onesc"], writes=[sbn])
                        p.emit("s", lambda en, sb=sb: en.activation(
                            r1, sb[0:1, :], AF.Ln, scale=1.0 / 128.0,
                            bias=c_eps[0:1, :]),
                            reads=[sbn, "c_eps"], writes=["r1"])
                        p.emit("s", lambda en: en.activation(
                            rsr, r1, AF.Exp, scale=-0.5),
                            reads=["r1"], writes=["rsr"])
                        p.emit("t", lambda en, sb=sb: en.matmul(
                            sb[:, :], onesr[0:1, :], rsr[0:1, :],
                            start=True, stop=True),
                            reads=["rsr", "onesr"], writes=[sbn])
                        p.emit("v", lambda en, sb=sb, gp=gp: en.tensor_tensor(
                            z_sb[:, :], ysb[gp][:, :], sb[:, :], ALU.mult),
                            reads=[f"ysb{gp}", sbn, "ysq"], writes=["ysq"])
                        def yfin(en, c=c, h0=h0, sg_c=sg_c):
                            dstap = bass.AP(bigA, h0 * T + c * 128,
                                            [[NK * T, 128], [T, 4], [1, 128]])
                            return en.tensor_tensor(
                                dstap,
                                z_sb[:, :].rearrange("p (j t) -> p j t", j=4),
                                sg_c[:, h0 * 128:(h0 + 4) * 128].rearrange(
                                    "p (j t) -> p j t", j=4), ALU.mult)
                        p.emit("v", yfin, reads=["ysq", sgcn],
                               writes=["yf"])
                    yield

        def phase_qgate(xn_src, xnname):
            wstA = pv(0, 128, 4096)
            wstB = pv(4096, 128, 4096)
            sc1 = pv(8192, 128, 1024)
            sc2 = pv(9216, 128, 1024)
            sc3 = pv(10240, 128, 1024)
            for m in range(H):
                par = m % 2
                stripe = wstA if par == 0 else wstB
                wb = "wstA" if par == 0 else "wstB"
                dma("g", stripe[:, :], dWq[m, :, :], writes=[wb], sem="wsA" if par == 0 else "wsB")
                pjn = [PB[par * 2], PB[par * 2 + 1]]
                fns = []
                for k in range(NK):
                    for nh in range(2):
                        fns.append(lambda en, k=k, nh=nh, stripe=stripe, par=par:
                                   en.matmul(
                            pb[par * 2 + nh][:, :],
                            stripe[:, k * 128:(k + 1) * 128],
                            xn_src[:, k * T + nh * 512:k * T + (nh + 1) * 512],
                            start=(k == 0), stop=(k == NK - 1)))
                p.group("t", fns, reads=[wb, xnname], writes=pjn)
                p.emit("s", lambda en, par=par: en.copy(
                    sc1[:, 0:512], pb[par * 2][:, :]), reads=[pjn[0]],
                    writes=["prs1"])
                p.emit("s", lambda en, par=par: en.copy(
                    sc1[:, 512:1024], pb[par * 2 + 1][:, :]), reads=[pjn[1]],
                    writes=["prs1"])
                fns = [(lambda en, nh=nh: en.matmul(
                    pb[4 + nh][:, :], S_Tc[:, :],
                    sc1[:, nh * 512:(nh + 1) * 512], start=True, stop=True))
                    for nh in range(2)]
                p.group("t", fns, reads=["prs1", "S_T"], writes=["pb4", "pb5"])
                for nh in range(2):
                    p.emit("v", lambda en, nh=nh: en.tensor_tensor(
                        sc2[:, nh * 512:(nh + 1) * 512], pb[4 + nh][:, :],
                        sinO[:, nh * 512:(nh + 1) * 512], ALU.mult),
                        reads=[PB[4 + nh], "sinO"], writes=["prs2"])
                    p.emit("v", lambda en, nh=nh, par=par: en.tensor_tensor(
                        sc3[:, nh * 512:(nh + 1) * 512], pb[par * 2 + nh][:, :],
                        cosO[:, nh * 512:(nh + 1) * 512], ALU.mult),
                        reads=[pjn[nh], "cosO"], writes=["prs3"])
                p.emit("v", lambda en: en.tensor_tensor(
                    sc3[:, :], sc3[:, :], sc2[:, :], ALU.add),
                    reads=["prs3", "prs2"], writes=["prs3"])
                def spq(en, m=m):
                    return en.dma_start(out=bass.AP(
                        sqsp, m * 128,
                        [[H * 128, 128], [128 * H * 128, NCH], [1, 128]]),
                        in_=sc3[:, :].rearrange("p (c t) -> p c t", c=NCH))
                p.emit("q", spq, reads=["prs3"], writes=["qspd"], sem="qspd", inc=16)
                yield
            for m in range(H):
                par = m % 2
                stripe = wstA if par == 0 else wstB
                wb = "wstA" if par == 0 else "wstB"
                dma("g", stripe[:, :], dWg[m, :, :], writes=[wb], sem="wsA" if par == 0 else "wsB")
                pjn = [PB[par * 2], PB[par * 2 + 1]]
                fns = []
                for k in range(NK):
                    for nh in range(2):
                        fns.append(lambda en, k=k, nh=nh, stripe=stripe, par=par:
                                   en.matmul(
                            pb[par * 2 + nh][:, :],
                            stripe[:, k * 128:(k + 1) * 128],
                            xn_src[:, k * T + nh * 512:k * T + (nh + 1) * 512],
                            start=(k == 0), stop=(k == NK - 1)))
                p.group("t", fns, reads=[wb, xnname], writes=pjn)
                p.emit("s", lambda en, par=par: en.activation(
                    sc3[:, 0:512], pb[par * 2][:, :], AF.Silu),
                    reads=[pjn[0]], writes=["prs3"])
                p.emit("s", lambda en, par=par: en.activation(
                    sc3[:, 512:1024], pb[par * 2 + 1][:, :], AF.Silu),
                    reads=[pjn[1]], writes=["prs3"])
                def spg(en, m=m):
                    return en.dma_start(out=bass.AP(
                        ssgsp, m * 128,
                        [[H * 128, 128], [128 * H * 128, NCH], [1, 128]]),
                        in_=sc3[:, :].rearrange("p (c t) -> p c t", c=NCH))
                p.emit("q", spg, reads=["prs3"], writes=["sgspd"], sem="sgspd", inc=16)
                yield

        def phase_ffnpart(xn_src, xnname):
            stf = [pv(0, 128, 4096), pv(4096, 128, 4096)]
            xTm = pv(8192, 128, 512, F32)
            t2 = pv(9216, 128, 512, F32)
            outst = pv(10240, 128, 512, F32)
            for m in range(NK):
                par = m % 2
                fb_ = "wstA" if par == 0 else "wstB"
                dma("g", stf[par][:, :], dWf[m, :, :], writes=[fb_],
                    sem="wsA" if par == 0 else "wsB")
                for nh in range(2):
                    dma("q", xTm[:, :],
                        dxT[m * 128:(m + 1) * 128, nh * 512:(nh + 1) * 512],
                        writes=["o_xT"], sem="o_xT")
                    fns = []
                    for k in range(NK):
                        fns.append(lambda en, k=k, nh=nh, par=par: en.matmul(
                            pb[7][:, :],
                            stf[par][:, k * 128:(k + 1) * 128],
                            xn_src[:, k * T + nh * 512:k * T + (nh + 1) * 512],
                            start=(k == 0), stop=(k == NK - 1)))
                    p.group("t", fns, reads=[fb_, xnname], writes=["pb7f"])
                    p.emit("v", lambda en, m=m, nh=nh: en.scalar_tensor_tensor(
                        t2, pb[7][:, :], rowselc[:, m:m + 1],
                        qselB[:, nh * 512:(nh + 1) * 512],
                        ALU.mult, ALU.mult),
                        reads=["pb7f", "rowselc", "qselB"], writes=["o_t2"])
                    p.emit("v", lambda en, m=m, nh=nh: en.scalar_tensor_tensor(
                        t2, qselB[:, nh * 512:(nh + 1) * 512],
                        biasrs[:, m:m + 1], t2, ALU.mult, ALU.add),
                        reads=["qselB", "biasrs", "o_t2"], writes=["o_t2"])
                    p.emit("v", lambda en: en.tensor_tensor(
                        outst, t2, xTm[:, :], ALU.add),
                        reads=["o_t2", "o_xT"], writes=["o_out"])
                    dma("q", spart.ap()[m * 128:(m + 1) * 128,
                                        nh * 512:(nh + 1) * 512],
                        outst, reads=["o_out"], writes=["partd"], sem="partd")
                    yield

        def phase_wo():
            stw = [pv(0, 128, 4096), pv(4096, 128, 4096)]
            prt = pv(16384, 128, 1024, F32)
            t4 = pv(20480, 128, 512, F32)
            outst = pv(21504, 128, 512, F32)
            for m in range(NK):
                par = m % 2
                wb_ = "wstA" if par == 0 else "wstB"
                dma("g", stw[par][:, :], dWo[m, :, :], writes=[wb_], sem="wsA" if par == 0 else "wsB")
                dma("q", prt[:, :], spart.ap()[m * 128:(m + 1) * 128, :],
                    reads=["partd"], writes=["o_prt"])
                pjw = [PB[par * 2], PB[par * 2 + 1]]
                fns = []
                for k in range(NK):
                    for nh in range(2):
                        fns.append(lambda en, k=k, nh=nh, par=par: en.matmul(
                            pb[par * 2 + nh][:, :],
                            stw[par][:, k * 128:(k + 1) * 128],
                            bigA[:, k * T + nh * 512:k * T + (nh + 1) * 512],
                            start=(k == 0), stop=(k == NK - 1)))
                p.group("t", fns, reads=[wb_, "yf"], writes=pjw)
                for nh in range(2):
                    p.emit("v", lambda en, nh=nh, par=par: en.scalar_tensor_tensor(
                        t4, pb[par * 2 + nh][:, :], 1.0,
                        prt[:, nh * 512:(nh + 1) * 512], ALU.mult, ALU.add),
                        reads=[pjw[nh], "o_prt"], writes=["o_t4"])
                    p.emit("v", lambda en: en.tensor_copy(outst, t4),
                           reads=["o_t4"], writes=["o_out"])
                    dma("q", dout[m * 128:(m + 1) * 128, nh * 512:(nh + 1) * 512],
                        outst, reads=["o_out"], writes=["outd"])

        def phase_debug():
            if not DEBUG:
                return
            for n, t_ in [("dmax", dmax), ("selc", selc), ("rowselc", rowselc),
                          ("qselB", qselB), ("biasrs", biasrs), ("qrowf", qrowf),
                          ("states", states), ("dtsb", dtsb), ("decB", decB),
                          ("negG", negG), ("ekd", ekd)]:
                if t_.dtype if False else True:
                    pass
                def mk(en, n=n, t_=t_):
                    return en.dma_start(out=ddbg[n], in_=t_[:, :])
                p.emit("g", mk, reads=[n if n in p.bufs else "outd"],
                       extra_deps=[("v", p.cnt["v"]), ("s", p.cnt["s"]),
                                   ("t", p.cnt["t"])], sem="dbgd", inc=16)

        def run2(ga, gb):
            ga, gb = iter(ga), iter(gb)
            da = db = False
            while not (da and db):
                if not da:
                    try:
                        next(ga)
                    except StopIteration:
                        da = True
                if not db:
                    try:
                        next(gb)
                    except StopIteration:
                        db = True

        def drain(g):
            for _ in g:
                pass

        phase_ln(dxw, bigA, "xnA", "lw")
        p.fence()
        phase_kvdt(bigA[:, :], "xnA", cosW, "cosW", sinW, "sinW", "w",
                   sksp_w, "kspw", svsp_w, "vspw",
                   dtsbW, "dtsbW", dteW, "dteW", decBW, "decBW")
        p.fence()
        phase_ln(dx, bigA, "xnA", "lo")
        p.fence()
        phase_kvdt(bigA[:, :], "xnA", cosO, "cosO", sinO, "sinO", "o",
                   sksp, "ksp", svsp, "vsp",
                   dtsb, "dtsb", dte, "dte", decB, "decB")
        p.fence()
        run2(phase_qgate(bigA[:, :], "xnA"), phase_p1())
        p.fence()
        run2(phase_ffnpart(bigA[:, :], "xnA"),
             phase_attn(False, "wa", sksp_w, svsp_w, "kspw", "vspw",
                        [0, 1], [0, 1], [0, 1], 2,
                        dtsbW, "dtsbW", dteW, "dteW", decBW, "decBW"))
        p.fence()
        drain(phase_attn(True, "oa", sksp, svsp, "ksp", "vsp",
                         [0, 1], [2, 3], [4, 5], 6,
                         dtsb, "dtsb", dte, "dte", decB, "decB"))
        p.fence()
        phase_wo()
        phase_debug()
        with nc.Block() as block:
            p.replay(nc, es, block)

    return nc


_CACHE = {}


def _blk(w, n_m):
    return np.ascontiguousarray(
        w.reshape(n_m, 128, NK, 128).transpose(0, 3, 2, 1).reshape(n_m, 128, HID))


def _rope_tables(pos):
    inv = 1.0 / ROPE_THETA ** (np.arange(0, D, 2, dtype=np.float64) / D)
    ang = np.concatenate([inv, inv])[:, None] * pos[None, :].astype(np.float64)
    return (np.ascontiguousarray(np.cos(ang).astype(np.float32)),
            np.ascontiguousarray(np.sin(ang).astype(np.float32)))


def kernel(L0_attn_hidden_states, L0_attn_Wq, L0_attn_Wk, L0_attn_Wv,
           L0_attn_Wo, L0_attn_Wg, L0_attn_Win, L0_attn_dt_bias,
           L0_attn_A_log_bias, L0_attn_gnorm_w, L0_ffn_weight, L0_ffn_bias,
           L0_ffn_centroids, L0_ffn_assignments):
    x = np.ascontiguousarray(np.asarray(L0_attn_hidden_states, dtype=np.float32))
    Wq = np.asarray(L0_attn_Wq, np.float32)
    Wk = np.asarray(L0_attn_Wk, np.float32)
    Wv = np.asarray(L0_attn_Wv, np.float32)
    Wo = np.asarray(L0_attn_Wo, np.float32)
    Wg = np.asarray(L0_attn_Wg, np.float32)
    Win = np.asarray(L0_attn_Win, np.float32)
    dtb = np.asarray(L0_attn_dt_bias, np.float32)
    Alog = np.asarray(L0_attn_A_log_bias, np.float32)
    gnw = np.asarray(L0_attn_gnorm_w, np.float32)
    Wf = np.asarray(L0_ffn_weight, np.float32)
    fb = np.asarray(L0_ffn_bias, np.float32)
    cent = np.asarray(L0_ffn_centroids, np.float32)
    assign = np.asarray(L0_ffn_assignments, np.int32)

    if "nc" not in _CACHE:
        _CACHE["nc"] = build_program()
    nc = _CACHE["nc"]

    xTall = np.ascontiguousarray(x.reshape(ALLT, HID).T)
    Wq_b = _blk(Wq, 32)
    Wk_b = _blk(Wk, 8)
    Wv_b = _blk(Wv, 8)
    Wg_b = _blk(Wg, 32)
    Wo_b = _blk(Wo, 32)
    Wf_b = _blk(Wf, 32)
    Win_t = np.ascontiguousarray(Win.T)
    CWO = np.ascontiguousarray(
        np.concatenate([cent.T, np.ones((HID, 1), np.float32)], axis=1))
    fb_c = np.ascontiguousarray(fb.reshape(32, 128).T)
    ident = np.eye(128, dtype=np.float32)
    S = np.zeros((128, 128), np.float32)
    S[np.arange(64), np.arange(64) + 64] = -1.0
    S[np.arange(64) + 64, np.arange(64)] = 1.0
    S_T = np.ascontiguousarray(S.T)
    mask1 = np.where(np.arange(128)[None, :] >= np.arange(128)[:, None],
                     0.0, NEG_BIG).astype(np.float32)
    mask4 = np.ascontiguousarray(np.tile(mask1, (1, 4)))
    iota = np.arange(64, dtype=np.float32).reshape(64, 1)

    in_maps = []
    for c in range(NCORES):
        b, s = divmod(c, 4)
        t0 = s * T
        cos_o, sin_o = _rope_tables(np.arange(t0, t0 + T))
        w0 = max(s - 1, 0) * T
        cos_w, sin_w = _rope_tables(np.arange(w0, w0 + T))
        xw = (np.zeros((T, HID), np.float32) if s == 0
              else np.ascontiguousarray(x[b, t0 - T:t0, :]))
        in_maps.append({
            "x_own": np.ascontiguousarray(x[b, t0:t0 + T, :]),
            "x_win": xw,
            "xT_own": np.ascontiguousarray(x[b, t0:t0 + T, :].T),
            "xT_all": xTall,
            "Wq_blk": Wq_b, "Wk_blk": Wk_b, "Wv_blk": Wv_b,
            "Wg_blk": Wg_b, "Wo_blk": Wo_b, "Wf_blk": Wf_b,
            "Win_t": Win_t, "CWO": CWO,
            "ffn_b": fb_c,
            "dt_b": dtb.reshape(32, 1),
            "A_log": Alog.reshape(32, 1),
            "gnorm": gnw.reshape(128, 1),
            "assign": assign.reshape(1, HID),
            "ident": ident, "S_T": S_T, "maskneg4": mask4,
            "cos_own": cos_o, "sin_own": sin_o,
            "cos_win": cos_w, "sin_win": sin_w,
            "iota64_in": iota,
            "own_off": np.array([[b * 32 + s * 8]], np.int32),
        })

    res = run_bass_kernel_spmd(nc, in_maps, core_ids=list(range(NCORES)))
    out = np.empty((B, L, HID), np.float32)
    for c in range(NCORES):
        b, s = divmod(c, 4)
        out[b, s * T:(s + 1) * T, :] = res.results[c]["out"].T
    return out


# version 06
# speedup vs baseline: 1.1396x; 1.1396x over previous
import numpy as np
import concourse.bass as bass
import concourse.mybir as mybir
from concourse.bass_utils import run_bass_kernel_spmd

F32 = mybir.dt.float32
BF16 = mybir.dt.bfloat16
I32 = mybir.dt.int32
AF = mybir.ActivationFunctionType
ALU = mybir.AluOpType
AX = mybir.AxisListType

B, L, HID = 2, 4096, 4096
NCORES = 8
T = 1024
NT = 8
NCH = 8
NK = 32
H, KVH, D = 32, 8, 128
ROPE_THETA = 10000.0
RMS_EPS = 1e-5
LN_EPS = 1e-5
FFN_T = 0.1
FFN_TH = 0.01
ALLT = B * L
NEG_BIG = -1.0e30
DEBUG = False


class Prog:
    ENGINES = ("t", "v", "s", "g", "q")
    WAIT_ALL = ("CONSTg", "CONSTq")

    def __init__(self):
        self.ops = {e: [] for e in self.ENGINES}
        self.cnt = {e: 0 for e in self.ENGINES}
        self.bufs = {}

    def _bump(self, key, inc):
        self.cnt[key] = self.cnt.get(key, 0) + inc

    def _deps(self, reads, writes, extra):
        deps = []
        for b in reads:
            st = self.bufs.get(b)
            if st is not None and st[0] is not None:
                deps.append(st[0])
        for b in writes:
            st = self.bufs.get(b)
            if st is not None:
                if st[0] is not None:
                    deps.append(st[0])
                deps.extend(st[1])
        for d in extra:
            if d is not None:
                deps.append(d)
        best = {}
        for (e, v) in deps:
            if best.get(e, -1) < v:
                best[e] = v
        return list(best.items())

    def emit(self, eng, fn, reads=(), writes=(), extra_deps=(), inc=None,
             sem=None):
        key = sem if sem is not None else eng
        if inc is None:
            inc = 16 if key not in self.ENGINES else (
                0 if eng in ("g", "q") else 1)
            if eng not in ("g", "q") and key in self.ENGINES:
                inc = 1
        deps = self._deps(reads, writes, extra_deps)
        self._bump(key, inc)
        tok = (key, self.cnt[key])
        self.ops[eng].append((fn, deps, inc, key))
        for b in writes:
            self.bufs[b] = [tok, []]
        for b in reads:
            self.bufs.setdefault(b, [None, []])[1].append(tok)
        return tok

    def group(self, eng, fns, reads=(), writes=(), extra_deps=()):
        deps = self._deps(reads, writes, extra_deps)
        inc = 1
        for i, fn in enumerate(fns):
            self.ops[eng].append((fn, deps if i == 0 else [],
                                  inc if i == len(fns) - 1 else 0, eng))
        self._bump(eng, inc)
        tok = (eng, self.cnt[eng])
        for b in writes:
            self.bufs[b] = [tok, []]
        for b in reads:
            self.bufs.setdefault(b, [None, []])[1].append(tok)
        return tok

    def fence(self):
        snap = dict(self.cnt)
        for e in self.ENGINES:
            deps = [(k, v) for k, v in snap.items() if k != e and v > 0]
            self.emit(e, lambda en: en.nop(), extra_deps=deps, inc=0)

    def replay(self, nc, es, block):
        sems = {}
        for k in self.cnt:
            sems[k] = es.enter_context(nc.semaphore(f"sm_{k}"))
        totals = dict(self.cnt)

        def body(eng_name):
            def run(en):
                waited = {}
                for (fn, deps, inc, key) in self.ops[eng_name]:
                    for (dk, dv) in deps:
                        if dk == eng_name and eng_name == "t":
                            continue
                        if dk in self.WAIT_ALL:
                            dv = totals[dk]
                        if waited.get(dk, 0) >= dv:
                            continue
                        en.wait_ge(sems[dk], dv)
                        waited[dk] = dv
                    inst = fn(en)
                    if inc > 0:
                        inst.then_inc(sems[key], inc)
                if eng_name == "q":
                    for k, tv in totals.items():
                        if k not in self.ENGINES and tv > 0 and \
                                waited.get(k, 0) < tv:
                            en.wait_ge(sems[k], tv)
                elif totals.get(eng_name, 0) > 0:
                    en.wait_ge(sems[eng_name], totals[eng_name])
            return run

        block.tensor(body("t"))
        block.vector(body("v"))
        block.scalar(body("s"))
        block.gpsimd(body("g"))
        block.sync(body("q"))


PB = [f"pb{i}" for i in range(8)]


def build_program():
    nc = bass.Bass("TRN2", target_bir_lowering=False)

    dx = nc.dram_tensor("x_own", [T, HID], F32, kind="ExternalInput").ap()
    dxw = nc.dram_tensor("x_win", [T, HID], F32, kind="ExternalInput").ap()
    dxT = nc.dram_tensor("xT_own", [HID, T], F32, kind="ExternalInput").ap()
    dxTall = nc.dram_tensor("xT_all", [HID, ALLT], F32, kind="ExternalInput").ap()
    dWq = nc.dram_tensor("Wq_blk", [32, 128, HID], F32, kind="ExternalInput").ap()
    dWk = nc.dram_tensor("Wk_blk", [8, 128, HID], F32, kind="ExternalInput").ap()
    dWv = nc.dram_tensor("Wv_blk", [8, 128, HID], F32, kind="ExternalInput").ap()
    dWg = nc.dram_tensor("Wg_blk", [32, 128, HID], F32, kind="ExternalInput").ap()
    dWo = nc.dram_tensor("Wo_blk", [32, 128, HID], F32, kind="ExternalInput").ap()
    dWf = nc.dram_tensor("Wf_blk", [32, 128, HID], F32, kind="ExternalInput").ap()
    dWin = nc.dram_tensor("Win_t", [HID, 32], F32, kind="ExternalInput").ap()
    dCWO = nc.dram_tensor("CWO", [HID, 65], F32, kind="ExternalInput").ap()
    dfb = nc.dram_tensor("ffn_b", [128, 32], F32, kind="ExternalInput").ap()
    ddtb = nc.dram_tensor("dt_b", [32, 1], F32, kind="ExternalInput").ap()
    dAlog = nc.dram_tensor("A_log", [32, 1], F32, kind="ExternalInput").ap()
    dgn = nc.dram_tensor("gnorm", [128, 1], F32, kind="ExternalInput").ap()
    dassign = nc.dram_tensor("assign", [1, HID], I32, kind="ExternalInput").ap()
    dI = nc.dram_tensor("ident", [128, 128], F32, kind="ExternalInput").ap()
    dS_T = nc.dram_tensor("S_T", [128, 128], F32, kind="ExternalInput").ap()
    dmask = nc.dram_tensor("maskneg4", [128, 512], F32, kind="ExternalInput").ap()
    dcos_o = nc.dram_tensor("cos_own", [128, T], F32, kind="ExternalInput").ap()
    dsin_o = nc.dram_tensor("sin_own", [128, T], F32, kind="ExternalInput").ap()
    dcos_w = nc.dram_tensor("cos_win", [128, T], F32, kind="ExternalInput").ap()
    dsin_w = nc.dram_tensor("sin_win", [128, T], F32, kind="ExternalInput").ap()
    diota = nc.dram_tensor("iota64_in", [64, 1], F32, kind="ExternalInput").ap()
    doff = nc.dram_tensor("own_off", [1, 1], I32, kind="ExternalInput").ap()

    dout = nc.dram_tensor("out", [HID, T], F32, kind="ExternalOutput").ap()
    if DEBUG:
        ddbg = {n: nc.dram_tensor(f"dbg_{n}", sh, F32, kind="ExternalOutput").ap()
                for n, sh in [("dmax", [64, 1]), ("selc", [64, 1]),
                              ("rowselc", [128, 32]), ("qselB", [128, T]),
                              ("biasrs", [128, 32]), ("qrowf", [1, T]),
                              ("states", [128, H * 128]),
                              ("dtsb", [128, 256]), ("decB", [128, 256]),
                              ("negG", [128, 256]), ("ekd", [128, 256])]}

    sGd = nc.dram_tensor("Gdram", [32, T], F32)
    sEGb = nc.dram_tensor("EGbf", [32, T], BF16)
    sdec = nc.dram_tensor("decdram", [32, 8], BF16)
    sqsp = nc.dram_tensor("qspill", [NCH, 128, H, 128], BF16)
    ssgsp = nc.dram_tensor("sgspill", [NCH, 128, H, 128], BF16)
    sksp = nc.dram_tensor("kspill", [NCH, 128, KVH, 128], BF16)
    svsp = nc.dram_tensor("vspill", [NCH, 128, KVH, 128], BF16)
    sksp_w = nc.dram_tensor("kspill_w", [NCH, 128, KVH, 128], BF16)
    svsp_w = nc.dram_tensor("vspill_w", [NCH, 128, KVH, 128], BF16)
    spart = nc.dram_tensor("spartial", [HID, T], F32)
    sqsel = nc.dram_tensor("qseld", [128, 64], BF16)
    srowd = nc.dram_tensor("rowseld", [1, HID], F32)

    p = Prog()

    from contextlib import ExitStack
    es = ExitStack()
    SBt = lambda name, shape, dt: es.enter_context(nc.sbuf_tensor(name, shape, dt))
    PSt = lambda name, shape, dt: es.enter_context(nc.psum_tensor(name, shape, dt))

    with es:
        bigA = SBt("bigA", [128, NK * T], BF16)
        states = SBt("states", [128, H * 128], F32)
        dtsb = SBt("dtsb", [128, 256], F32)
        ekd = SBt("ekd", [128, 256], F32)
        dte = SBt("dte", [128, 256], F32)
        decB = SBt("decB", [128, 256], F32)
        dtsbW = SBt("dtsbW", [128, 256], F32)
        dteW = SBt("dteW", [128, 256], F32)
        decBW = SBt("decBW", [128, 256], F32)
        negG = SBt("negG", [128, 256], F32)
        wint = SBt("wint", [128, NK * 32], BF16)
        cwot = SBt("cwot", [128, NK * 65], BF16)
        I_bf = SBt("I_bf", [128, 128], BF16)
        I_f = SBt("I_f", [128, 128], F32)
        S_Tc = SBt("S_Tc", [128, 128], BF16)
        maskc = SBt("maskc", [128, 512], F32)
        cosO = SBt("cosO", [128, T], BF16)
        sinO = SBt("sinO", [128, T], BF16)
        cosW = SBt("cosW", [128, T], BF16)
        sinW = SBt("sinW", [128, T], BF16)
        onesr = SBt("onesr", [1, 128], BF16)
        onesc = SBt("onesc", [128, 1], BF16)
        c_eps = SBt("c_eps", [128, 1], F32)
        c_one = SBt("c_one", [128, 1], F32)
        iota64 = SBt("iota64", [64, 1], F32)
        alog = SBt("alog", [32, 1], F32)
        expA = SBt("expA", [32, 1], F32)
        dtbias = SBt("dtbias", [32, 1], F32)
        gnormc = SBt("gnormc", [128, 1], F32)
        ffnb = SBt("ffnb", [128, 32], F32)
        zeros32 = SBt("zeros32", [32, 128], F32)
        qselB = SBt("qselB", [128, T], BF16)
        rowselc = SBt("rowselc", [128, 32], F32)
        biasrs = SBt("biasrs", [128, 32], F32)
        dmax = SBt("dmax", [64, 1], F32)
        tmax = SBt("tmax", [64, 1], F32)
        selc = SBt("selc", [64, 1], F32)
        selbf = SBt("selbf", [64, 1], BF16)
        qrowf = SBt("qrowf", [1, T], BF16)
        Scol = SBt("Scol", [128, 1], F32)
        Scbf = SBt("Scbf", [1, 65], BF16)
        ScB = SBt("ScB", [128, 64], F32)
        offsb = SBt("offsb", [1, 1], I32)
        POOLN = 23808
        pool = SBt("pool", [128, POOLN], BF16)
        qcb = [SBt(f"qcb{i}", [128, H * 128], BF16) for i in range(2)]
        sgcb = [SBt(f"sgcb{i}", [128, H * 128], BF16) for i in range(2)]
        kcb = [SBt(f"kcb{i}", [128, KVH * 128], BF16) for i in range(2)]
        vcb = [SBt(f"vcb{i}", [128, KVH * 128], BF16) for i in range(2)]
        pb = [PSt(f"psb{i}", [128, 512], F32) for i in range(8)]

        roff = es.enter_context(nc.sync.register("roff"))

        def pv(off, part, free, dt=BF16):
            if dt == BF16:
                return pool[0:part, off:off + free]
            return pool[:, off:off + 2 * free].bitcast(dt)[0:part, 0:free]

        def dma(qe, out_ap, in_ap, reads=(), writes=(), deps=(), slow=False,
                sem=None):
            def fn(en, out_ap=out_ap, in_ap=in_ap, slow=slow):
                return en.dma_start(out=out_ap, in_=in_ap,
                                    allow_slow_non_contiguous=slow)
            if sem is None:
                sem = writes[0] if writes else "misc_dma"
            return p.emit(qe, fn, reads=reads, writes=writes, extra_deps=deps,
                          sem=sem, inc=16)
        
        dma("g", I_bf[:, :], dI, writes=["I_bf"], sem="CONSTg")
        dma("q", I_f[:, :], dI, writes=["I_f"], sem="CONSTq")
        dma("g", S_Tc[:, :], dS_T, writes=["S_T"], sem="CONSTg")
        dma("q", maskc[:, :], dmask, writes=["maskc"], sem="CONSTq")
        dma("g", cosO[:, :], dcos_o, writes=["cosO"], sem="CONSTg")
        dma("g", sinO[:, :], dsin_o, writes=["sinO"], sem="CONSTg")
        dma("g", cosW[:, :], dcos_w, writes=["cosW"], sem="CONSTg")
        dma("g", sinW[:, :], dsin_w, writes=["sinW"], sem="CONSTg")
        dma("q", iota64[:, :], diota, writes=["iota64"], sem="CONSTq")
        dma("q", alog[:, :], dAlog, writes=["alog"], sem="CONSTq")
        dma("q", dtbias[:, :], ddtb, writes=["dtbias"], sem="CONSTq")
        dma("q", gnormc[:, :], dgn, writes=["gnormc"], sem="CONSTq")
        dma("q", ffnb[:, :], dfb, writes=["ffnb"], sem="CONSTq")
        dma("q", offsb[:, :], doff, writes=["offsb"], sem="CONSTq")
        dma("g", wint[:, :].rearrange("p (k h) -> p k h", k=NK),
            dWin.rearrange("(k p) h -> p k h", p=128), writes=["wint"], sem="CONSTg")
        dma("g", cwot[:, :].rearrange("p (k c) -> p k c", k=NK),
            dCWO.rearrange("(k p) c -> p k c", p=128), writes=["cwot"], sem="CONSTg")
        p.emit("v", lambda en: en.memset(onesr[:, :], 1.0), writes=["onesr"])
        p.emit("v", lambda en: en.memset(c_eps[:, :], LN_EPS), writes=["c_eps"])
        p.emit("v", lambda en: en.memset(c_one[:, :], 1.0), writes=["c_one"])
        p.emit("v", lambda en: en.memset(onesc[:, :], 1.0), writes=["onesc"])
        p.emit("v", lambda en: en.memset(zeros32[:, :], 0.0), writes=["zeros32"])
        p.emit("v", lambda en: en.memset(states[:, :], 0.0), writes=["states"])
        p.emit("v", lambda en: en.memset(dmax[:, :], NEG_BIG), writes=["dmax"])
        p.emit("s", lambda en: en.activation(expA[:, :], alog[:, :], AF.Exp),
               reads=["alog"], writes=["expA"])
        p.emit("q", lambda en: en.reg_load(roff, offsb[0:1, 0:1]),
               reads=["offsb"], writes=["roff"], inc=0)

        def phase_p1():
            xt = [pv(11520, 128, 512), pv(12032, 128, 512)]
            xq = [pv(12544, 128, 512), pv(13056, 128, 512)]
            dots = pv(13568, 65, 512, F32)
            dotsB = pv(14592, 65, 512, F32)
            a_sb = pv(15616, 128, 64, F32)
            e_sb = pv(15744, 128, 64, F32)
            col = lambda i: pv(15872 + i * 2, 128, 1, F32)
            mcol, m2c, varc, rstdc, zmaxc, nzmaxc, Zc, rZc, qmx = (
                col(i) for i in range(9))
            dots_bf = pv(15936, 128, 64)
            qse = pv(16000, 128, 1, F32)
            Mblk = pv(16064, 64, 512)
            ai_blk = pv(16576, 1, 512, I32)
            ab_blk = pv(17600, 1, 512)
            rowblk = pv(18112, 1, 512, F32)

            fns = [(lambda en, k=k: en.matmul(
                pb[7][0:65, 0:1], cwot[:, k * 65:(k + 1) * 65], onesc[:, :],
                start=(k == 0), stop=(k == NK - 1))) for k in range(NK)]
            p.group("t", fns, reads=["cwot", "onesc"], writes=["pb7"])
            p.emit("s", lambda en: en.copy(Scol[0:65, 0:1], pb[7][0:65, 0:1]),
                   reads=["pb7"], writes=["Scol"])
            p.emit("t", lambda en: en.transpose(
                pb[6][0:1, 0:65], Scol[0:65, 0:1], I_f[0:65, 0:65]),
                reads=["Scol", "I_f"], writes=["pb6"])
            p.emit("s", lambda en: en.copy(Scbf[0:1, 0:65], pb[6][0:1, 0:65]),
                   reads=["pb6"], writes=["Scbf"])
            p.emit("t", lambda en: en.matmul(
                pb[6][:, 0:64], onesr[0:1, :], Scbf[0:1, 0:64],
                start=True, stop=True),
                reads=["onesr", "Scbf"], writes=["pb6"])
            p.emit("s", lambda en: en.copy(ScB[:, :], pb[6][:, 0:64]),
                   reads=["pb6"], writes=["ScB"])
            yield

            for tg in range(16):
                for k in range(NK):
                    par = k % 2
                    bx, bq = f"p1x{par}", f"p1q{par}"
                    dma("g", xt[par][:, :],
                        dxTall[k * 128:(k + 1) * 128, tg * 512:(tg + 1) * 512],
                        writes=[bx])
                    p.emit("s", lambda en, par=par: en.activation(
                        xq[par][:, :], xt[par][:, :], AF.Square),
                        reads=[bx], writes=[bq])
                    fns = [
                        (lambda en, k=k, par=par: en.matmul(
                            pb[6][0:65, :], cwot[:, k * 65:(k + 1) * 65],
                            xt[par][:, :],
                            start=(k == 0), stop=(k == NK - 1))),
                        (lambda en, k=k, par=par: en.matmul(
                            pb[7][0:65, :], cwot[:, k * 65:(k + 1) * 65],
                            xq[par][:, :],
                            start=(k == 0), stop=(k == NK - 1))),
                    ]
                    p.group("t", fns, reads=[bx, bq, "cwot"],
                            writes=["pb6", "pb7"])
                    if k % 8 == 7:
                        yield
                p.emit("s", lambda en: en.copy(dots[:, :], pb[6][0:65, :]),
                       reads=["pb6"], writes=["p1d"])
                p.emit("s", lambda en: en.copy(dotsB[:, :], pb[7][0:65, :]),
                       reads=["pb7"], writes=["p1dB"])
                for j in range(4):
                    tile = tg * 4 + j
                    p.emit("t", lambda en, j=j: en.transpose(
                        pb[6][:, 0:65], dots[:, j * 128:(j + 1) * 128],
                        I_f[0:65, 0:65]),
                        reads=["p1d", "I_f"], writes=["pb6"])
                    p.emit("t", lambda en, j=j: en.transpose(
                        pb[7][:, 0:65], dotsB[:, j * 128:(j + 1) * 128],
                        I_f[0:65, 0:65]),
                        reads=["p1dB", "I_f"], writes=["pb7"])
                    p.emit("v", lambda en: en.tensor_scalar(
                        mcol, pb[6][:, 64:65], 1.0 / HID, None, ALU.mult),
                        reads=["pb6"], writes=["p1m"])
                    p.emit("v", lambda en: en.tensor_tensor(
                        m2c, mcol, mcol, ALU.mult),
                        reads=["p1m"], writes=["p1m2"])
                    p.emit("v", lambda en: en.scalar_tensor_tensor(
                        varc, pb[7][:, 64:65], 1.0 / HID, m2c,
                        ALU.mult, ALU.subtract),
                        reads=["pb7", "p1m2"], writes=["p1var"])
                    p.emit("s", lambda en: en.activation(
                        rstdc, varc, AF.Ln, bias=c_eps[:, :]),
                        reads=["p1var", "c_eps"], writes=["p1rstd"])
                    p.emit("s", lambda en: en.activation(
                        rstdc, rstdc, AF.Exp, scale=-0.5),
                        reads=["p1rstd"], writes=["p1rstd"])
                    p.emit("v", lambda en: en.tensor_scalar(
                        m2c, mcol, -1.0, None, ALU.mult),
                        reads=["p1m"], writes=["p1m2"])
                    p.emit("v", lambda en: en.scalar_tensor_tensor(
                        a_sb, ScB[:, :], m2c, pb[6][:, 0:64],
                        ALU.mult, ALU.add),
                        reads=["ScB", "p1m2", "pb6"], writes=["p1a"])
                    p.emit("v", lambda en: en.tensor_scalar(
                        a_sb, a_sb, rstdc, 1.0 / FFN_T, ALU.mult, ALU.mult),
                        reads=["p1a", "p1rstd"], writes=["p1a"])
                    p.emit("v", lambda en: en.reduce_max(zmaxc, a_sb, axis=AX.X),
                           reads=["p1a"], writes=["p1zm"])
                    p.emit("v", lambda en: en.tensor_scalar(
                        nzmaxc, zmaxc, -1.0, None, ALU.mult),
                        reads=["p1zm"], writes=["p1nzm"])
                    p.emit("s", lambda en: en.activation(
                        e_sb, a_sb, AF.Exp, bias=nzmaxc),
                        reads=["p1a", "p1nzm"], writes=["p1e"])
                    p.emit("v", lambda en: en.reduce_sum(Zc, e_sb, axis=AX.X),
                           reads=["p1e"], writes=["p1Z"])
                    p.emit("v", lambda en: en.reciprocal(rZc, Zc),
                           reads=["p1Z"], writes=["p1rZ"])
                    p.emit("v", lambda en: en.tensor_scalar(
                        dots_bf, e_sb, rZc, None, ALU.mult),
                        reads=["p1e", "p1rZ"], writes=["p1dbf"])
                    p.emit("v", lambda en: en.reduce_max(qmx, dots_bf, axis=AX.X),
                           reads=["p1dbf"], writes=["p1qmx"])
                    p.emit("v", lambda en: en.tensor_scalar(
                        qse, qmx, FFN_TH, None, ALU.is_gt),
                        reads=["p1qmx"], writes=["p1qse"])
                    dma("g", bass.AP(sqsel, tile, [[64, 128], [1, 1]]), qse,
                        reads=["p1qse"], writes=["qseld"], slow=True)
                    p.emit("t", lambda en: en.transpose(
                        pb[7][:, :].bitcast(BF16)[0:64, 0:128],
                        dots_bf, I_bf[:, :]),
                        reads=["p1dbf", "I_bf"], writes=["pb7"])
                    p.emit("v", lambda en: en.reduce_max(
                        tmax[:, :], pb[7][:, :].bitcast(BF16)[0:64, 0:128],
                        axis=AX.X),
                        reads=["pb7"], writes=["tmax"])
                    p.emit("v", lambda en: en.tensor_tensor(
                        dmax[:, :], dmax[:, :], tmax[:, :], ALU.max),
                        reads=["tmax", "dmax"], writes=["dmax"])
                    yield

            p.emit("v", lambda en: en.tensor_scalar(
                selc[:, :], dmax[:, :], FFN_TH, None, ALU.is_gt),
                reads=["dmax"], writes=["selc"])
            p.emit("v", lambda en: en.tensor_copy(selbf[:, :], selc[:, :]),
                   reads=["selc"], writes=["selbf"])
            for och in range(8):
                dma("q", ai_blk, dassign[0:1, och * 512:(och + 1) * 512],
                    writes=["p1ai"])
                p.emit("v", lambda en: en.tensor_copy(ab_blk, ai_blk),
                       reads=["p1ai"], writes=["p1ab"])
                p.emit("t", lambda en: en.matmul(
                    pb[6][0:64, :], onesr[0:1, 0:64], ab_blk,
                    start=True, stop=True),
                    reads=["onesr", "p1ab"], writes=["pb6"])
                p.emit("v", lambda en: en.tensor_scalar(
                    Mblk, pb[6][0:64, :], iota64[:, :], None, ALU.is_equal),
                    reads=["pb6", "iota64"], writes=["p1M"])
                p.emit("t", lambda en: en.matmul(
                    pb[7][0:1, :], selbf[:, :], Mblk, start=True, stop=True),
                    reads=["selbf", "p1M"], writes=["pb7"])
                p.emit("v", lambda en: en.tensor_scalar(
                    rowblk, pb[7][0:1, :], 0.5, None, ALU.is_gt),
                    reads=["pb7"], writes=["p1row"])
                dma("q", srowd.ap()[0:1, och * 512:(och + 1) * 512], rowblk,
                    reads=["p1row"], writes=["rowd"])
                yield
            dma("g", rowselc[:, :],
                bass.AP(srowd, 0, [[1, 128], [128, 32]]),
                reads=["rowd"], writes=["rowselc"], slow=True)
            p.emit("v", lambda en: en.tensor_tensor(
                biasrs[:, :], ffnb[:, :], rowselc[:, :], ALU.mult),
                reads=["ffnb", "rowselc"], writes=["biasrs"])

            def rd_qrow(en):
                return en.dma_start(
                    out=qrowf[0:1, :].rearrange("o (tt pp) -> o tt pp", tt=8),
                    in_=bass.AP(sqsel, roff, [[0, 1], [1, 8], [64, 128]]),
                    allow_slow_non_contiguous=True)
            p.emit("q", rd_qrow, reads=["qseld", "roff"], writes=["qrowf"],
                   sem="qrowf", inc=16)
            p.emit("t", lambda en: en.matmul(
                pb[6][:, :], onesr[0:1, :], qrowf[0:1, 0:512],
                start=True, stop=True),
                reads=["onesr", "qrowf"], writes=["pb6"])
            p.emit("t", lambda en: en.matmul(
                pb[7][:, :], onesr[0:1, :], qrowf[0:1, 512:1024],
                start=True, stop=True),
                reads=["onesr", "qrowf"], writes=["pb7"])
            p.emit("s", lambda en: en.copy(qselB[:, 0:512], pb[6][:, :]),
                   reads=["pb6"], writes=["qselB"])
            p.emit("s", lambda en: en.copy(qselB[:, 512:1024], pb[7][:, :]),
                   reads=["pb7"], writes=["qselB"])
            yield

        def phase_ln(xdram, dst_h, dstname, tag):
            xt = pv(0, 128, 4096)
            xnt = pv(4096, 128, 4096)
            xsq = pv(8192, 128, 4096)
            col = lambda i: pv(12544 + i * 2, 128, 1, F32)
            scol, mcol, sscol, varc, rstdc, nmr = (col(i) for i in range(6))
            for tt in range(NT):
                dma("g", xt[:, :], xdram[tt * 128:(tt + 1) * 128, :],
                    writes=[f"{tag}x"])
                p.emit("v", lambda en: en.reduce_sum(scol, xt[:, :], axis=AX.X),
                       reads=[f"{tag}x"], writes=[f"{tag}s"])
                p.emit("v", lambda en: en.tensor_scalar(
                    mcol, scol, 1.0 / HID, None, ALU.mult),
                    reads=[f"{tag}s"], writes=[f"{tag}m"])
                p.emit("s", lambda en: en.activation(xsq[:, :], xt[:, :], AF.Square),
                       reads=[f"{tag}x"], writes=[f"{tag}xsq"])
                p.emit("v", lambda en: en.reduce_sum(sscol, xsq[:, :], axis=AX.X),
                       reads=[f"{tag}xsq"], writes=[f"{tag}ss"])
                p.emit("v", lambda en: en.tensor_tensor(varc, mcol, mcol, ALU.mult),
                       reads=[f"{tag}m"], writes=[f"{tag}var"])
                p.emit("v", lambda en: en.scalar_tensor_tensor(
                    varc, sscol, 1.0 / HID, varc, ALU.mult, ALU.subtract),
                    reads=[f"{tag}ss", f"{tag}var"], writes=[f"{tag}var"])
                p.emit("s", lambda en: en.activation(rstdc, varc, AF.Ln, bias=c_eps[:, :]),
                       reads=[f"{tag}var", "c_eps"], writes=[f"{tag}r"])
                p.emit("s", lambda en: en.activation(rstdc, rstdc, AF.Exp, scale=-0.5),
                       reads=[f"{tag}r"], writes=[f"{tag}r"])
                p.emit("v", lambda en: en.scalar_tensor_tensor(
                    nmr, mcol, -1.0, rstdc, ALU.mult, ALU.mult),
                    reads=[f"{tag}m", f"{tag}r"], writes=[f"{tag}nmr"])
                p.emit("v", lambda en: en.tensor_scalar(
                    xnt[:, :], xt[:, :], rstdc, nmr, ALU.mult, ALU.add),
                    reads=[f"{tag}x", f"{tag}r", f"{tag}nmr"], writes=[f"{tag}xn"])
                for kq in range(0, NK, 8):
                    par = (kq // 8) % 2
                    bank = pb[6 + par]
                    pbf = bank[:, :].bitcast(BF16)
                    fns = []
                    for j in range(8):
                        fns.append(lambda en, k=kq + j, j=j, pbf=pbf: en.transpose(
                            pbf[:, j * 128:(j + 1) * 128],
                            xnt[:, k * 128:(k + 1) * 128], I_bf[:, :]))
                    p.group("t", fns, reads=[f"{tag}xn", "I_bf"],
                            writes=[PB[6 + par]])
                    def evac(en, kq=kq, tt=tt, pbf=pbf):
                        dstap = bass.AP(dst_h, kq * T + tt * 128,
                                        [[NK * T, 128], [T, 8], [1, 128]])
                        return en.tensor_copy(
                            dstap, pbf[:, :].rearrange("p (j t) -> p j t", j=8))
                    p.emit("v", evac, reads=[PB[6 + par]], writes=[dstname])

        def phase_kvdt(xn_src, xnname, cosT, cname, sinT, sname, tag,
                       kdst, kdstn, vdst, vdstn, dtsbX, dtsbn, dteX, dten,
                       decBX, decBn):
            wstA = pv(0, 128, 4096)
            wstB = pv(4096, 128, 4096)
            sc1 = pv(8192, 128, 1024)
            sc2 = pv(9216, 128, 1024)
            sc3 = pv(10240, 128, 1024)
            wintv = wint
            GtT = pv(12288, 32, 1024, F32)
            dtT = pv(14336, 32, 1024, F32)
            gTt = pv(16384, 32, 1024, F32)
            eGtT = pv(18432, 32, 1024)
            zb = pv(19456, 32, 512, F32)
            absz = pv(20480, 32, 512, F32)
            easb = pv(21504, 32, 512, F32)
            rel = pv(22528, 32, 512, F32)
            ek_sb = pv(23552, 32, 128, F32)

            def proj8(wblk, dst, dstname, is_k):
                for m in range(KVH):
                    par = m % 2
                    stripe = wstA if par == 0 else wstB
                    wb = "wstA" if par == 0 else "wstB"
                    dma("g", stripe[:, :], wblk[m, :, :], writes=[wb], sem="wsA" if par == 0 else "wsB")
                    pjn = [PB[par * 2], PB[par * 2 + 1]]
                    fns = []
                    for k in range(NK):
                        for nh in range(2):
                            fns.append(lambda en, k=k, nh=nh, par=par,
                                       stripe=stripe: en.matmul(
                                pb[par * 2 + nh][:, :],
                                stripe[:, k * 128:(k + 1) * 128],
                                xn_src[:, k * T + nh * 512:k * T + (nh + 1) * 512],
                                start=(k == 0), stop=(k == NK - 1)))
                    p.group("t", fns, reads=[wb, xnname], writes=pjn)
                    p.emit("s", lambda en, par=par: en.copy(
                        sc1[:, 0:512], pb[par * 2][:, :]),
                        reads=[pjn[0]], writes=["prs1"])
                    p.emit("s", lambda en, par=par: en.copy(
                        sc1[:, 512:1024], pb[par * 2 + 1][:, :]),
                        reads=[pjn[1]], writes=["prs1"])
                    if is_k:
                        fns = [(lambda en, nh=nh: en.matmul(
                            pb[4 + nh][:, :], S_Tc[:, :],
                            sc1[:, nh * 512:(nh + 1) * 512],
                            start=True, stop=True)) for nh in range(2)]
                        p.group("t", fns, reads=["prs1", "S_T"],
                                writes=["pb4", "pb5"])
                        for nh in range(2):
                            p.emit("v", lambda en, nh=nh: en.tensor_tensor(
                                sc2[:, nh * 512:(nh + 1) * 512], pb[4 + nh][:, :],
                                sinT[:, nh * 512:(nh + 1) * 512], ALU.mult),
                                reads=[PB[4 + nh], sname], writes=["prs2"])
                            p.emit("v", lambda en, nh=nh, par=par: en.tensor_tensor(
                                sc3[:, nh * 512:(nh + 1) * 512],
                                pb[par * 2 + nh][:, :],
                                cosT[:, nh * 512:(nh + 1) * 512], ALU.mult),
                                reads=[pjn[nh], cname], writes=["prs3"])
                        p.emit("v", lambda en: en.tensor_tensor(
                            sc3[:, :], sc3[:, :], sc2[:, :], ALU.add),
                            reads=["prs3", "prs2"], writes=["prs3"])
                    else:
                        pbf4 = pb[4][:, :].bitcast(BF16)
                        pbf5 = pb[5][:, :].bitcast(BF16)
                        fns = []
                        for j in range(8):
                            tgt = pbf4 if j < 4 else pbf5
                            fns.append(lambda en, j=j, tgt=tgt: en.transpose(
                                tgt[:, (j % 4) * 128:(j % 4 + 1) * 128],
                                sc1[:, j * 128:(j + 1) * 128], I_bf[:, :]))
                        p.group("t", fns, reads=["prs1", "I_bf"],
                                writes=["pb4", "pb5"])
                        p.emit("v", lambda en, pbf4=pbf4: en.tensor_copy(
                            sc3[:, 0:512], pbf4[:, 0:512]),
                            reads=["pb4"], writes=["prs3"])
                        p.emit("v", lambda en, pbf5=pbf5: en.tensor_copy(
                            sc3[:, 512:1024], pbf5[:, 0:512]),
                            reads=["pb5"], writes=["prs3"])
                    def sp(en, m=m, dst=dst):
                        return en.dma_start(out=bass.AP(
                            dst, m * 128,
                            [[KVH * 128, 128], [128 * KVH * 128, NCH], [1, 128]]),
                            in_=sc3[:, :].rearrange("p (c t) -> p c t", c=NCH))
                    p.emit("q", sp, reads=["prs3"], writes=[dstname], sem=dstname, inc=16)

            proj8(dWk, kdst, kdstn, True)
            proj8(dWv, vdst, vdstn, False)

            fns = []
            for k in range(NK):
                for nh in range(2):
                    fns.append(lambda en, k=k, nh=nh: en.matmul(
                        pb[nh][0:32, :], wint[:, k * 32:(k + 1) * 32],
                        xn_src[:, k * T + nh * 512:k * T + (nh + 1) * 512],
                        start=(k == 0), stop=(k == NK - 1)))
            p.group("t", fns, reads=["wint", xnname], writes=["pb0", "pb1"])
            for nh in range(2):
                sl = slice(nh * 512, (nh + 1) * 512)
                p.emit("v", lambda en, nh=nh: en.tensor_scalar(
                    zb, pb[nh][0:32, :], dtbias[:, :], None, ALU.add),
                    reads=[PB[nh], "dtbias"], writes=[f"{tag}zb"])
                p.emit("v", lambda en: en.tensor_scalar(
                    rel, zb, -1.0, None, ALU.mult),
                    reads=[f"{tag}zb"], writes=[f"{tag}rel"])
                p.emit("v", lambda en: en.tensor_tensor(
                    absz, zb, rel, ALU.max),
                    reads=[f"{tag}zb", f"{tag}rel"], writes=[f"{tag}ab"])
                p.emit("s", lambda en: en.activation(easb, absz, AF.Exp, scale=-1.0),
                       reads=[f"{tag}ab"], writes=[f"{tag}ea"])
                p.emit("s", lambda en: en.activation(easb, easb, AF.Ln, bias=c_one[0:32, :]),
                       reads=[f"{tag}ea", "c_one"], writes=[f"{tag}ea"])
                p.emit("v", lambda en: en.tensor_relu(rel, zb),
                       reads=[f"{tag}zb"], writes=[f"{tag}rel"])
                p.emit("v", lambda en, sl=sl: en.tensor_tensor(
                    dtT[:, sl], rel, easb, ALU.add),
                    reads=[f"{tag}rel", f"{tag}ea"], writes=["dtT"])
            p.emit("v", lambda en: en.tensor_scalar(
                gTt[:, :], dtT[:, :], expA[:, :], -1.0, ALU.mult, ALU.mult),
                reads=["dtT", "expA"], writes=["gTt"])
            for c in range(NCH):
                p.emit("v", lambda en, c=c: en.tensor_tensor_scan(
                    GtT[:, c * 128:(c + 1) * 128], gTt[:, c * 128:(c + 1) * 128],
                    zeros32[:, :], 0.0, ALU.add, ALU.add),
                    reads=["gTt", "zeros32"], writes=["GtT"])
            p.emit("s", lambda en: en.activation(eGtT[:, :], GtT[:, :], AF.Exp),
                   reads=["GtT"], writes=["eGtT"])
            for c in range(NCH):
                sl = slice(c * 128, (c + 1) * 128)
                cs = slice(c * 32, (c + 1) * 32)
                p.emit("t", lambda en, sl=sl: en.transpose(
                    pb[6][:, 0:32], GtT[:, sl], I_f[0:32, 0:32]),
                    reads=["GtT", "I_f"], writes=["pb6"])
                p.emit("s", lambda en, cs=cs: en.activation(
                    negG[:, cs], pb[6][:, 0:32], AF.Copy, scale=-1.0),
                    reads=["pb6"], writes=["negG"])
                p.emit("s", lambda en, sl=sl, c=c: en.activation(
                    ek_sb, GtT[:, sl], AF.Exp, scale=-1.0,
                    bias=GtT[:, c * 128 + 127:c * 128 + 128]),
                    reads=["GtT"], writes=[f"{tag}ek"])
                p.emit("t", lambda en: en.transpose(
                    pb[7][:, 0:32], ek_sb, I_f[0:32, 0:32]),
                    reads=[f"{tag}ek", "I_f"], writes=["pb7"])
                p.emit("s", lambda en, cs=cs: en.copy(ekd[:, cs], pb[7][:, 0:32]),
                       reads=["pb7"], writes=["ekd"])
                p.emit("t", lambda en, sl=sl: en.transpose(
                    pb[6][:, 0:32], dtT[:, sl], I_f[0:32, 0:32]),
                    reads=["dtT", "I_f"], writes=["pb6"])
                p.emit("s", lambda en, cs=cs: en.copy(dtsbX[:, cs], pb[6][:, 0:32]),
                       reads=["pb6"], writes=[dtsbn])
            p.emit("v", lambda en: en.tensor_tensor(
                dteX[:, :], dtsbX[:, :], ekd[:, :], ALU.mult),
                reads=[dtsbn, "ekd"], writes=[dten])
            dma("q", sGd.ap(), GtT[:, :], reads=["GtT"], writes=["Gd"])
            dma("q", sEGb.ap(), eGtT[:, :], reads=["eGtT"], writes=["EGd"])
            dma("q", sdec.ap(),
                bass.AP(pool, eGtT.offset + 127, [[POOLN, 32], [128, 8]]),
                reads=["eGtT"], writes=["decd"], slow=True)
            dma("g", decBX[:, :],
                bass.AP(sdec, 0, [[0, 128], [8, 32], [1, 8]]),
                reads=["decd"], writes=[decBn], sem=decBn)

        def phase_attn(with_y, tag, kspill, vspill, kspn, vspn,
                       scb, csb, ybanks, ktb, dtsbX, dtsbn, dteX, dten,
                       decBX, decBn):
            ktr = [pv(11520, 128, 128), pv(11648, 128, 128)]
            xc4 = [pv(11776, 128, 512), pv(12288, 128, 512)]
            xc24 = [pv(12800, 128, 512), pv(13312, 128, 512)]
            stbf = pv(13824, 128, 512)
            qhat = [pv(14336, 128, 512), pv(14848, 128, 512)]
            Gb = [pv(15360, 128, 512, F32), pv(16384, 128, 512, F32)]
            E_sb = [pv(17408, 128, 512), pv(17920, 128, 512)]
            dct = [pv(18432, 128, 512), pv(18944, 128, 512)]
            STs = [pv(19456, 128, 512), pv(19968, 128, 512)]
            ysq = pv(20480, 128, 512)
            z_sb = pv(20480, 128, 512)
            ysb = [pv(20992, 128, 512), pv(21504, 128, 512)]
            r1 = pv(22016, 1, 512, F32)
            rsr = pv(23040, 1, 512)

            def load_chunk(c):
                par = c % 2
                dma("q", kcb[par][:, :].rearrange("p (m t) -> p m t", m=KVH),
                    bass.AP(kspill, c * 128 * KVH * 128,
                            [[KVH * 128, 128], [128, KVH], [1, 128]]),
                    reads=[kspn], writes=[f"kc{par}"], sem=f"kc{par}")
                dma("q", vcb[par][:, :].rearrange("p (m t) -> p m t", m=KVH),
                    bass.AP(vspill, c * 128 * KVH * 128,
                            [[KVH * 128, 128], [128, KVH], [1, 128]]),
                    reads=[vspn], writes=[f"vc{par}"], sem=f"vc{par}")
                if with_y:
                    dma("q", qcb[par][:, :].rearrange("p (m t) -> p m t", m=H),
                        bass.AP(sqsp, c * 128 * H * 128,
                                [[H * 128, 128], [128, H], [1, 128]]),
                        reads=["qspd"], writes=[f"qc{par}"], sem=f"qc{par}")
                    dma("q", sgcb[par][:, :].rearrange("p (m t) -> p m t", m=H),
                        bass.AP(ssgsp, c * 128 * H * 128,
                                [[H * 128, 128], [128, H], [1, 128]]),
                        reads=["sgspd"], writes=[f"sgc{par}"], sem=f"sgc{par}")

            def load_ge(c, g):
                if not with_y:
                    return
                gp = g % 2
                h0 = 4 * g
                def gb_dma(en, c=c, h0=h0, gp=gp):
                    return en.dma_start(
                        out=Gb[gp][:, :].rearrange("p (j t) -> p j t", j=4),
                        in_=bass.AP(sGd, h0 * T + c * 128,
                                    [[0, 128], [T, 4], [1, 128]]))
                p.emit("g", gb_dma, reads=["Gd"], writes=[f"Gb{gp}"],
                       sem=f"Gb{gp}", inc=16)
                def e_dma(en, c=c, h0=h0, gp=gp):
                    return en.dma_start(
                        out=E_sb[gp][:, :].rearrange("p (j t) -> p j t", j=4),
                        in_=bass.AP(sEGb, h0 * T + c * 128,
                                    [[0, 128], [T, 4], [1, 128]]))
                p.emit("g", e_dma, reads=["EGd"], writes=[f"E{gp}"],
                       sem=f"E{gp}", inc=16)

            load_chunk(0)
            load_ge(0, 0)
            for c in range(NCH):
                par = c % 2
                if c + 1 < NCH:
                    load_chunk(c + 1)
                k_c, v_c = kcb[par], vcb[par]
                q_c, sg_c = qcb[par], sgcb[par]
                kcn, vcn = f"kc{par}", f"vc{par}"
                qcn, sgcn = f"qc{par}", f"sgc{par}"
                for g in range(KVH):
                    if g + 1 < KVH:
                        load_ge(c, g + 1)
                    elif c + 1 < NCH:
                        load_ge(c + 1, 0)
                    gp = g % 2
                    h0 = 4 * g
                    sb = pb[scb[gp]]
                    cbk = pb[csb[gp]]
                    yb = pb[ybanks[gp]]
                    sbn = PB[scb[gp]]
                    cbn = PB[csb[gp]]
                    ybn = PB[ybanks[gp]]
                    ktpb = pb[ktb]
                    ktpbn = PB[ktb]
                    p.emit("t", lambda en, g=g, k_c=k_c, gp=gp, ktpb=ktpb:
                           en.transpose(
                        ktpb[:, :].bitcast(BF16)[:, gp * 128:gp * 128 + 128],
                        k_c[:, g * 128:(g + 1) * 128], I_bf[:, :]),
                        reads=[kcn, "I_bf"], writes=[ktpbn])
                    p.emit("s", lambda en, gp=gp, ktpb=ktpb: en.copy(
                        ktr[gp][:, :],
                        ktpb[:, :].bitcast(BF16)[:, gp * 128:gp * 128 + 128]),
                        reads=[ktpbn], writes=[f"ktr{gp}"])
                    vblk = bass.AP(vcb[par], g * 128,
                                   [[KVH * 128, 128], [0, 4], [1, 128]])
                    dtc = bass.AP(dtsbX, c * 32 + h0, [[256, 128], [1, 4], [0, 128]])
                    dtec = bass.AP(dteX, c * 32 + h0, [[256, 128], [1, 4], [0, 128]])
                    if with_y:
                        p.emit("v", lambda en, vblk=vblk, dtc=dtc, gp=gp:
                               en.tensor_tensor(
                            xc4[gp][:, :].rearrange("p (j t) -> p j t", j=4),
                            vblk, dtc, ALU.mult),
                            reads=[vcn, dtsbn], writes=[f"xc4{gp}"])
                    p.emit("v", lambda en, vblk=vblk, dtec=dtec, gp=gp:
                           en.tensor_tensor(
                        xc24[gp][:, :].rearrange("p (j t) -> p j t", j=4),
                        vblk, dtec, ALU.mult),
                        reads=[vcn, dten], writes=[f"xc24{gp}"])
                    if with_y:
                        fns = [(lambda en, hh=hh, g=g, sb=sb, h0=h0, k_c=k_c,
                                q_c=q_c: en.matmul(
                            sb[:, hh * 128:(hh + 1) * 128],
                            k_c[:, g * 128:(g + 1) * 128],
                            q_c[:, (h0 + hh) * 128:(h0 + hh + 1) * 128],
                            start=True, stop=True)) for hh in range(4)]
                        p.group("t", fns, reads=[kcn, qcn], writes=[sbn])
                        p.emit("v", lambda en, gp=gp: en.tensor_tensor(
                            Gb[gp][:, :], Gb[gp][:, :], maskc[:, :], ALU.add),
                            reads=[f"Gb{gp}", "maskc"], writes=[f"Gb{gp}"])
                        for hh in range(4):
                            p.emit("s", lambda en, hh=hh, c=c, h0=h0, gp=gp:
                                   en.activation(
                                dct[gp][:, hh * 128:(hh + 1) * 128],
                                Gb[gp][:, hh * 128:(hh + 1) * 128], AF.Exp,
                                bias=negG[:, c * 32 + h0 + hh:c * 32 + h0 + hh + 1]),
                                reads=[f"Gb{gp}", "negG"], writes=[f"dct{gp}"])
                        p.emit("v", lambda en, sb=sb, gp=gp: en.tensor_tensor(
                            STs[gp][:, :], sb[:, :], dct[gp][:, :], ALU.mult),
                            reads=[sbn, f"dct{gp}"], writes=[f"ST{gp}"])
                        p.emit("v", lambda en, h0=h0, gp=gp, q_c=q_c:
                               en.tensor_tensor(
                            qhat[gp][:, :], q_c[:, h0 * 128:(h0 + 4) * 128],
                            E_sb[gp][:, :], ALU.mult),
                            reads=[qcn, f"E{gp}"], writes=[f"qh{gp}"])
                    fns = [(lambda en, hh=hh, cbk=cbk, gp=gp: en.matmul(
                        cbk[:, hh * 128:(hh + 1) * 128], ktr[gp][:, :],
                        xc24[gp][:, hh * 128:(hh + 1) * 128],
                        start=True, stop=True)) for hh in range(4)]
                    p.group("t", fns, reads=[f"ktr{gp}", f"xc24{gp}"],
                            writes=[cbn])
                    if with_y:
                        p.emit("v", lambda en, h0=h0: en.tensor_copy(
                            stbf[:, :], states[:, h0 * 128:(h0 + 4) * 128]),
                            reads=["states"], writes=["stbf"])
                        for hh in range(4):
                            h = h0 + hh
                            fns = [
                                (lambda en, hh=hh, yb=yb, gp=gp: en.matmul(
                                    yb[:, hh * 128:(hh + 1) * 128],
                                    xc4[gp][:, hh * 128:(hh + 1) * 128],
                                    STs[gp][:, hh * 128:(hh + 1) * 128],
                                    start=True, stop=False)),
                                (lambda en, hh=hh, yb=yb, gp=gp: en.matmul(
                                    yb[:, hh * 128:(hh + 1) * 128],
                                    stbf[:, hh * 128:(hh + 1) * 128],
                                    qhat[gp][:, hh * 128:(hh + 1) * 128],
                                    start=False, stop=True)),
                            ]
                            p.group("t", fns,
                                    reads=[f"xc4{gp}", f"ST{gp}",
                                           "stbf", f"qh{gp}"],
                                    writes=[ybn])
                            p.emit("v", lambda en, h=h, c=c, cbk=cbk, hh=hh:
                                   en.scalar_tensor_tensor(
                                states[:, h * 128:(h + 1) * 128],
                                states[:, h * 128:(h + 1) * 128],
                                decBX[:, h * 8 + c:h * 8 + c + 1],
                                cbk[:, hh * 128:(hh + 1) * 128],
                                ALU.mult, ALU.add),
                                reads=[decBn, cbn, "stbf"],
                                writes=["states"])
                        p.emit("s", lambda en, yb=yb: en.activation(
                            ysq[:, :], yb[:, :], AF.Square),
                            reads=[ybn], writes=["ysq"])
                        p.emit("s", lambda en, yb=yb, gp=gp: en.activation(
                            ysb[gp][:, :], yb[:, :], AF.Copy, scale=gnormc[:, :]),
                            reads=[ybn, "gnormc"], writes=[f"ysb{gp}"])
                        p.emit("t", lambda en, sb=sb: en.matmul(
                            sb[0:1, :], onesc[:, :], ysq[:, :],
                            start=True, stop=True),
                            reads=["ysq", "onesc"], writes=[sbn])
                        p.emit("s", lambda en, sb=sb: en.activation(
                            r1, sb[0:1, :], AF.Ln, scale=1.0 / 128.0,
                            bias=c_eps[0:1, :]),
                            reads=[sbn, "c_eps"], writes=["r1"])
                        p.emit("s", lambda en: en.activation(
                            rsr, r1, AF.Exp, scale=-0.5),
                            reads=["r1"], writes=["rsr"])
                        p.emit("t", lambda en, sb=sb: en.matmul(
                            sb[:, :], onesr[0:1, :], rsr[0:1, :],
                            start=True, stop=True),
                            reads=["rsr", "onesr"], writes=[sbn])
                        p.emit("v", lambda en, sb=sb, gp=gp: en.tensor_tensor(
                            z_sb[:, :], ysb[gp][:, :], sb[:, :], ALU.mult),
                            reads=[f"ysb{gp}", sbn, "ysq"], writes=["ysq"])
                        def yfin(en, c=c, h0=h0, sg_c=sg_c):
                            dstap = bass.AP(bigA, h0 * T + c * 128,
                                            [[NK * T, 128], [T, 4], [1, 128]])
                            return en.tensor_tensor(
                                dstap,
                                z_sb[:, :].rearrange("p (j t) -> p j t", j=4),
                                sg_c[:, h0 * 128:(h0 + 4) * 128].rearrange(
                                    "p (j t) -> p j t", j=4), ALU.mult)
                        p.emit("v", yfin, reads=["ysq", sgcn],
                               writes=["yf"])
                    yield

        def phase_qgate(xn_src, xnname):
            wstA = pv(0, 128, 4096)
            wstB = pv(4096, 128, 4096)
            sc1 = pv(8192, 128, 1024)
            sc2 = pv(9216, 128, 1024)
            sc3 = pv(10240, 128, 1024)
            for m in range(H):
                par = m % 2
                stripe = wstA if par == 0 else wstB
                wb = "wstA" if par == 0 else "wstB"
                dma("g", stripe[:, :], dWq[m, :, :], writes=[wb], sem="wsA" if par == 0 else "wsB")
                pjn = [PB[par * 2], PB[par * 2 + 1]]
                fns = []
                for k in range(NK):
                    for nh in range(2):
                        fns.append(lambda en, k=k, nh=nh, stripe=stripe, par=par:
                                   en.matmul(
                            pb[par * 2 + nh][:, :],
                            stripe[:, k * 128:(k + 1) * 128],
                            xn_src[:, k * T + nh * 512:k * T + (nh + 1) * 512],
                            start=(k == 0), stop=(k == NK - 1)))
                p.group("t", fns, reads=[wb, xnname], writes=pjn)
                p.emit("s", lambda en, par=par: en.copy(
                    sc1[:, 0:512], pb[par * 2][:, :]), reads=[pjn[0]],
                    writes=["prs1"])
                p.emit("s", lambda en, par=par: en.copy(
                    sc1[:, 512:1024], pb[par * 2 + 1][:, :]), reads=[pjn[1]],
                    writes=["prs1"])
                fns = [(lambda en, nh=nh: en.matmul(
                    pb[4 + nh][:, :], S_Tc[:, :],
                    sc1[:, nh * 512:(nh + 1) * 512], start=True, stop=True))
                    for nh in range(2)]
                p.group("t", fns, reads=["prs1", "S_T"], writes=["pb4", "pb5"])
                for nh in range(2):
                    p.emit("v", lambda en, nh=nh: en.tensor_tensor(
                        sc2[:, nh * 512:(nh + 1) * 512], pb[4 + nh][:, :],
                        sinO[:, nh * 512:(nh + 1) * 512], ALU.mult),
                        reads=[PB[4 + nh], "sinO"], writes=["prs2"])
                    p.emit("v", lambda en, nh=nh, par=par: en.tensor_tensor(
                        sc3[:, nh * 512:(nh + 1) * 512], pb[par * 2 + nh][:, :],
                        cosO[:, nh * 512:(nh + 1) * 512], ALU.mult),
                        reads=[pjn[nh], "cosO"], writes=["prs3"])
                p.emit("v", lambda en: en.tensor_tensor(
                    sc3[:, :], sc3[:, :], sc2[:, :], ALU.add),
                    reads=["prs3", "prs2"], writes=["prs3"])
                def spq(en, m=m):
                    return en.dma_start(out=bass.AP(
                        sqsp, m * 128,
                        [[H * 128, 128], [128 * H * 128, NCH], [1, 128]]),
                        in_=sc3[:, :].rearrange("p (c t) -> p c t", c=NCH))
                p.emit("q", spq, reads=["prs3"], writes=["qspd"], sem="qspd", inc=16)
                yield
            for m in range(H):
                par = m % 2
                stripe = wstA if par == 0 else wstB
                wb = "wstA" if par == 0 else "wstB"
                dma("g", stripe[:, :], dWg[m, :, :], writes=[wb], sem="wsA" if par == 0 else "wsB")
                pjn = [PB[par * 2], PB[par * 2 + 1]]
                fns = []
                for k in range(NK):
                    for nh in range(2):
                        fns.append(lambda en, k=k, nh=nh, stripe=stripe, par=par:
                                   en.matmul(
                            pb[par * 2 + nh][:, :],
                            stripe[:, k * 128:(k + 1) * 128],
                            xn_src[:, k * T + nh * 512:k * T + (nh + 1) * 512],
                            start=(k == 0), stop=(k == NK - 1)))
                p.group("t", fns, reads=[wb, xnname], writes=pjn)
                p.emit("s", lambda en, par=par: en.activation(
                    sc3[:, 0:512], pb[par * 2][:, :], AF.Silu),
                    reads=[pjn[0]], writes=["prs3"])
                p.emit("s", lambda en, par=par: en.activation(
                    sc3[:, 512:1024], pb[par * 2 + 1][:, :], AF.Silu),
                    reads=[pjn[1]], writes=["prs3"])
                def spg(en, m=m):
                    return en.dma_start(out=bass.AP(
                        ssgsp, m * 128,
                        [[H * 128, 128], [128 * H * 128, NCH], [1, 128]]),
                        in_=sc3[:, :].rearrange("p (c t) -> p c t", c=NCH))
                p.emit("q", spg, reads=["prs3"], writes=["sgspd"], sem="sgspd", inc=16)
                yield

        def phase_ffnpart(xn_src, xnname):
            stf = [pv(0, 128, 4096), pv(4096, 128, 4096)]
            xTm = pv(8192, 128, 512, F32)
            t2 = pv(9216, 128, 512, F32)
            outst = pv(10240, 128, 512, F32)
            for m in range(NK):
                par = m % 2
                fb_ = "wstA" if par == 0 else "wstB"
                dma("g", stf[par][:, :], dWf[m, :, :], writes=[fb_],
                    sem="wsA" if par == 0 else "wsB")
                for nh in range(2):
                    dma("q", xTm[:, :],
                        dxT[m * 128:(m + 1) * 128, nh * 512:(nh + 1) * 512],
                        writes=["o_xT"], sem="o_xT")
                    fns = []
                    for k in range(NK):
                        fns.append(lambda en, k=k, nh=nh, par=par: en.matmul(
                            pb[7][:, :],
                            stf[par][:, k * 128:(k + 1) * 128],
                            xn_src[:, k * T + nh * 512:k * T + (nh + 1) * 512],
                            start=(k == 0), stop=(k == NK - 1)))
                    p.group("t", fns, reads=[fb_, xnname], writes=["pb7f"])
                    p.emit("v", lambda en, m=m, nh=nh: en.scalar_tensor_tensor(
                        t2, pb[7][:, :], rowselc[:, m:m + 1],
                        qselB[:, nh * 512:(nh + 1) * 512],
                        ALU.mult, ALU.mult),
                        reads=["pb7f", "rowselc", "qselB"], writes=["o_t2"])
                    p.emit("v", lambda en, m=m, nh=nh: en.scalar_tensor_tensor(
                        t2, qselB[:, nh * 512:(nh + 1) * 512],
                        biasrs[:, m:m + 1], t2, ALU.mult, ALU.add),
                        reads=["qselB", "biasrs", "o_t2"], writes=["o_t2"])
                    p.emit("v", lambda en: en.tensor_tensor(
                        outst, t2, xTm[:, :], ALU.add),
                        reads=["o_t2", "o_xT"], writes=["o_out"])
                    dma("q", spart.ap()[m * 128:(m + 1) * 128,
                                        nh * 512:(nh + 1) * 512],
                        outst, reads=["o_out"], writes=["partd"], sem="partd")
                    yield

        def phase_wo():
            stw = [pv(0, 128, 4096), pv(4096, 128, 4096)]
            prt = pv(16384, 128, 1024, F32)
            t4 = pv(20480, 128, 512, F32)
            outst = pv(21504, 128, 512, F32)
            for m in range(NK):
                par = m % 2
                wb_ = "wstA" if par == 0 else "wstB"
                dma("g", stw[par][:, :], dWo[m, :, :], writes=[wb_], sem="wsA" if par == 0 else "wsB")
                dma("q", prt[:, :], spart.ap()[m * 128:(m + 1) * 128, :],
                    reads=["partd"], writes=["o_prt"])
                pjw = [PB[par * 2], PB[par * 2 + 1]]
                fns = []
                for k in range(NK):
                    for nh in range(2):
                        fns.append(lambda en, k=k, nh=nh, par=par: en.matmul(
                            pb[par * 2 + nh][:, :],
                            stw[par][:, k * 128:(k + 1) * 128],
                            bigA[:, k * T + nh * 512:k * T + (nh + 1) * 512],
                            start=(k == 0), stop=(k == NK - 1)))
                p.group("t", fns, reads=[wb_, "yf"], writes=pjw)
                for nh in range(2):
                    p.emit("v", lambda en, nh=nh, par=par: en.scalar_tensor_tensor(
                        t4, pb[par * 2 + nh][:, :], 1.0,
                        prt[:, nh * 512:(nh + 1) * 512], ALU.mult, ALU.add),
                        reads=[pjw[nh], "o_prt"], writes=["o_t4"])
                    p.emit("v", lambda en: en.tensor_copy(outst, t4),
                           reads=["o_t4"], writes=["o_out"])
                    dma("q", dout[m * 128:(m + 1) * 128, nh * 512:(nh + 1) * 512],
                        outst, reads=["o_out"], writes=["outd"])

        def phase_debug():
            if not DEBUG:
                return
            for n, t_ in [("dmax", dmax), ("selc", selc), ("rowselc", rowselc),
                          ("qselB", qselB), ("biasrs", biasrs), ("qrowf", qrowf),
                          ("states", states), ("dtsb", dtsb), ("decB", decB),
                          ("negG", negG), ("ekd", ekd)]:
                if t_.dtype if False else True:
                    pass
                def mk(en, n=n, t_=t_):
                    return en.dma_start(out=ddbg[n], in_=t_[:, :])
                p.emit("g", mk, reads=[n if n in p.bufs else "outd"],
                       extra_deps=[("v", p.cnt["v"]), ("s", p.cnt["s"]),
                                   ("t", p.cnt["t"])], sem="dbgd", inc=16)

        def run2(ga, gb):
            ga, gb = iter(ga), iter(gb)
            da = db = False
            while not (da and db):
                if not da:
                    try:
                        next(ga)
                    except StopIteration:
                        da = True
                if not db:
                    try:
                        next(gb)
                    except StopIteration:
                        db = True

        def drain(g):
            for _ in g:
                pass

        phase_ln(dxw, bigA, "xnA", "lw")
        p.fence()
        phase_kvdt(bigA[:, :], "xnA", cosW, "cosW", sinW, "sinW", "w",
                   sksp_w, "kspw", svsp_w, "vspw",
                   dtsbW, "dtsbW", dteW, "dteW", decBW, "decBW")
        p.fence()
        phase_ln(dx, bigA, "xnA", "lo")
        p.fence()
        phase_kvdt(bigA[:, :], "xnA", cosO, "cosO", sinO, "sinO", "o",
                   sksp, "ksp", svsp, "vsp",
                   dtsb, "dtsb", dte, "dte", decB, "decB")
        p.fence()
        drain(phase_attn(False, "wa", sksp_w, svsp_w, "kspw", "vspw",
                         [0, 1], [0, 1], [0, 1], 2,
                         dtsbW, "dtsbW", dteW, "dteW", decBW, "decBW"))
        p.fence()
        drain(phase_qgate(bigA[:, :], "xnA"))
        p.fence()
        drain(phase_p1())
        p.fence()
        drain(phase_ffnpart(bigA[:, :], "xnA"))
        p.fence()
        drain(phase_attn(True, "oa", sksp, svsp, "ksp", "vsp",
                         [0, 1], [2, 3], [4, 5], 6,
                         dtsb, "dtsb", dte, "dte", decB, "decB"))
        p.fence()
        phase_wo()
        phase_debug()
        with nc.Block() as block:
            p.replay(nc, es, block)

    return nc


_CACHE = {}


def _blk(w, n_m):
    return np.ascontiguousarray(
        w.reshape(n_m, 128, NK, 128).transpose(0, 3, 2, 1).reshape(n_m, 128, HID))


def _rope_tables(pos):
    inv = 1.0 / ROPE_THETA ** (np.arange(0, D, 2, dtype=np.float64) / D)
    ang = np.concatenate([inv, inv])[:, None] * pos[None, :].astype(np.float64)
    return (np.ascontiguousarray(np.cos(ang).astype(np.float32)),
            np.ascontiguousarray(np.sin(ang).astype(np.float32)))


def kernel(L0_attn_hidden_states, L0_attn_Wq, L0_attn_Wk, L0_attn_Wv,
           L0_attn_Wo, L0_attn_Wg, L0_attn_Win, L0_attn_dt_bias,
           L0_attn_A_log_bias, L0_attn_gnorm_w, L0_ffn_weight, L0_ffn_bias,
           L0_ffn_centroids, L0_ffn_assignments):
    x = np.ascontiguousarray(np.asarray(L0_attn_hidden_states, dtype=np.float32))
    Wq = np.asarray(L0_attn_Wq, np.float32)
    Wk = np.asarray(L0_attn_Wk, np.float32)
    Wv = np.asarray(L0_attn_Wv, np.float32)
    Wo = np.asarray(L0_attn_Wo, np.float32)
    Wg = np.asarray(L0_attn_Wg, np.float32)
    Win = np.asarray(L0_attn_Win, np.float32)
    dtb = np.asarray(L0_attn_dt_bias, np.float32)
    Alog = np.asarray(L0_attn_A_log_bias, np.float32)
    gnw = np.asarray(L0_attn_gnorm_w, np.float32)
    Wf = np.asarray(L0_ffn_weight, np.float32)
    fb = np.asarray(L0_ffn_bias, np.float32)
    cent = np.asarray(L0_ffn_centroids, np.float32)
    assign = np.asarray(L0_ffn_assignments, np.int32)

    if "nc" not in _CACHE:
        _CACHE["nc"] = build_program()
    nc = _CACHE["nc"]

    xTall = np.ascontiguousarray(x.reshape(ALLT, HID).T)
    Wq_b = _blk(Wq, 32)
    Wk_b = _blk(Wk, 8)
    Wv_b = _blk(Wv, 8)
    Wg_b = _blk(Wg, 32)
    Wo_b = _blk(Wo, 32)
    Wf_b = _blk(Wf, 32)
    Win_t = np.ascontiguousarray(Win.T)
    CWO = np.ascontiguousarray(
        np.concatenate([cent.T, np.ones((HID, 1), np.float32)], axis=1))
    fb_c = np.ascontiguousarray(fb.reshape(32, 128).T)
    ident = np.eye(128, dtype=np.float32)
    S = np.zeros((128, 128), np.float32)
    S[np.arange(64), np.arange(64) + 64] = -1.0
    S[np.arange(64) + 64, np.arange(64)] = 1.0
    S_T = np.ascontiguousarray(S.T)
    mask1 = np.where(np.arange(128)[None, :] >= np.arange(128)[:, None],
                     0.0, NEG_BIG).astype(np.float32)
    mask4 = np.ascontiguousarray(np.tile(mask1, (1, 4)))
    iota = np.arange(64, dtype=np.float32).reshape(64, 1)

    in_maps = []
    for c in range(NCORES):
        b, s = divmod(c, 4)
        t0 = s * T
        cos_o, sin_o = _rope_tables(np.arange(t0, t0 + T))
        w0 = max(s - 1, 0) * T
        cos_w, sin_w = _rope_tables(np.arange(w0, w0 + T))
        xw = (np.zeros((T, HID), np.float32) if s == 0
              else np.ascontiguousarray(x[b, t0 - T:t0, :]))
        in_maps.append({
            "x_own": np.ascontiguousarray(x[b, t0:t0 + T, :]),
            "x_win": xw,
            "xT_own": np.ascontiguousarray(x[b, t0:t0 + T, :].T),
            "xT_all": xTall,
            "Wq_blk": Wq_b, "Wk_blk": Wk_b, "Wv_blk": Wv_b,
            "Wg_blk": Wg_b, "Wo_blk": Wo_b, "Wf_blk": Wf_b,
            "Win_t": Win_t, "CWO": CWO,
            "ffn_b": fb_c,
            "dt_b": dtb.reshape(32, 1),
            "A_log": Alog.reshape(32, 1),
            "gnorm": gnw.reshape(128, 1),
            "assign": assign.reshape(1, HID),
            "ident": ident, "S_T": S_T, "maskneg4": mask4,
            "cos_own": cos_o, "sin_own": sin_o,
            "cos_win": cos_w, "sin_win": sin_w,
            "iota64_in": iota,
            "own_off": np.array([[b * 32 + s * 8]], np.int32),
        })

    res = run_bass_kernel_spmd(nc, in_maps, core_ids=list(range(NCORES)))
    out = np.empty((B, L, HID), np.float32)
    for c in range(NCORES):
        b, s = divmod(c, 4)
        out[b, s * T:(s + 1) * T, :] = res.results[c]["out"].T
    return out


# version 07
# speedup vs baseline: 1.3294x; 1.1666x over previous
import numpy as np
import concourse.bass as bass
import concourse.mybir as mybir
from concourse.bass_utils import run_bass_kernel_spmd

F32 = mybir.dt.float32
BF16 = mybir.dt.bfloat16
I32 = mybir.dt.int32
AF = mybir.ActivationFunctionType
ALU = mybir.AluOpType
AX = mybir.AxisListType

B, L, HID = 2, 4096, 4096
NCORES = 8
T = 1024
NT = 8
NCH = 8
NK = 32
H, KVH, D = 32, 8, 128
ROPE_THETA = 10000.0
RMS_EPS = 1e-5
LN_EPS = 1e-5
FFN_T = 0.1
FFN_TH = 0.01
ALLT = B * L
NEG_BIG = -1.0e30
DEBUG = False


class Prog:
    ENGINES = ("t", "v", "s", "g", "q")
    WAIT_ALL = ("CONSTg", "CONSTq")

    def __init__(self):
        self.ops = {e: [] for e in self.ENGINES}
        self.cnt = {e: 0 for e in self.ENGINES}
        self.bufs = {}

    def _bump(self, key, inc):
        self.cnt[key] = self.cnt.get(key, 0) + inc

    def _deps(self, reads, writes, extra):
        deps = []
        for b in reads:
            st = self.bufs.get(b)
            if st is not None and st[0] is not None:
                deps.append(st[0])
        for b in writes:
            st = self.bufs.get(b)
            if st is not None:
                if st[0] is not None:
                    deps.append(st[0])
                deps.extend(st[1])
        for d in extra:
            if d is not None:
                deps.append(d)
        best = {}
        for (e, v) in deps:
            if best.get(e, -1) < v:
                best[e] = v
        return list(best.items())

    def emit(self, eng, fn, reads=(), writes=(), extra_deps=(), inc=None,
             sem=None):
        key = sem if sem is not None else eng
        if inc is None:
            inc = 16 if key not in self.ENGINES else (
                0 if eng in ("g", "q") else 1)
            if eng not in ("g", "q") and key in self.ENGINES:
                inc = 1
        deps = self._deps(reads, writes, extra_deps)
        self._bump(key, inc)
        tok = (key, self.cnt[key])
        self.ops[eng].append((fn, deps, inc, key))
        for b in writes:
            self.bufs[b] = [tok, []]
        for b in reads:
            self.bufs.setdefault(b, [None, []])[1].append(tok)
        return tok

    def group(self, eng, fns, reads=(), writes=(), extra_deps=()):
        deps = self._deps(reads, writes, extra_deps)
        inc = 1
        for i, fn in enumerate(fns):
            self.ops[eng].append((fn, deps if i == 0 else [],
                                  inc if i == len(fns) - 1 else 0, eng))
        self._bump(eng, inc)
        tok = (eng, self.cnt[eng])
        for b in writes:
            self.bufs[b] = [tok, []]
        for b in reads:
            self.bufs.setdefault(b, [None, []])[1].append(tok)
        return tok

    def fence(self):
        snap = dict(self.cnt)
        for e in self.ENGINES:
            deps = [(k, v) for k, v in snap.items() if k != e and v > 0]
            self.emit(e, lambda en: en.nop(), extra_deps=deps, inc=0)

    def replay(self, nc, es, block):
        sems = {}
        for k in self.cnt:
            sems[k] = es.enter_context(nc.semaphore(f"sm_{k}"))
        totals = dict(self.cnt)

        def body(eng_name):
            def run(en):
                waited = {}
                for (fn, deps, inc, key) in self.ops[eng_name]:
                    for (dk, dv) in deps:
                        if dk == eng_name and eng_name == "t":
                            continue
                        if dk in self.WAIT_ALL:
                            dv = totals[dk]
                        if waited.get(dk, 0) >= dv:
                            continue
                        en.wait_ge(sems[dk], dv)
                        waited[dk] = dv
                    inst = fn(en)
                    if inc > 0:
                        inst.then_inc(sems[key], inc)
                if eng_name == "q":
                    for k, tv in totals.items():
                        if k not in self.ENGINES and tv > 0 and \
                                waited.get(k, 0) < tv:
                            en.wait_ge(sems[k], tv)
                elif totals.get(eng_name, 0) > 0:
                    en.wait_ge(sems[eng_name], totals[eng_name])
            return run

        block.tensor(body("t"))
        block.vector(body("v"))
        block.scalar(body("s"))
        block.gpsimd(body("g"))
        block.sync(body("q"))


PB = [f"pb{i}" for i in range(8)]


def build_program():
    nc = bass.Bass("TRN2", target_bir_lowering=False)

    dx = nc.dram_tensor("x_own", [T, HID], F32, kind="ExternalInput").ap()
    dxw = nc.dram_tensor("x_win", [T, HID], F32, kind="ExternalInput").ap()
    dxT = nc.dram_tensor("xT_own", [HID, T], F32, kind="ExternalInput").ap()
    dxTall = nc.dram_tensor("xT_all", [HID, ALLT], F32, kind="ExternalInput").ap()
    dWq = nc.dram_tensor("Wq_blk", [32, 128, HID], F32, kind="ExternalInput").ap()
    dWk = nc.dram_tensor("Wk_blk", [8, 128, HID], F32, kind="ExternalInput").ap()
    dWv = nc.dram_tensor("Wv_blk", [8, 128, HID], F32, kind="ExternalInput").ap()
    dWg = nc.dram_tensor("Wg_blk", [32, 128, HID], F32, kind="ExternalInput").ap()
    dWo = nc.dram_tensor("Wo_blk", [32, 128, HID], F32, kind="ExternalInput").ap()
    dWf = nc.dram_tensor("Wf_blk", [32, 128, HID], F32, kind="ExternalInput").ap()
    dWin = nc.dram_tensor("Win_t", [HID, 32], F32, kind="ExternalInput").ap()
    dCWO = nc.dram_tensor("CWO", [HID, 65], F32, kind="ExternalInput").ap()
    dfb = nc.dram_tensor("ffn_b", [128, 32], F32, kind="ExternalInput").ap()
    ddtb = nc.dram_tensor("dt_b", [32, 1], F32, kind="ExternalInput").ap()
    dAlog = nc.dram_tensor("A_log", [32, 1], F32, kind="ExternalInput").ap()
    dgn = nc.dram_tensor("gnorm", [128, 1], F32, kind="ExternalInput").ap()
    dassign = nc.dram_tensor("assign", [1, HID], I32, kind="ExternalInput").ap()
    dI = nc.dram_tensor("ident", [128, 128], F32, kind="ExternalInput").ap()
    dS_T = nc.dram_tensor("S_T", [128, 128], F32, kind="ExternalInput").ap()
    dmask = nc.dram_tensor("maskneg4", [128, 512], F32, kind="ExternalInput").ap()
    dcos_o = nc.dram_tensor("cos_own", [128, T], F32, kind="ExternalInput").ap()
    dsin_o = nc.dram_tensor("sin_own", [128, T], F32, kind="ExternalInput").ap()
    dcos_w = nc.dram_tensor("cos_win", [128, T], F32, kind="ExternalInput").ap()
    dsin_w = nc.dram_tensor("sin_win", [128, T], F32, kind="ExternalInput").ap()
    diota = nc.dram_tensor("iota64_in", [64, 1], F32, kind="ExternalInput").ap()
    doff = nc.dram_tensor("own_off", [1, 1], I32, kind="ExternalInput").ap()

    dout = nc.dram_tensor("out", [HID, T], F32, kind="ExternalOutput").ap()
    if DEBUG:
        ddbg = {n: nc.dram_tensor(f"dbg_{n}", sh, F32, kind="ExternalOutput").ap()
                for n, sh in [("dmax", [64, 1]), ("selc", [64, 1]),
                              ("rowselc", [128, 32]), ("qselB", [128, T]),
                              ("biasrs", [128, 32]), ("qrowf", [1, T]),
                              ("states", [128, H * 128]),
                              ("dtsb", [128, 256]), ("decB", [128, 256]),
                              ("negG", [128, 256]), ("ekd", [128, 256])]}

    sGd = nc.dram_tensor("Gdram", [32, T], F32)
    sEGb = nc.dram_tensor("EGbf", [32, T], BF16)
    sdec = nc.dram_tensor("decdram", [32, 8], BF16)
    sqsp = nc.dram_tensor("qspill", [NCH, 128, H, 128], BF16)
    ssgsp = nc.dram_tensor("sgspill", [NCH, 128, H, 128], BF16)
    sksp = nc.dram_tensor("kspill", [NCH, 128, KVH, 128], BF16)
    svsp = nc.dram_tensor("vspill", [NCH, 128, KVH, 128], BF16)
    sksp_w = nc.dram_tensor("kspill_w", [NCH, 128, KVH, 128], BF16)
    svsp_w = nc.dram_tensor("vspill_w", [NCH, 128, KVH, 128], BF16)
    spart = nc.dram_tensor("spartial", [HID, T], F32)
    sqsel = nc.dram_tensor("qseld", [128, 64], BF16)
    srowd = nc.dram_tensor("rowseld", [1, HID], F32)

    p = Prog()

    from contextlib import ExitStack
    es = ExitStack()
    SBt = lambda name, shape, dt: es.enter_context(nc.sbuf_tensor(name, shape, dt))
    PSt = lambda name, shape, dt: es.enter_context(nc.psum_tensor(name, shape, dt))

    with es:
        bigA = SBt("bigA", [128, NK * T], BF16)
        states = SBt("states", [128, H * 128], F32)
        dtsb = SBt("dtsb", [128, 256], F32)
        ekd = SBt("ekd", [128, 256], F32)
        dte = SBt("dte", [128, 256], F32)
        decB = SBt("decB", [128, 256], F32)
        dtsbW = SBt("dtsbW", [128, 256], F32)
        dteW = SBt("dteW", [128, 256], F32)
        decBW = SBt("decBW", [128, 256], F32)
        negG = SBt("negG", [128, 256], F32)
        wint = SBt("wint", [128, NK * 32], BF16)
        cwot = SBt("cwot", [128, NK * 65], BF16)
        I_bf = SBt("I_bf", [128, 128], BF16)
        I_f = SBt("I_f", [128, 128], F32)
        S_Tc = SBt("S_Tc", [128, 128], BF16)
        maskc = SBt("maskc", [128, 512], F32)
        cosO = SBt("cosO", [128, T], BF16)
        sinO = SBt("sinO", [128, T], BF16)
        cosW = SBt("cosW", [128, T], BF16)
        sinW = SBt("sinW", [128, T], BF16)
        onesr = SBt("onesr", [1, 128], BF16)
        onesc = SBt("onesc", [128, 1], BF16)
        c_eps = SBt("c_eps", [128, 1], F32)
        c_one = SBt("c_one", [128, 1], F32)
        iota64 = SBt("iota64", [64, 1], F32)
        alog = SBt("alog", [32, 1], F32)
        expA = SBt("expA", [32, 1], F32)
        dtbias = SBt("dtbias", [32, 1], F32)
        gnormc = SBt("gnormc", [128, 1], F32)
        ffnb = SBt("ffnb", [128, 32], F32)
        zeros32 = SBt("zeros32", [32, 128], F32)
        qselB = SBt("qselB", [128, T], BF16)
        rowselc = SBt("rowselc", [128, 32], F32)
        biasrs = SBt("biasrs", [128, 32], F32)
        dmax = SBt("dmax", [64, 1], F32)
        tmax = SBt("tmax", [64, 1], F32)
        selc = SBt("selc", [64, 1], F32)
        selbf = SBt("selbf", [64, 1], BF16)
        qrowf = SBt("qrowf", [1, T], BF16)
        Scol = SBt("Scol", [128, 1], F32)
        Scbf = SBt("Scbf", [1, 65], BF16)
        ScB = SBt("ScB", [128, 64], F32)
        offsb = SBt("offsb", [1, 1], I32)
        POOLN = 23808
        pool = SBt("pool", [128, POOLN], BF16)
        qcb = [SBt(f"qcb{i}", [128, H * 128], BF16) for i in range(2)]
        sgcb = [SBt(f"sgcb{i}", [128, H * 128], BF16) for i in range(2)]
        kcb = [SBt(f"kcb{i}", [128, KVH * 128], BF16) for i in range(2)]
        vcb = [SBt(f"vcb{i}", [128, KVH * 128], BF16) for i in range(2)]
        pb = [PSt(f"psb{i}", [128, 512], F32) for i in range(8)]

        roff = es.enter_context(nc.sync.register("roff"))

        def pv(off, part, free, dt=BF16):
            if dt == BF16:
                return pool[0:part, off:off + free]
            return pool[:, off:off + 2 * free].bitcast(dt)[0:part, 0:free]

        def dma(qe, out_ap, in_ap, reads=(), writes=(), deps=(), slow=False,
                sem=None):
            def fn(en, out_ap=out_ap, in_ap=in_ap, slow=slow):
                return en.dma_start(out=out_ap, in_=in_ap,
                                    allow_slow_non_contiguous=slow)
            if sem is None:
                sem = writes[0] if writes else "misc_dma"
            return p.emit(qe, fn, reads=reads, writes=writes, extra_deps=deps,
                          sem=sem, inc=16)
        
        dma("g", I_bf[:, :], dI, writes=["I_bf"], sem="CONSTg")
        dma("q", I_f[:, :], dI, writes=["I_f"], sem="CONSTq")
        dma("g", S_Tc[:, :], dS_T, writes=["S_T"], sem="CONSTg")
        dma("q", maskc[:, :], dmask, writes=["maskc"], sem="CONSTq")
        dma("g", cosO[:, :], dcos_o, writes=["cosO"], sem="CONSTg")
        dma("g", sinO[:, :], dsin_o, writes=["sinO"], sem="CONSTg")
        dma("g", cosW[:, :], dcos_w, writes=["cosW"], sem="CONSTg")
        dma("g", sinW[:, :], dsin_w, writes=["sinW"], sem="CONSTg")
        dma("q", iota64[:, :], diota, writes=["iota64"], sem="CONSTq")
        dma("q", alog[:, :], dAlog, writes=["alog"], sem="CONSTq")
        dma("q", dtbias[:, :], ddtb, writes=["dtbias"], sem="CONSTq")
        dma("q", gnormc[:, :], dgn, writes=["gnormc"], sem="CONSTq")
        dma("q", ffnb[:, :], dfb, writes=["ffnb"], sem="CONSTq")
        dma("q", offsb[:, :], doff, writes=["offsb"], sem="CONSTq")
        dma("g", wint[:, :].rearrange("p (k h) -> p k h", k=NK),
            dWin.rearrange("(k p) h -> p k h", p=128), writes=["wint"], sem="CONSTg")
        dma("g", cwot[:, :].rearrange("p (k c) -> p k c", k=NK),
            dCWO.rearrange("(k p) c -> p k c", p=128), writes=["cwot"], sem="CONSTg")
        p.emit("v", lambda en: en.memset(onesr[:, :], 1.0), writes=["onesr"])
        p.emit("v", lambda en: en.memset(c_eps[:, :], LN_EPS), writes=["c_eps"])
        p.emit("v", lambda en: en.memset(c_one[:, :], 1.0), writes=["c_one"])
        p.emit("v", lambda en: en.memset(onesc[:, :], 1.0), writes=["onesc"])
        p.emit("v", lambda en: en.memset(zeros32[:, :], 0.0), writes=["zeros32"])
        p.emit("v", lambda en: en.memset(states[:, :], 0.0), writes=["states"])
        p.emit("v", lambda en: en.memset(dmax[:, :], NEG_BIG), writes=["dmax"])
        p.emit("s", lambda en: en.activation(expA[:, :], alog[:, :], AF.Exp),
               reads=["alog"], writes=["expA"])
        p.emit("q", lambda en: en.reg_load(roff, offsb[0:1, 0:1]),
               reads=["offsb"], writes=["roff"], inc=0)

        def phase_p1():
            xt = [pv(0, 128, 2048), pv(2048, 128, 2048)]
            xq = [pv(4096, 128, 2048), pv(6144, 128, 2048)]
            dots = pv(8192, 65, 512, F32)
            dotsB = pv(9216, 65, 512, F32)
            a_sb = pv(10240, 128, 64, F32)
            e_sb = pv(10368, 128, 64, F32)
            col = lambda i: pv(10496 + i * 2, 128, 1, F32)
            mcol, m2c, varc, rstdc, zmaxc, nzmaxc, Zc, rZc, qmx = (
                col(i) for i in range(9))
            dots_bf = pv(10560, 128, 64)
            qse = pv(10624, 128, 1, F32)
            Mblk = pv(10688, 64, 512)
            ai_blk = pv(11200, 1, 512, I32)
            ab_blk = pv(12224, 1, 512)
            rowblk = pv(12736, 1, 512, F32)

            fns = [(lambda en, k=k: en.matmul(
                pb[7][0:65, 0:1], cwot[:, k * 65:(k + 1) * 65], onesc[:, :],
                start=(k == 0), stop=(k == NK - 1))) for k in range(NK)]
            p.group("t", fns, reads=["cwot", "onesc"], writes=["pb7"])
            p.emit("s", lambda en: en.copy(Scol[0:65, 0:1], pb[7][0:65, 0:1]),
                   reads=["pb7"], writes=["Scol"])
            p.emit("t", lambda en: en.transpose(
                pb[6][0:1, 0:65], Scol[0:65, 0:1], I_f[0:65, 0:65]),
                reads=["Scol", "I_f"], writes=["pb6"])
            p.emit("s", lambda en: en.copy(Scbf[0:1, 0:65], pb[6][0:1, 0:65]),
                   reads=["pb6"], writes=["Scbf"])
            p.emit("t", lambda en: en.matmul(
                pb[6][:, 0:64], onesr[0:1, :], Scbf[0:1, 0:64],
                start=True, stop=True),
                reads=["onesr", "Scbf"], writes=["pb6"])
            p.emit("s", lambda en: en.copy(ScB[:, :], pb[6][:, 0:64]),
                   reads=["pb6"], writes=["ScB"])
            yield

            for tg in range(4):
                for k in range(NK):
                    par = k % 2
                    bx, bq = f"p1x{par}", f"p1q{par}"
                    dma("g", xt[par][:, :],
                        dxTall[k * 128:(k + 1) * 128, tg * 2048:(tg + 1) * 2048],
                        writes=[bx])
                    p.emit("s", lambda en, par=par: en.activation(
                        xq[par][:, :], xt[par][:, :], AF.Square),
                        reads=[bx], writes=[bq])
                    fns = []
                    for n in range(4):
                        fns.append(lambda en, k=k, n=n, par=par: en.matmul(
                            pb[n][0:65, :], cwot[:, k * 65:(k + 1) * 65],
                            xt[par][:, n * 512:(n + 1) * 512],
                            start=(k == 0), stop=(k == NK - 1)))
                    for n in range(4):
                        fns.append(lambda en, k=k, n=n, par=par: en.matmul(
                            pb[4 + n][0:65, :], cwot[:, k * 65:(k + 1) * 65],
                            xq[par][:, n * 512:(n + 1) * 512],
                            start=(k == 0), stop=(k == NK - 1)))
                    p.group("t", fns, reads=[bx, bq, "cwot"], writes=list(PB))
                    if k % 8 == 7:
                        yield
                for n in range(4):
                    p.emit("s", lambda en, n=n: en.copy(dots[:, :], pb[n][0:65, :]),
                           reads=[PB[n]], writes=["p1d"])
                    p.emit("s", lambda en, n=n: en.copy(dotsB[:, :],
                                                        pb[4 + n][0:65, :]),
                           reads=[PB[4 + n]], writes=["p1dB"])
                    for j in range(4):
                        tile = tg * 16 + n * 4 + j
                        p.emit("t", lambda en, j=j, n=n: en.transpose(
                            pb[n][:, 0:65], dots[:, j * 128:(j + 1) * 128],
                            I_f[0:65, 0:65]),
                            reads=["p1d", "I_f"], writes=[PB[n]])
                        p.emit("t", lambda en, j=j, n=n: en.transpose(
                            pb[4 + n][:, 0:65], dotsB[:, j * 128:(j + 1) * 128],
                            I_f[0:65, 0:65]),
                            reads=["p1dB", "I_f"], writes=[PB[4 + n]])
                        p.emit("v", lambda en, n=n: en.tensor_scalar(
                            mcol, pb[n][:, 64:65], 1.0 / HID, None, ALU.mult),
                            reads=[PB[n]], writes=["p1m"])
                        p.emit("v", lambda en: en.tensor_tensor(
                            m2c, mcol, mcol, ALU.mult),
                            reads=["p1m"], writes=["p1m2"])
                        p.emit("v", lambda en, n=n: en.scalar_tensor_tensor(
                            varc, pb[4 + n][:, 64:65], 1.0 / HID, m2c,
                            ALU.mult, ALU.subtract),
                            reads=[PB[4 + n], "p1m2"], writes=["p1var"])
                        p.emit("s", lambda en: en.activation(
                            rstdc, varc, AF.Ln, bias=c_eps[:, :]),
                            reads=["p1var", "c_eps"], writes=["p1rstd"])
                        p.emit("s", lambda en: en.activation(
                            rstdc, rstdc, AF.Exp, scale=-0.5),
                            reads=["p1rstd"], writes=["p1rstd"])
                        p.emit("v", lambda en: en.tensor_scalar(
                            m2c, mcol, -1.0, None, ALU.mult),
                            reads=["p1m"], writes=["p1m2"])
                        p.emit("v", lambda en, n=n: en.scalar_tensor_tensor(
                            a_sb, ScB[:, :], m2c, pb[n][:, 0:64],
                            ALU.mult, ALU.add),
                            reads=["ScB", "p1m2", PB[n]], writes=["p1a"])
                        p.emit("v", lambda en: en.tensor_scalar(
                            a_sb, a_sb, rstdc, 1.0 / FFN_T, ALU.mult, ALU.mult),
                            reads=["p1a", "p1rstd"], writes=["p1a"])
                        p.emit("v", lambda en: en.reduce_max(zmaxc, a_sb, axis=AX.X),
                               reads=["p1a"], writes=["p1zm"])
                        p.emit("v", lambda en: en.tensor_scalar(
                            nzmaxc, zmaxc, -1.0, None, ALU.mult),
                            reads=["p1zm"], writes=["p1nzm"])
                        p.emit("s", lambda en: en.activation(
                            e_sb, a_sb, AF.Exp, bias=nzmaxc),
                            reads=["p1a", "p1nzm"], writes=["p1e"])
                        p.emit("v", lambda en: en.reduce_sum(Zc, e_sb, axis=AX.X),
                               reads=["p1e"], writes=["p1Z"])
                        p.emit("v", lambda en: en.reciprocal(rZc, Zc),
                               reads=["p1Z"], writes=["p1rZ"])
                        p.emit("v", lambda en: en.tensor_scalar(
                            dots_bf, e_sb, rZc, None, ALU.mult),
                            reads=["p1e", "p1rZ"], writes=["p1dbf"])
                        p.emit("v", lambda en: en.reduce_max(qmx, dots_bf, axis=AX.X),
                               reads=["p1dbf"], writes=["p1qmx"])
                        p.emit("v", lambda en: en.tensor_scalar(
                            qse, qmx, FFN_TH, None, ALU.is_gt),
                            reads=["p1qmx"], writes=["p1qse"])
                        dma("g", bass.AP(sqsel, tile, [[64, 128], [1, 1]]), qse,
                            reads=["p1qse"], writes=["qseld"], slow=True)
                        p.emit("t", lambda en, n=n: en.transpose(
                            pb[4 + n][:, :].bitcast(BF16)[0:64, 0:128],
                            dots_bf, I_bf[:, :]),
                            reads=["p1dbf", "I_bf"], writes=[PB[4 + n]])
                        p.emit("v", lambda en, n=n: en.reduce_max(
                            tmax[:, :], pb[4 + n][:, :].bitcast(BF16)[0:64, 0:128],
                            axis=AX.X),
                            reads=[PB[4 + n]], writes=["tmax"])
                        p.emit("v", lambda en: en.tensor_tensor(
                            dmax[:, :], dmax[:, :], tmax[:, :], ALU.max),
                            reads=["tmax", "dmax"], writes=["dmax"])
                        yield

            p.emit("v", lambda en: en.tensor_scalar(
                selc[:, :], dmax[:, :], FFN_TH, None, ALU.is_gt),
                reads=["dmax"], writes=["selc"])
            p.emit("v", lambda en: en.tensor_copy(selbf[:, :], selc[:, :]),
                   reads=["selc"], writes=["selbf"])
            for och in range(8):
                dma("q", ai_blk, dassign[0:1, och * 512:(och + 1) * 512],
                    writes=["p1ai"])
                p.emit("v", lambda en: en.tensor_copy(ab_blk, ai_blk),
                       reads=["p1ai"], writes=["p1ab"])
                p.emit("t", lambda en: en.matmul(
                    pb[6][0:64, :], onesr[0:1, 0:64], ab_blk,
                    start=True, stop=True),
                    reads=["onesr", "p1ab"], writes=["pb6"])
                p.emit("v", lambda en: en.tensor_scalar(
                    Mblk, pb[6][0:64, :], iota64[:, :], None, ALU.is_equal),
                    reads=["pb6", "iota64"], writes=["p1M"])
                p.emit("t", lambda en: en.matmul(
                    pb[7][0:1, :], selbf[:, :], Mblk, start=True, stop=True),
                    reads=["selbf", "p1M"], writes=["pb7"])
                p.emit("v", lambda en: en.tensor_scalar(
                    rowblk, pb[7][0:1, :], 0.5, None, ALU.is_gt),
                    reads=["pb7"], writes=["p1row"])
                dma("q", srowd.ap()[0:1, och * 512:(och + 1) * 512], rowblk,
                    reads=["p1row"], writes=["rowd"])
                yield
            dma("g", rowselc[:, :],
                bass.AP(srowd, 0, [[1, 128], [128, 32]]),
                reads=["rowd"], writes=["rowselc"], slow=True)
            p.emit("v", lambda en: en.tensor_tensor(
                biasrs[:, :], ffnb[:, :], rowselc[:, :], ALU.mult),
                reads=["ffnb", "rowselc"], writes=["biasrs"])

            def rd_qrow(en):
                return en.dma_start(
                    out=qrowf[0:1, :].rearrange("o (tt pp) -> o tt pp", tt=8),
                    in_=bass.AP(sqsel, roff, [[0, 1], [1, 8], [64, 128]]),
                    allow_slow_non_contiguous=True)
            p.emit("q", rd_qrow, reads=["qseld", "roff"], writes=["qrowf"],
                   sem="qrowf", inc=16)
            p.emit("t", lambda en: en.matmul(
                pb[6][:, :], onesr[0:1, :], qrowf[0:1, 0:512],
                start=True, stop=True),
                reads=["onesr", "qrowf"], writes=["pb6"])
            p.emit("t", lambda en: en.matmul(
                pb[7][:, :], onesr[0:1, :], qrowf[0:1, 512:1024],
                start=True, stop=True),
                reads=["onesr", "qrowf"], writes=["pb7"])
            p.emit("s", lambda en: en.copy(qselB[:, 0:512], pb[6][:, :]),
                   reads=["pb6"], writes=["qselB"])
            p.emit("s", lambda en: en.copy(qselB[:, 512:1024], pb[7][:, :]),
                   reads=["pb7"], writes=["qselB"])
            yield

        def phase_ln(xdram, dst_h, dstname, tag):
            xt = pv(0, 128, 4096)
            xnt = pv(4096, 128, 4096)
            xsq = pv(8192, 128, 4096)
            col = lambda i: pv(12544 + i * 2, 128, 1, F32)
            scol, mcol, sscol, varc, rstdc, nmr = (col(i) for i in range(6))
            for tt in range(NT):
                dma("g", xt[:, :], xdram[tt * 128:(tt + 1) * 128, :],
                    writes=[f"{tag}x"])
                p.emit("v", lambda en: en.reduce_sum(scol, xt[:, :], axis=AX.X),
                       reads=[f"{tag}x"], writes=[f"{tag}s"])
                p.emit("v", lambda en: en.tensor_scalar(
                    mcol, scol, 1.0 / HID, None, ALU.mult),
                    reads=[f"{tag}s"], writes=[f"{tag}m"])
                p.emit("s", lambda en: en.activation(xsq[:, :], xt[:, :], AF.Square),
                       reads=[f"{tag}x"], writes=[f"{tag}xsq"])
                p.emit("v", lambda en: en.reduce_sum(sscol, xsq[:, :], axis=AX.X),
                       reads=[f"{tag}xsq"], writes=[f"{tag}ss"])
                p.emit("v", lambda en: en.tensor_tensor(varc, mcol, mcol, ALU.mult),
                       reads=[f"{tag}m"], writes=[f"{tag}var"])
                p.emit("v", lambda en: en.scalar_tensor_tensor(
                    varc, sscol, 1.0 / HID, varc, ALU.mult, ALU.subtract),
                    reads=[f"{tag}ss", f"{tag}var"], writes=[f"{tag}var"])
                p.emit("s", lambda en: en.activation(rstdc, varc, AF.Ln, bias=c_eps[:, :]),
                       reads=[f"{tag}var", "c_eps"], writes=[f"{tag}r"])
                p.emit("s", lambda en: en.activation(rstdc, rstdc, AF.Exp, scale=-0.5),
                       reads=[f"{tag}r"], writes=[f"{tag}r"])
                p.emit("v", lambda en: en.scalar_tensor_tensor(
                    nmr, mcol, -1.0, rstdc, ALU.mult, ALU.mult),
                    reads=[f"{tag}m", f"{tag}r"], writes=[f"{tag}nmr"])
                p.emit("v", lambda en: en.tensor_scalar(
                    xnt[:, :], xt[:, :], rstdc, nmr, ALU.mult, ALU.add),
                    reads=[f"{tag}x", f"{tag}r", f"{tag}nmr"], writes=[f"{tag}xn"])
                for kq in range(0, NK, 8):
                    par = (kq // 8) % 2
                    bank = pb[6 + par]
                    pbf = bank[:, :].bitcast(BF16)
                    fns = []
                    for j in range(8):
                        fns.append(lambda en, k=kq + j, j=j, pbf=pbf: en.transpose(
                            pbf[:, j * 128:(j + 1) * 128],
                            xnt[:, k * 128:(k + 1) * 128], I_bf[:, :]))
                    p.group("t", fns, reads=[f"{tag}xn", "I_bf"],
                            writes=[PB[6 + par]])
                    def evac(en, kq=kq, tt=tt, pbf=pbf):
                        dstap = bass.AP(dst_h, kq * T + tt * 128,
                                        [[NK * T, 128], [T, 8], [1, 128]])
                        return en.tensor_copy(
                            dstap, pbf[:, :].rearrange("p (j t) -> p j t", j=8))
                    p.emit("v", evac, reads=[PB[6 + par]], writes=[dstname])

        def phase_kvdt(xn_src, xnname, cosT, cname, sinT, sname, tag,
                       kdst, kdstn, vdst, vdstn, dtsbX, dtsbn, dteX, dten,
                       decBX, decBn):
            wstA = pv(0, 128, 4096)
            wstB = pv(4096, 128, 4096)
            sc1 = pv(8192, 128, 1024)
            sc2 = pv(9216, 128, 1024)
            sc3 = pv(10240, 128, 1024)
            wintv = wint
            GtT = pv(12288, 32, 1024, F32)
            dtT = pv(14336, 32, 1024, F32)
            gTt = pv(16384, 32, 1024, F32)
            eGtT = pv(18432, 32, 1024)
            zb = pv(19456, 32, 512, F32)
            absz = pv(20480, 32, 512, F32)
            easb = pv(21504, 32, 512, F32)
            rel = pv(22528, 32, 512, F32)
            ek_sb = pv(23552, 32, 128, F32)

            def proj8(wblk, dst, dstname, is_k):
                for m in range(KVH):
                    par = m % 2
                    stripe = wstA if par == 0 else wstB
                    wb = "wstA" if par == 0 else "wstB"
                    dma("g", stripe[:, :], wblk[m, :, :], writes=[wb], sem="wsA" if par == 0 else "wsB")
                    pjn = [PB[par * 2], PB[par * 2 + 1]]
                    fns = []
                    for k in range(NK):
                        for nh in range(2):
                            fns.append(lambda en, k=k, nh=nh, par=par,
                                       stripe=stripe: en.matmul(
                                pb[par * 2 + nh][:, :],
                                stripe[:, k * 128:(k + 1) * 128],
                                xn_src[:, k * T + nh * 512:k * T + (nh + 1) * 512],
                                start=(k == 0), stop=(k == NK - 1)))
                    p.group("t", fns, reads=[wb, xnname], writes=pjn)
                    p.emit("s", lambda en, par=par: en.copy(
                        sc1[:, 0:512], pb[par * 2][:, :]),
                        reads=[pjn[0]], writes=["prs1"])
                    p.emit("s", lambda en, par=par: en.copy(
                        sc1[:, 512:1024], pb[par * 2 + 1][:, :]),
                        reads=[pjn[1]], writes=["prs1"])
                    if is_k:
                        fns = [(lambda en, nh=nh: en.matmul(
                            pb[4 + nh][:, :], S_Tc[:, :],
                            sc1[:, nh * 512:(nh + 1) * 512],
                            start=True, stop=True)) for nh in range(2)]
                        p.group("t", fns, reads=["prs1", "S_T"],
                                writes=["pb4", "pb5"])
                        for nh in range(2):
                            p.emit("v", lambda en, nh=nh: en.tensor_tensor(
                                sc2[:, nh * 512:(nh + 1) * 512], pb[4 + nh][:, :],
                                sinT[:, nh * 512:(nh + 1) * 512], ALU.mult),
                                reads=[PB[4 + nh], sname], writes=["prs2"])
                            p.emit("v", lambda en, nh=nh, par=par: en.tensor_tensor(
                                sc3[:, nh * 512:(nh + 1) * 512],
                                pb[par * 2 + nh][:, :],
                                cosT[:, nh * 512:(nh + 1) * 512], ALU.mult),
                                reads=[pjn[nh], cname], writes=["prs3"])
                        p.emit("v", lambda en: en.tensor_tensor(
                            sc3[:, :], sc3[:, :], sc2[:, :], ALU.add),
                            reads=["prs3", "prs2"], writes=["prs3"])
                    else:
                        pbf4 = pb[4][:, :].bitcast(BF16)
                        pbf5 = pb[5][:, :].bitcast(BF16)
                        fns = []
                        for j in range(8):
                            tgt = pbf4 if j < 4 else pbf5
                            fns.append(lambda en, j=j, tgt=tgt: en.transpose(
                                tgt[:, (j % 4) * 128:(j % 4 + 1) * 128],
                                sc1[:, j * 128:(j + 1) * 128], I_bf[:, :]))
                        p.group("t", fns, reads=["prs1", "I_bf"],
                                writes=["pb4", "pb5"])
                        p.emit("v", lambda en, pbf4=pbf4: en.tensor_copy(
                            sc3[:, 0:512], pbf4[:, 0:512]),
                            reads=["pb4"], writes=["prs3"])
                        p.emit("v", lambda en, pbf5=pbf5: en.tensor_copy(
                            sc3[:, 512:1024], pbf5[:, 0:512]),
                            reads=["pb5"], writes=["prs3"])
                    def sp(en, m=m, dst=dst):
                        return en.dma_start(out=bass.AP(
                            dst, m * 128,
                            [[KVH * 128, 128], [128 * KVH * 128, NCH], [1, 128]]),
                            in_=sc3[:, :].rearrange("p (c t) -> p c t", c=NCH))
                    p.emit("q", sp, reads=["prs3"], writes=[dstname], sem=dstname, inc=16)

            proj8(dWk, kdst, kdstn, True)
            proj8(dWv, vdst, vdstn, False)

            fns = []
            for k in range(NK):
                for nh in range(2):
                    fns.append(lambda en, k=k, nh=nh: en.matmul(
                        pb[nh][0:32, :], wint[:, k * 32:(k + 1) * 32],
                        xn_src[:, k * T + nh * 512:k * T + (nh + 1) * 512],
                        start=(k == 0), stop=(k == NK - 1)))
            p.group("t", fns, reads=["wint", xnname], writes=["pb0", "pb1"])
            for nh in range(2):
                sl = slice(nh * 512, (nh + 1) * 512)
                p.emit("v", lambda en, nh=nh: en.tensor_scalar(
                    zb, pb[nh][0:32, :], dtbias[:, :], None, ALU.add),
                    reads=[PB[nh], "dtbias"], writes=[f"{tag}zb"])
                p.emit("v", lambda en: en.tensor_scalar(
                    rel, zb, -1.0, None, ALU.mult),
                    reads=[f"{tag}zb"], writes=[f"{tag}rel"])
                p.emit("v", lambda en: en.tensor_tensor(
                    absz, zb, rel, ALU.max),
                    reads=[f"{tag}zb", f"{tag}rel"], writes=[f"{tag}ab"])
                p.emit("s", lambda en: en.activation(easb, absz, AF.Exp, scale=-1.0),
                       reads=[f"{tag}ab"], writes=[f"{tag}ea"])
                p.emit("s", lambda en: en.activation(easb, easb, AF.Ln, bias=c_one[0:32, :]),
                       reads=[f"{tag}ea", "c_one"], writes=[f"{tag}ea"])
                p.emit("v", lambda en: en.tensor_relu(rel, zb),
                       reads=[f"{tag}zb"], writes=[f"{tag}rel"])
                p.emit("v", lambda en, sl=sl: en.tensor_tensor(
                    dtT[:, sl], rel, easb, ALU.add),
                    reads=[f"{tag}rel", f"{tag}ea"], writes=["dtT"])
            p.emit("v", lambda en: en.tensor_scalar(
                gTt[:, :], dtT[:, :], expA[:, :], -1.0, ALU.mult, ALU.mult),
                reads=["dtT", "expA"], writes=["gTt"])
            for c in range(NCH):
                p.emit("v", lambda en, c=c: en.tensor_tensor_scan(
                    GtT[:, c * 128:(c + 1) * 128], gTt[:, c * 128:(c + 1) * 128],
                    zeros32[:, :], 0.0, ALU.add, ALU.add),
                    reads=["gTt", "zeros32"], writes=["GtT"])
            p.emit("s", lambda en: en.activation(eGtT[:, :], GtT[:, :], AF.Exp),
                   reads=["GtT"], writes=["eGtT"])
            for c in range(NCH):
                sl = slice(c * 128, (c + 1) * 128)
                cs = slice(c * 32, (c + 1) * 32)
                p.emit("t", lambda en, sl=sl: en.transpose(
                    pb[6][:, 0:32], GtT[:, sl], I_f[0:32, 0:32]),
                    reads=["GtT", "I_f"], writes=["pb6"])
                p.emit("s", lambda en, cs=cs: en.activation(
                    negG[:, cs], pb[6][:, 0:32], AF.Copy, scale=-1.0),
                    reads=["pb6"], writes=["negG"])
                p.emit("s", lambda en, sl=sl, c=c: en.activation(
                    ek_sb, GtT[:, sl], AF.Exp, scale=-1.0,
                    bias=GtT[:, c * 128 + 127:c * 128 + 128]),
                    reads=["GtT"], writes=[f"{tag}ek"])
                p.emit("t", lambda en: en.transpose(
                    pb[7][:, 0:32], ek_sb, I_f[0:32, 0:32]),
                    reads=[f"{tag}ek", "I_f"], writes=["pb7"])
                p.emit("s", lambda en, cs=cs: en.copy(ekd[:, cs], pb[7][:, 0:32]),
                       reads=["pb7"], writes=["ekd"])
                p.emit("t", lambda en, sl=sl: en.transpose(
                    pb[6][:, 0:32], dtT[:, sl], I_f[0:32, 0:32]),
                    reads=["dtT", "I_f"], writes=["pb6"])
                p.emit("s", lambda en, cs=cs: en.copy(dtsbX[:, cs], pb[6][:, 0:32]),
                       reads=["pb6"], writes=[dtsbn])
            p.emit("v", lambda en: en.tensor_tensor(
                dteX[:, :], dtsbX[:, :], ekd[:, :], ALU.mult),
                reads=[dtsbn, "ekd"], writes=[dten])
            dma("q", sGd.ap(), GtT[:, :], reads=["GtT"], writes=["Gd"])
            dma("q", sEGb.ap(), eGtT[:, :], reads=["eGtT"], writes=["EGd"])
            dma("q", sdec.ap(),
                bass.AP(pool, eGtT.offset + 127, [[POOLN, 32], [128, 8]]),
                reads=["eGtT"], writes=["decd"], slow=True)
            dma("g", decBX[:, :],
                bass.AP(sdec, 0, [[0, 128], [8, 32], [1, 8]]),
                reads=["decd"], writes=[decBn], sem=decBn)

        def phase_attn(with_y, tag, kspill, vspill, kspn, vspn,
                       scb, csb, ybanks, ktb, dtsbX, dtsbn, dteX, dten,
                       decBX, decBn):
            ktr = [pv(11520, 128, 128), pv(11648, 128, 128)]
            xc4 = [pv(11776, 128, 512), pv(12288, 128, 512)]
            xc24 = [pv(12800, 128, 512), pv(13312, 128, 512)]
            stbf = pv(13824, 128, 512)
            qhat = [pv(14336, 128, 512), pv(14848, 128, 512)]
            Gb = [pv(15360, 128, 512, F32), pv(16384, 128, 512, F32)]
            E_sb = [pv(17408, 128, 512), pv(17920, 128, 512)]
            dct = [pv(18432, 128, 512), pv(18944, 128, 512)]
            STs = [pv(19456, 128, 512), pv(19968, 128, 512)]
            ysq = pv(20480, 128, 512)
            z_sb = pv(20480, 128, 512)
            ysb = [pv(20992, 128, 512), pv(21504, 128, 512)]
            r1 = pv(22016, 1, 512, F32)
            rsr = pv(23040, 1, 512)

            def load_chunk(c):
                par = c % 2
                dma("q", kcb[par][:, :].rearrange("p (m t) -> p m t", m=KVH),
                    bass.AP(kspill, c * 128 * KVH * 128,
                            [[KVH * 128, 128], [128, KVH], [1, 128]]),
                    reads=[kspn], writes=[f"kc{par}"], sem=f"kc{par}")
                dma("q", vcb[par][:, :].rearrange("p (m t) -> p m t", m=KVH),
                    bass.AP(vspill, c * 128 * KVH * 128,
                            [[KVH * 128, 128], [128, KVH], [1, 128]]),
                    reads=[vspn], writes=[f"vc{par}"], sem=f"vc{par}")
                if with_y:
                    dma("q", qcb[par][:, :].rearrange("p (m t) -> p m t", m=H),
                        bass.AP(sqsp, c * 128 * H * 128,
                                [[H * 128, 128], [128, H], [1, 128]]),
                        reads=["qspd"], writes=[f"qc{par}"], sem=f"qc{par}")
                    dma("q", sgcb[par][:, :].rearrange("p (m t) -> p m t", m=H),
                        bass.AP(ssgsp, c * 128 * H * 128,
                                [[H * 128, 128], [128, H], [1, 128]]),
                        reads=["sgspd"], writes=[f"sgc{par}"], sem=f"sgc{par}")

            def load_ge(c, g):
                if not with_y:
                    return
                gp = g % 2
                h0 = 4 * g
                def gb_dma(en, c=c, h0=h0, gp=gp):
                    return en.dma_start(
                        out=Gb[gp][:, :].rearrange("p (j t) -> p j t", j=4),
                        in_=bass.AP(sGd, h0 * T + c * 128,
                                    [[0, 128], [T, 4], [1, 128]]))
                p.emit("g", gb_dma, reads=["Gd"], writes=[f"Gb{gp}"],
                       sem=f"Gb{gp}", inc=16)
                def e_dma(en, c=c, h0=h0, gp=gp):
                    return en.dma_start(
                        out=E_sb[gp][:, :].rearrange("p (j t) -> p j t", j=4),
                        in_=bass.AP(sEGb, h0 * T + c * 128,
                                    [[0, 128], [T, 4], [1, 128]]))
                p.emit("g", e_dma, reads=["EGd"], writes=[f"E{gp}"],
                       sem=f"E{gp}", inc=16)

            load_chunk(0)
            load_ge(0, 0)
            for c in range(NCH):
                par = c % 2
                if c + 1 < NCH:
                    load_chunk(c + 1)
                k_c, v_c = kcb[par], vcb[par]
                q_c, sg_c = qcb[par], sgcb[par]
                kcn, vcn = f"kc{par}", f"vc{par}"
                qcn, sgcn = f"qc{par}", f"sgc{par}"
                for g in range(KVH):
                    if g + 1 < KVH:
                        load_ge(c, g + 1)
                    elif c + 1 < NCH:
                        load_ge(c + 1, 0)
                    gp = g % 2
                    h0 = 4 * g
                    sb = pb[scb[gp]]
                    cbk = pb[csb[gp]]
                    yb = pb[ybanks[gp]]
                    sbn = PB[scb[gp]]
                    cbn = PB[csb[gp]]
                    ybn = PB[ybanks[gp]]
                    ktpb = pb[ktb]
                    ktpbn = PB[ktb]
                    p.emit("t", lambda en, g=g, k_c=k_c, gp=gp, ktpb=ktpb:
                           en.transpose(
                        ktpb[:, :].bitcast(BF16)[:, gp * 128:gp * 128 + 128],
                        k_c[:, g * 128:(g + 1) * 128], I_bf[:, :]),
                        reads=[kcn, "I_bf"], writes=[ktpbn])
                    p.emit("s", lambda en, gp=gp, ktpb=ktpb: en.copy(
                        ktr[gp][:, :],
                        ktpb[:, :].bitcast(BF16)[:, gp * 128:gp * 128 + 128]),
                        reads=[ktpbn], writes=[f"ktr{gp}"])
                    vblk = bass.AP(vcb[par], g * 128,
                                   [[KVH * 128, 128], [0, 4], [1, 128]])
                    dtc = bass.AP(dtsbX, c * 32 + h0, [[256, 128], [1, 4], [0, 128]])
                    dtec = bass.AP(dteX, c * 32 + h0, [[256, 128], [1, 4], [0, 128]])
                    if with_y:
                        p.emit("v", lambda en, vblk=vblk, dtc=dtc, gp=gp:
                               en.tensor_tensor(
                            xc4[gp][:, :].rearrange("p (j t) -> p j t", j=4),
                            vblk, dtc, ALU.mult),
                            reads=[vcn, dtsbn], writes=[f"xc4{gp}"])
                    p.emit("v", lambda en, vblk=vblk, dtec=dtec, gp=gp:
                           en.tensor_tensor(
                        xc24[gp][:, :].rearrange("p (j t) -> p j t", j=4),
                        vblk, dtec, ALU.mult),
                        reads=[vcn, dten], writes=[f"xc24{gp}"])
                    if with_y:
                        fns = [(lambda en, hh=hh, g=g, sb=sb, h0=h0, k_c=k_c,
                                q_c=q_c: en.matmul(
                            sb[:, hh * 128:(hh + 1) * 128],
                            k_c[:, g * 128:(g + 1) * 128],
                            q_c[:, (h0 + hh) * 128:(h0 + hh + 1) * 128],
                            start=True, stop=True)) for hh in range(4)]
                        p.group("t", fns, reads=[kcn, qcn], writes=[sbn])
                        p.emit("v", lambda en, gp=gp: en.tensor_tensor(
                            Gb[gp][:, :], Gb[gp][:, :], maskc[:, :], ALU.add),
                            reads=[f"Gb{gp}", "maskc"], writes=[f"Gb{gp}"])
                        for hh in range(4):
                            p.emit("s", lambda en, hh=hh, c=c, h0=h0, gp=gp:
                                   en.activation(
                                dct[gp][:, hh * 128:(hh + 1) * 128],
                                Gb[gp][:, hh * 128:(hh + 1) * 128], AF.Exp,
                                bias=negG[:, c * 32 + h0 + hh:c * 32 + h0 + hh + 1]),
                                reads=[f"Gb{gp}", "negG"], writes=[f"dct{gp}"])
                        p.emit("v", lambda en, sb=sb, gp=gp: en.tensor_tensor(
                            STs[gp][:, :], sb[:, :], dct[gp][:, :], ALU.mult),
                            reads=[sbn, f"dct{gp}"], writes=[f"ST{gp}"])
                        p.emit("v", lambda en, h0=h0, gp=gp, q_c=q_c:
                               en.tensor_tensor(
                            qhat[gp][:, :], q_c[:, h0 * 128:(h0 + 4) * 128],
                            E_sb[gp][:, :], ALU.mult),
                            reads=[qcn, f"E{gp}"], writes=[f"qh{gp}"])
                    fns = [(lambda en, hh=hh, cbk=cbk, gp=gp: en.matmul(
                        cbk[:, hh * 128:(hh + 1) * 128], ktr[gp][:, :],
                        xc24[gp][:, hh * 128:(hh + 1) * 128],
                        start=True, stop=True)) for hh in range(4)]
                    p.group("t", fns, reads=[f"ktr{gp}", f"xc24{gp}"],
                            writes=[cbn])
                    if with_y:
                        p.emit("v", lambda en, h0=h0: en.tensor_copy(
                            stbf[:, :], states[:, h0 * 128:(h0 + 4) * 128]),
                            reads=["states"], writes=["stbf"])
                        for hh in range(4):
                            h = h0 + hh
                            fns = [
                                (lambda en, hh=hh, yb=yb, gp=gp: en.matmul(
                                    yb[:, hh * 128:(hh + 1) * 128],
                                    xc4[gp][:, hh * 128:(hh + 1) * 128],
                                    STs[gp][:, hh * 128:(hh + 1) * 128],
                                    start=True, stop=False)),
                                (lambda en, hh=hh, yb=yb, gp=gp: en.matmul(
                                    yb[:, hh * 128:(hh + 1) * 128],
                                    stbf[:, hh * 128:(hh + 1) * 128],
                                    qhat[gp][:, hh * 128:(hh + 1) * 128],
                                    start=False, stop=True)),
                            ]
                            p.group("t", fns,
                                    reads=[f"xc4{gp}", f"ST{gp}",
                                           "stbf", f"qh{gp}"],
                                    writes=[ybn])
                            p.emit("v", lambda en, h=h, c=c, cbk=cbk, hh=hh:
                                   en.scalar_tensor_tensor(
                                states[:, h * 128:(h + 1) * 128],
                                states[:, h * 128:(h + 1) * 128],
                                decBX[:, h * 8 + c:h * 8 + c + 1],
                                cbk[:, hh * 128:(hh + 1) * 128],
                                ALU.mult, ALU.add),
                                reads=[decBn, cbn, "stbf"],
                                writes=["states"])
                        p.emit("s", lambda en, yb=yb: en.activation(
                            ysq[:, :], yb[:, :], AF.Square),
                            reads=[ybn], writes=["ysq"])
                        p.emit("s", lambda en, yb=yb, gp=gp: en.activation(
                            ysb[gp][:, :], yb[:, :], AF.Copy, scale=gnormc[:, :]),
                            reads=[ybn, "gnormc"], writes=[f"ysb{gp}"])
                        p.emit("t", lambda en, sb=sb: en.matmul(
                            sb[0:1, :], onesc[:, :], ysq[:, :],
                            start=True, stop=True),
                            reads=["ysq", "onesc"], writes=[sbn])
                        p.emit("s", lambda en, sb=sb: en.activation(
                            r1, sb[0:1, :], AF.Ln, scale=1.0 / 128.0,
                            bias=c_eps[0:1, :]),
                            reads=[sbn, "c_eps"], writes=["r1"])
                        p.emit("s", lambda en: en.activation(
                            rsr, r1, AF.Exp, scale=-0.5),
                            reads=["r1"], writes=["rsr"])
                        p.emit("t", lambda en, sb=sb: en.matmul(
                            sb[:, :], onesr[0:1, :], rsr[0:1, :],
                            start=True, stop=True),
                            reads=["rsr", "onesr"], writes=[sbn])
                        p.emit("v", lambda en, sb=sb, gp=gp: en.tensor_tensor(
                            z_sb[:, :], ysb[gp][:, :], sb[:, :], ALU.mult),
                            reads=[f"ysb{gp}", sbn, "ysq"], writes=["ysq"])
                        def yfin(en, c=c, h0=h0, sg_c=sg_c):
                            dstap = bass.AP(bigA, h0 * T + c * 128,
                                            [[NK * T, 128], [T, 4], [1, 128]])
                            return en.tensor_tensor(
                                dstap,
                                z_sb[:, :].rearrange("p (j t) -> p j t", j=4),
                                sg_c[:, h0 * 128:(h0 + 4) * 128].rearrange(
                                    "p (j t) -> p j t", j=4), ALU.mult)
                        p.emit("v", yfin, reads=["ysq", sgcn],
                               writes=["yf"])
                    yield

        def phase_qgate(xn_src, xnname):
            wstA = pv(0, 128, 4096)
            wstB = pv(4096, 128, 4096)
            sc1 = pv(8192, 128, 1024)
            sc2 = pv(9216, 128, 1024)
            sc3 = pv(10240, 128, 1024)
            for m in range(H):
                par = m % 2
                stripe = wstA if par == 0 else wstB
                wb = "wstA" if par == 0 else "wstB"
                dma("g", stripe[:, :], dWq[m, :, :], writes=[wb], sem="wsA" if par == 0 else "wsB")
                pjn = [PB[par * 2], PB[par * 2 + 1]]
                fns = []
                for k in range(NK):
                    for nh in range(2):
                        fns.append(lambda en, k=k, nh=nh, stripe=stripe, par=par:
                                   en.matmul(
                            pb[par * 2 + nh][:, :],
                            stripe[:, k * 128:(k + 1) * 128],
                            xn_src[:, k * T + nh * 512:k * T + (nh + 1) * 512],
                            start=(k == 0), stop=(k == NK - 1)))
                p.group("t", fns, reads=[wb, xnname], writes=pjn)
                p.emit("s", lambda en, par=par: en.copy(
                    sc1[:, 0:512], pb[par * 2][:, :]), reads=[pjn[0]],
                    writes=["prs1"])
                p.emit("s", lambda en, par=par: en.copy(
                    sc1[:, 512:1024], pb[par * 2 + 1][:, :]), reads=[pjn[1]],
                    writes=["prs1"])
                fns = [(lambda en, nh=nh: en.matmul(
                    pb[4 + nh][:, :], S_Tc[:, :],
                    sc1[:, nh * 512:(nh + 1) * 512], start=True, stop=True))
                    for nh in range(2)]
                p.group("t", fns, reads=["prs1", "S_T"], writes=["pb4", "pb5"])
                for nh in range(2):
                    p.emit("v", lambda en, nh=nh: en.tensor_tensor(
                        sc2[:, nh * 512:(nh + 1) * 512], pb[4 + nh][:, :],
                        sinO[:, nh * 512:(nh + 1) * 512], ALU.mult),
                        reads=[PB[4 + nh], "sinO"], writes=["prs2"])
                    p.emit("v", lambda en, nh=nh, par=par: en.tensor_tensor(
                        sc3[:, nh * 512:(nh + 1) * 512], pb[par * 2 + nh][:, :],
                        cosO[:, nh * 512:(nh + 1) * 512], ALU.mult),
                        reads=[pjn[nh], "cosO"], writes=["prs3"])
                p.emit("v", lambda en: en.tensor_tensor(
                    sc3[:, :], sc3[:, :], sc2[:, :], ALU.add),
                    reads=["prs3", "prs2"], writes=["prs3"])
                def spq(en, m=m):
                    return en.dma_start(out=bass.AP(
                        sqsp, m * 128,
                        [[H * 128, 128], [128 * H * 128, NCH], [1, 128]]),
                        in_=sc3[:, :].rearrange("p (c t) -> p c t", c=NCH))
                p.emit("q", spq, reads=["prs3"], writes=["qspd"], sem="qspd", inc=16)
                yield
            for m in range(H):
                par = m % 2
                stripe = wstA if par == 0 else wstB
                wb = "wstA" if par == 0 else "wstB"
                dma("g", stripe[:, :], dWg[m, :, :], writes=[wb], sem="wsA" if par == 0 else "wsB")
                pjn = [PB[par * 2], PB[par * 2 + 1]]
                fns = []
                for k in range(NK):
                    for nh in range(2):
                        fns.append(lambda en, k=k, nh=nh, stripe=stripe, par=par:
                                   en.matmul(
                            pb[par * 2 + nh][:, :],
                            stripe[:, k * 128:(k + 1) * 128],
                            xn_src[:, k * T + nh * 512:k * T + (nh + 1) * 512],
                            start=(k == 0), stop=(k == NK - 1)))
                p.group("t", fns, reads=[wb, xnname], writes=pjn)
                p.emit("s", lambda en, par=par: en.activation(
                    sc3[:, 0:512], pb[par * 2][:, :], AF.Silu),
                    reads=[pjn[0]], writes=["prs3"])
                p.emit("s", lambda en, par=par: en.activation(
                    sc3[:, 512:1024], pb[par * 2 + 1][:, :], AF.Silu),
                    reads=[pjn[1]], writes=["prs3"])
                def spg(en, m=m):
                    return en.dma_start(out=bass.AP(
                        ssgsp, m * 128,
                        [[H * 128, 128], [128 * H * 128, NCH], [1, 128]]),
                        in_=sc3[:, :].rearrange("p (c t) -> p c t", c=NCH))
                p.emit("q", spg, reads=["prs3"], writes=["sgspd"], sem="sgspd", inc=16)
                yield

        def phase_ffnpart(xn_src, xnname):
            stf = [pv(0, 128, 4096), pv(4096, 128, 4096)]
            xTm = pv(8192, 128, 1024, F32)
            t2 = pv(10240, 128, 512, F32)
            outst = pv(11264, 128, 512, F32)
            for m in range(NK):
                par = m % 2
                fb_ = "wstA" if par == 0 else "wstB"
                dma("g", stf[par][:, :], dWf[m, :, :], writes=[fb_],
                    sem="wsA" if par == 0 else "wsB")
                dma("q", xTm[:, :], dxT[m * 128:(m + 1) * 128, :],
                    writes=["o_xT"], sem="o_xT")
                pjf = [PB[par * 2], PB[par * 2 + 1]]
                fns = []
                for k in range(NK):
                    for nh in range(2):
                        fns.append(lambda en, k=k, nh=nh, par=par: en.matmul(
                            pb[par * 2 + nh][:, :],
                            stf[par][:, k * 128:(k + 1) * 128],
                            xn_src[:, k * T + nh * 512:k * T + (nh + 1) * 512],
                            start=(k == 0), stop=(k == NK - 1)))
                p.group("t", fns, reads=[fb_, xnname], writes=pjf)
                for nh in range(2):
                    p.emit("v", lambda en, m=m, nh=nh, par=par:
                           en.scalar_tensor_tensor(
                        t2, pb[par * 2 + nh][:, :], rowselc[:, m:m + 1],
                        qselB[:, nh * 512:(nh + 1) * 512],
                        ALU.mult, ALU.mult),
                        reads=[pjf[nh], "rowselc", "qselB"], writes=["o_t2"])
                    p.emit("v", lambda en, m=m, nh=nh: en.scalar_tensor_tensor(
                        t2, qselB[:, nh * 512:(nh + 1) * 512],
                        biasrs[:, m:m + 1], t2, ALU.mult, ALU.add),
                        reads=["qselB", "biasrs", "o_t2"], writes=["o_t2"])
                    p.emit("v", lambda en, nh=nh: en.tensor_tensor(
                        outst, t2, xTm[:, nh * 512:(nh + 1) * 512], ALU.add),
                        reads=["o_t2", "o_xT"], writes=["o_out"])
                    dma("q", spart.ap()[m * 128:(m + 1) * 128,
                                        nh * 512:(nh + 1) * 512],
                        outst, reads=["o_out"], writes=["partd"], sem="partd")
                yield

        def phase_wo():
            stw = [pv(0, 128, 4096), pv(4096, 128, 4096)]
            prt = pv(16384, 128, 1024, F32)
            t4 = pv(20480, 128, 512, F32)
            outst = pv(21504, 128, 512, F32)
            for m in range(NK):
                par = m % 2
                wb_ = "wstA" if par == 0 else "wstB"
                dma("g", stw[par][:, :], dWo[m, :, :], writes=[wb_], sem="wsA" if par == 0 else "wsB")
                dma("q", prt[:, :], spart.ap()[m * 128:(m + 1) * 128, :],
                    reads=["partd"], writes=["o_prt"])
                pjw = [PB[par * 2], PB[par * 2 + 1]]
                fns = []
                for k in range(NK):
                    for nh in range(2):
                        fns.append(lambda en, k=k, nh=nh, par=par: en.matmul(
                            pb[par * 2 + nh][:, :],
                            stw[par][:, k * 128:(k + 1) * 128],
                            bigA[:, k * T + nh * 512:k * T + (nh + 1) * 512],
                            start=(k == 0), stop=(k == NK - 1)))
                p.group("t", fns, reads=[wb_, "yf"], writes=pjw)
                for nh in range(2):
                    p.emit("v", lambda en, nh=nh, par=par: en.scalar_tensor_tensor(
                        t4, pb[par * 2 + nh][:, :], 1.0,
                        prt[:, nh * 512:(nh + 1) * 512], ALU.mult, ALU.add),
                        reads=[pjw[nh], "o_prt"], writes=["o_t4"])
                    p.emit("v", lambda en: en.tensor_copy(outst, t4),
                           reads=["o_t4"], writes=["o_out"])
                    dma("q", dout[m * 128:(m + 1) * 128, nh * 512:(nh + 1) * 512],
                        outst, reads=["o_out"], writes=["outd"])

        def phase_debug():
            if not DEBUG:
                return
            for n, t_ in [("dmax", dmax), ("selc", selc), ("rowselc", rowselc),
                          ("qselB", qselB), ("biasrs", biasrs), ("qrowf", qrowf),
                          ("states", states), ("dtsb", dtsb), ("decB", decB),
                          ("negG", negG), ("ekd", ekd)]:
                if t_.dtype if False else True:
                    pass
                def mk(en, n=n, t_=t_):
                    return en.dma_start(out=ddbg[n], in_=t_[:, :])
                p.emit("g", mk, reads=[n if n in p.bufs else "outd"],
                       extra_deps=[("v", p.cnt["v"]), ("s", p.cnt["s"]),
                                   ("t", p.cnt["t"])], sem="dbgd", inc=16)

        def run2(ga, gb):
            ga, gb = iter(ga), iter(gb)
            da = db = False
            while not (da and db):
                if not da:
                    try:
                        next(ga)
                    except StopIteration:
                        da = True
                if not db:
                    try:
                        next(gb)
                    except StopIteration:
                        db = True

        def drain(g):
            for _ in g:
                pass

        phase_ln(dxw, bigA, "xnA", "lw")
        p.fence()
        phase_kvdt(bigA[:, :], "xnA", cosW, "cosW", sinW, "sinW", "w",
                   sksp_w, "kspw", svsp_w, "vspw",
                   dtsbW, "dtsbW", dteW, "dteW", decBW, "decBW")
        p.fence()
        phase_ln(dx, bigA, "xnA", "lo")
        p.fence()
        phase_kvdt(bigA[:, :], "xnA", cosO, "cosO", sinO, "sinO", "o",
                   sksp, "ksp", svsp, "vsp",
                   dtsb, "dtsb", dte, "dte", decB, "decB")
        p.fence()
        drain(phase_attn(False, "wa", sksp_w, svsp_w, "kspw", "vspw",
                         [0, 1], [0, 1], [0, 1], 2,
                         dtsbW, "dtsbW", dteW, "dteW", decBW, "decBW"))
        p.fence()
        drain(phase_qgate(bigA[:, :], "xnA"))
        p.fence()
        drain(phase_p1())
        p.fence()
        drain(phase_ffnpart(bigA[:, :], "xnA"))
        p.fence()
        drain(phase_attn(True, "oa", sksp, svsp, "ksp", "vsp",
                         [0, 1], [2, 3], [4, 5], 6,
                         dtsb, "dtsb", dte, "dte", decB, "decB"))
        p.fence()
        phase_wo()
        phase_debug()
        with nc.Block() as block:
            p.replay(nc, es, block)

    return nc


_CACHE = {}


def _blk(w, n_m):
    return np.ascontiguousarray(
        w.reshape(n_m, 128, NK, 128).transpose(0, 3, 2, 1).reshape(n_m, 128, HID))


def _rope_tables(pos):
    inv = 1.0 / ROPE_THETA ** (np.arange(0, D, 2, dtype=np.float64) / D)
    ang = np.concatenate([inv, inv])[:, None] * pos[None, :].astype(np.float64)
    return (np.ascontiguousarray(np.cos(ang).astype(np.float32)),
            np.ascontiguousarray(np.sin(ang).astype(np.float32)))


def kernel(L0_attn_hidden_states, L0_attn_Wq, L0_attn_Wk, L0_attn_Wv,
           L0_attn_Wo, L0_attn_Wg, L0_attn_Win, L0_attn_dt_bias,
           L0_attn_A_log_bias, L0_attn_gnorm_w, L0_ffn_weight, L0_ffn_bias,
           L0_ffn_centroids, L0_ffn_assignments):
    x = np.ascontiguousarray(np.asarray(L0_attn_hidden_states, dtype=np.float32))
    Wq = np.asarray(L0_attn_Wq, np.float32)
    Wk = np.asarray(L0_attn_Wk, np.float32)
    Wv = np.asarray(L0_attn_Wv, np.float32)
    Wo = np.asarray(L0_attn_Wo, np.float32)
    Wg = np.asarray(L0_attn_Wg, np.float32)
    Win = np.asarray(L0_attn_Win, np.float32)
    dtb = np.asarray(L0_attn_dt_bias, np.float32)
    Alog = np.asarray(L0_attn_A_log_bias, np.float32)
    gnw = np.asarray(L0_attn_gnorm_w, np.float32)
    Wf = np.asarray(L0_ffn_weight, np.float32)
    fb = np.asarray(L0_ffn_bias, np.float32)
    cent = np.asarray(L0_ffn_centroids, np.float32)
    assign = np.asarray(L0_ffn_assignments, np.int32)

    if "nc" not in _CACHE:
        _CACHE["nc"] = build_program()
    nc = _CACHE["nc"]

    xTall = np.ascontiguousarray(x.reshape(ALLT, HID).T)
    Wq_b = _blk(Wq, 32)
    Wk_b = _blk(Wk, 8)
    Wv_b = _blk(Wv, 8)
    Wg_b = _blk(Wg, 32)
    Wo_b = _blk(Wo, 32)
    Wf_b = _blk(Wf, 32)
    Win_t = np.ascontiguousarray(Win.T)
    CWO = np.ascontiguousarray(
        np.concatenate([cent.T, np.ones((HID, 1), np.float32)], axis=1))
    fb_c = np.ascontiguousarray(fb.reshape(32, 128).T)
    ident = np.eye(128, dtype=np.float32)
    S = np.zeros((128, 128), np.float32)
    S[np.arange(64), np.arange(64) + 64] = -1.0
    S[np.arange(64) + 64, np.arange(64)] = 1.0
    S_T = np.ascontiguousarray(S.T)
    mask1 = np.where(np.arange(128)[None, :] >= np.arange(128)[:, None],
                     0.0, NEG_BIG).astype(np.float32)
    mask4 = np.ascontiguousarray(np.tile(mask1, (1, 4)))
    iota = np.arange(64, dtype=np.float32).reshape(64, 1)

    in_maps = []
    for c in range(NCORES):
        b, s = divmod(c, 4)
        t0 = s * T
        cos_o, sin_o = _rope_tables(np.arange(t0, t0 + T))
        w0 = max(s - 1, 0) * T
        cos_w, sin_w = _rope_tables(np.arange(w0, w0 + T))
        xw = (np.zeros((T, HID), np.float32) if s == 0
              else np.ascontiguousarray(x[b, t0 - T:t0, :]))
        in_maps.append({
            "x_own": np.ascontiguousarray(x[b, t0:t0 + T, :]),
            "x_win": xw,
            "xT_own": np.ascontiguousarray(x[b, t0:t0 + T, :].T),
            "xT_all": xTall,
            "Wq_blk": Wq_b, "Wk_blk": Wk_b, "Wv_blk": Wv_b,
            "Wg_blk": Wg_b, "Wo_blk": Wo_b, "Wf_blk": Wf_b,
            "Win_t": Win_t, "CWO": CWO,
            "ffn_b": fb_c,
            "dt_b": dtb.reshape(32, 1),
            "A_log": Alog.reshape(32, 1),
            "gnorm": gnw.reshape(128, 1),
            "assign": assign.reshape(1, HID),
            "ident": ident, "S_T": S_T, "maskneg4": mask4,
            "cos_own": cos_o, "sin_own": sin_o,
            "cos_win": cos_w, "sin_win": sin_w,
            "iota64_in": iota,
            "own_off": np.array([[b * 32 + s * 8]], np.int32),
        })

    res = run_bass_kernel_spmd(nc, in_maps, core_ids=list(range(NCORES)))
    out = np.empty((B, L, HID), np.float32)
    for c in range(NCORES):
        b, s = divmod(c, 4)
        out[b, s * T:(s + 1) * T, :] = res.results[c]["out"].T
    return out


# version 08
# speedup vs baseline: 1.4124x; 1.0624x over previous
import numpy as np
import concourse.bass as bass
import concourse.mybir as mybir
from concourse.bass_utils import run_bass_kernel_spmd

F32 = mybir.dt.float32
BF16 = mybir.dt.bfloat16
I32 = mybir.dt.int32
AF = mybir.ActivationFunctionType
ALU = mybir.AluOpType
AX = mybir.AxisListType

B, L, HID = 2, 4096, 4096
NCORES = 8
T = 1024
NT = 8
NCH = 8
NK = 32
H, KVH, D = 32, 8, 128
ROPE_THETA = 10000.0
RMS_EPS = 1e-5
LN_EPS = 1e-5
FFN_T = 0.1
FFN_TH = 0.01
ALLT = B * L
NEG_BIG = -1.0e30
DEBUG = False


class Prog:
    ENGINES = ("t", "v", "s", "g", "q")
    WAIT_ALL = ("CONSTg", "CONSTq")

    def __init__(self):
        self.ops = {e: [] for e in self.ENGINES}
        self.cnt = {e: 0 for e in self.ENGINES}
        self.bufs = {}

    def _bump(self, key, inc):
        self.cnt[key] = self.cnt.get(key, 0) + inc

    def _deps(self, reads, writes, extra):
        deps = []
        for b in reads:
            st = self.bufs.get(b)
            if st is not None and st[0] is not None:
                deps.append(st[0])
        for b in writes:
            st = self.bufs.get(b)
            if st is not None:
                if st[0] is not None:
                    deps.append(st[0])
                deps.extend(st[1])
        for d in extra:
            if d is not None:
                deps.append(d)
        best = {}
        for (e, v) in deps:
            if best.get(e, -1) < v:
                best[e] = v
        return list(best.items())

    def emit(self, eng, fn, reads=(), writes=(), extra_deps=(), inc=None,
             sem=None):
        key = sem if sem is not None else eng
        if inc is None:
            inc = 16 if key not in self.ENGINES else (
                0 if eng in ("g", "q") else 1)
            if eng not in ("g", "q") and key in self.ENGINES:
                inc = 1
        deps = self._deps(reads, writes, extra_deps)
        self._bump(key, inc)
        tok = (key, self.cnt[key])
        self.ops[eng].append((fn, deps, inc, key))
        for b in writes:
            self.bufs[b] = [tok, []]
        for b in reads:
            self.bufs.setdefault(b, [None, []])[1].append(tok)
        return tok

    def group(self, eng, fns, reads=(), writes=(), extra_deps=()):
        deps = self._deps(reads, writes, extra_deps)
        inc = 1
        for i, fn in enumerate(fns):
            self.ops[eng].append((fn, deps if i == 0 else [],
                                  inc if i == len(fns) - 1 else 0, eng))
        self._bump(eng, inc)
        tok = (eng, self.cnt[eng])
        for b in writes:
            self.bufs[b] = [tok, []]
        for b in reads:
            self.bufs.setdefault(b, [None, []])[1].append(tok)
        return tok

    def fence(self):
        snap = dict(self.cnt)
        for e in self.ENGINES:
            deps = [(k, v) for k, v in snap.items() if k != e and v > 0]
            self.emit(e, lambda en: en.nop(), extra_deps=deps, inc=0)

    def replay(self, nc, es, block):
        sems = {}
        for k in self.cnt:
            sems[k] = es.enter_context(nc.semaphore(f"sm_{k}"))
        totals = dict(self.cnt)

        def body(eng_name):
            def run(en):
                waited = {}
                for (fn, deps, inc, key) in self.ops[eng_name]:
                    for (dk, dv) in deps:
                        if dk == eng_name and eng_name == "t":
                            continue
                        if dk in self.WAIT_ALL:
                            dv = totals[dk]
                        if waited.get(dk, 0) >= dv:
                            continue
                        en.wait_ge(sems[dk], dv)
                        waited[dk] = dv
                    inst = fn(en)
                    if inc > 0:
                        inst.then_inc(sems[key], inc)
                if eng_name == "q":
                    for k, tv in totals.items():
                        if k not in self.ENGINES and tv > 0 and \
                                waited.get(k, 0) < tv:
                            en.wait_ge(sems[k], tv)
                elif totals.get(eng_name, 0) > 0:
                    en.wait_ge(sems[eng_name], totals[eng_name])
            return run

        block.tensor(body("t"))
        block.vector(body("v"))
        block.scalar(body("s"))
        block.gpsimd(body("g"))
        block.sync(body("q"))


PB = [f"pb{i}" for i in range(8)]


def build_program():
    nc = bass.Bass("TRN2", target_bir_lowering=False)

    dx = nc.dram_tensor("x_own", [T, HID], F32, kind="ExternalInput").ap()
    dxw = nc.dram_tensor("x_win", [T, HID], F32, kind="ExternalInput").ap()
    dxT = nc.dram_tensor("xT_own", [HID, T], F32, kind="ExternalInput").ap()
    dxTall = nc.dram_tensor("xT_all", [HID, ALLT], F32, kind="ExternalInput").ap()
    dWq = nc.dram_tensor("Wq_blk", [32, 128, HID], F32, kind="ExternalInput").ap()
    dWk = nc.dram_tensor("Wk_blk", [8, 128, HID], F32, kind="ExternalInput").ap()
    dWv = nc.dram_tensor("Wv_blk", [8, 128, HID], F32, kind="ExternalInput").ap()
    dWg = nc.dram_tensor("Wg_blk", [32, 128, HID], F32, kind="ExternalInput").ap()
    dWo = nc.dram_tensor("Wo_blk", [32, 128, HID], F32, kind="ExternalInput").ap()
    dWf = nc.dram_tensor("Wf_blk", [32, 128, HID], F32, kind="ExternalInput").ap()
    dWin = nc.dram_tensor("Win_t", [HID, 32], F32, kind="ExternalInput").ap()
    dCWO = nc.dram_tensor("CWO", [HID, 65], F32, kind="ExternalInput").ap()
    dfb = nc.dram_tensor("ffn_b", [128, 32], F32, kind="ExternalInput").ap()
    ddtb = nc.dram_tensor("dt_b", [32, 1], F32, kind="ExternalInput").ap()
    dAlog = nc.dram_tensor("A_log", [32, 1], F32, kind="ExternalInput").ap()
    dgn = nc.dram_tensor("gnorm", [128, 1], F32, kind="ExternalInput").ap()
    dassign = nc.dram_tensor("assign", [1, HID], I32, kind="ExternalInput").ap()
    dI = nc.dram_tensor("ident", [128, 128], F32, kind="ExternalInput").ap()
    dS_T = nc.dram_tensor("S_T", [128, 128], F32, kind="ExternalInput").ap()
    dmask = nc.dram_tensor("maskneg4", [128, 512], F32, kind="ExternalInput").ap()
    dcos_o = nc.dram_tensor("cos_own", [128, T], F32, kind="ExternalInput").ap()
    dsin_o = nc.dram_tensor("sin_own", [128, T], F32, kind="ExternalInput").ap()
    dcos_w = nc.dram_tensor("cos_win", [128, T], F32, kind="ExternalInput").ap()
    dsin_w = nc.dram_tensor("sin_win", [128, T], F32, kind="ExternalInput").ap()
    diota = nc.dram_tensor("iota64_in", [64, 1], F32, kind="ExternalInput").ap()
    doff = nc.dram_tensor("own_off", [1, 1], I32, kind="ExternalInput").ap()

    dout = nc.dram_tensor("out", [HID, T], F32, kind="ExternalOutput").ap()
    if DEBUG:
        ddbg = {n: nc.dram_tensor(f"dbg_{n}", sh, F32, kind="ExternalOutput").ap()
                for n, sh in [("dmax", [64, 1]), ("selc", [64, 1]),
                              ("rowselc", [128, 32]), ("qselB", [128, T]),
                              ("biasrs", [128, 32]), ("qrowf", [1, T]),
                              ("states", [128, H * 128]),
                              ("dtsb", [128, 256]), ("decB", [128, 256]),
                              ("negG", [128, 256]), ("ekd", [128, 256])]}

    sGd = nc.dram_tensor("Gdram", [32, T], F32)
    sEGb = nc.dram_tensor("EGbf", [32, T], BF16)
    sdec = nc.dram_tensor("decdram", [32, 8], BF16)
    sqsp = nc.dram_tensor("qspill", [NCH, 128, H, 128], BF16)
    ssgsp = nc.dram_tensor("sgspill", [NCH, 128, H, 128], BF16)
    sksp = nc.dram_tensor("kspill", [NCH, 128, KVH, 128], BF16)
    svsp = nc.dram_tensor("vspill", [NCH, 128, KVH, 128], BF16)
    sksp_w = nc.dram_tensor("kspill_w", [NCH, 128, KVH, 128], BF16)
    svsp_w = nc.dram_tensor("vspill_w", [NCH, 128, KVH, 128], BF16)
    spart = nc.dram_tensor("spartial", [HID, T], F32)
    sqsel = nc.dram_tensor("qseld", [128, 64], BF16)
    srowd = nc.dram_tensor("rowseld", [1, HID], F32)

    p = Prog()

    from contextlib import ExitStack
    es = ExitStack()
    SBt = lambda name, shape, dt: es.enter_context(nc.sbuf_tensor(name, shape, dt))
    PSt = lambda name, shape, dt: es.enter_context(nc.psum_tensor(name, shape, dt))

    with es:
        bigA = SBt("bigA", [128, NK * T], BF16)
        states = SBt("states", [128, H * 128], F32)
        dtsb = SBt("dtsb", [128, 256], F32)
        ekd = SBt("ekd", [128, 256], F32)
        dte = SBt("dte", [128, 256], F32)
        decB = SBt("decB", [128, 256], F32)
        dtsbW = SBt("dtsbW", [128, 256], F32)
        dteW = SBt("dteW", [128, 256], F32)
        decBW = SBt("decBW", [128, 256], F32)
        negG = SBt("negG", [128, 256], F32)
        wint = SBt("wint", [128, NK * 32], BF16)
        cwot = SBt("cwot", [128, NK * 65], BF16)
        I_bf = SBt("I_bf", [128, 128], BF16)
        I_f = SBt("I_f", [128, 128], F32)
        S_Tc = SBt("S_Tc", [128, 128], BF16)
        maskc = SBt("maskc", [128, 512], F32)
        cosO = SBt("cosO", [128, T], BF16)
        sinO = SBt("sinO", [128, T], BF16)
        cosW = SBt("cosW", [128, T], BF16)
        sinW = SBt("sinW", [128, T], BF16)
        onesr = SBt("onesr", [1, 128], BF16)
        onesc = SBt("onesc", [128, 1], BF16)
        c_eps = SBt("c_eps", [128, 1], F32)
        c_one = SBt("c_one", [128, 1], F32)
        iota64 = SBt("iota64", [64, 1], F32)
        alog = SBt("alog", [32, 1], F32)
        expA = SBt("expA", [32, 1], F32)
        dtbias = SBt("dtbias", [32, 1], F32)
        gnormc = SBt("gnormc", [128, 1], F32)
        ffnb = SBt("ffnb", [128, 32], F32)
        zeros32 = SBt("zeros32", [32, 128], F32)
        qselB = SBt("qselB", [128, T], BF16)
        rowselc = SBt("rowselc", [128, 32], F32)
        biasrs = SBt("biasrs", [128, 32], F32)
        dmax = SBt("dmax", [64, 1], F32)
        tmax = SBt("tmax", [64, 1], F32)
        selc = SBt("selc", [64, 1], F32)
        selbf = SBt("selbf", [64, 1], BF16)
        qrowf = SBt("qrowf", [1, T], BF16)
        Scol = SBt("Scol", [128, 1], F32)
        Scbf = SBt("Scbf", [1, 65], BF16)
        ScB = SBt("ScB", [128, 64], F32)
        offsb = SBt("offsb", [1, 1], I32)
        POOLN = 23808
        pool = SBt("pool", [128, POOLN], BF16)
        qcb = [SBt(f"qcb{i}", [128, H * 128], BF16) for i in range(2)]
        sgcb = [SBt(f"sgcb{i}", [128, H * 128], BF16) for i in range(2)]
        kcb = [SBt(f"kcb{i}", [128, KVH * 128], BF16) for i in range(2)]
        vcb = [SBt(f"vcb{i}", [128, KVH * 128], BF16) for i in range(2)]
        pb = [PSt(f"psb{i}", [128, 512], F32) for i in range(8)]

        roff = es.enter_context(nc.sync.register("roff"))

        def pv(off, part, free, dt=BF16):
            if dt == BF16:
                return pool[0:part, off:off + free]
            return pool[:, off:off + 2 * free].bitcast(dt)[0:part, 0:free]

        def dma(qe, out_ap, in_ap, reads=(), writes=(), deps=(), slow=False,
                sem=None):
            def fn(en, out_ap=out_ap, in_ap=in_ap, slow=slow):
                return en.dma_start(out=out_ap, in_=in_ap,
                                    allow_slow_non_contiguous=slow)
            if sem is None:
                sem = writes[0] if writes else "misc_dma"
            return p.emit(qe, fn, reads=reads, writes=writes, extra_deps=deps,
                          sem=sem, inc=16)
        
        dma("g", I_bf[:, :], dI, writes=["I_bf"], sem="CONSTg")
        dma("q", I_f[:, :], dI, writes=["I_f"], sem="CONSTq")
        dma("g", S_Tc[:, :], dS_T, writes=["S_T"], sem="CONSTg")
        dma("q", maskc[:, :], dmask, writes=["maskc"], sem="CONSTq")
        dma("g", cosO[:, :], dcos_o, writes=["cosO"], sem="CONSTg")
        dma("g", sinO[:, :], dsin_o, writes=["sinO"], sem="CONSTg")
        dma("g", cosW[:, :], dcos_w, writes=["cosW"], sem="CONSTg")
        dma("g", sinW[:, :], dsin_w, writes=["sinW"], sem="CONSTg")
        dma("q", iota64[:, :], diota, writes=["iota64"], sem="CONSTq")
        dma("q", alog[:, :], dAlog, writes=["alog"], sem="CONSTq")
        dma("q", dtbias[:, :], ddtb, writes=["dtbias"], sem="CONSTq")
        dma("q", gnormc[:, :], dgn, writes=["gnormc"], sem="CONSTq")
        dma("q", ffnb[:, :], dfb, writes=["ffnb"], sem="CONSTq")
        dma("q", offsb[:, :], doff, writes=["offsb"], sem="CONSTq")
        dma("g", wint[:, :].rearrange("p (k h) -> p k h", k=NK),
            dWin.rearrange("(k p) h -> p k h", p=128), writes=["wint"], sem="CONSTg")
        dma("g", cwot[:, :].rearrange("p (k c) -> p k c", k=NK),
            dCWO.rearrange("(k p) c -> p k c", p=128), writes=["cwot"], sem="CONSTg")
        p.emit("v", lambda en: en.memset(onesr[:, :], 1.0), writes=["onesr"])
        p.emit("v", lambda en: en.memset(c_eps[:, :], LN_EPS), writes=["c_eps"])
        p.emit("v", lambda en: en.memset(c_one[:, :], 1.0), writes=["c_one"])
        p.emit("v", lambda en: en.memset(onesc[:, :], 1.0), writes=["onesc"])
        p.emit("v", lambda en: en.memset(zeros32[:, :], 0.0), writes=["zeros32"])
        p.emit("v", lambda en: en.memset(states[:, :], 0.0), writes=["states"])
        p.emit("v", lambda en: en.memset(dmax[:, :], NEG_BIG), writes=["dmax"])
        p.emit("s", lambda en: en.activation(expA[:, :], alog[:, :], AF.Exp),
               reads=["alog"], writes=["expA"])
        p.emit("q", lambda en: en.reg_load(roff, offsb[0:1, 0:1]),
               reads=["offsb"], writes=["roff"], inc=0)

        def phase_p1():
            xt = [pv(0, 128, 2048), pv(2048, 128, 2048), pv(4096, 128, 2048)]
            xq = [pv(6144, 128, 2048), pv(8192, 128, 2048)]
            dots = pv(10240, 65, 512, F32)
            dotsB = pv(11264, 65, 512, F32)
            a_sb = pv(12288, 128, 64, F32)
            e_sb = pv(12416, 128, 64, F32)
            col = lambda i: pv(12544 + i * 2, 128, 1, F32)
            mcol, m2c, varc, rstdc, zmaxc, nzmaxc, Zc, rZc, qmx = (
                col(i) for i in range(9))
            dots_bf = pv(12608, 128, 64)
            qse = pv(12672, 128, 1, F32)
            Mblk = pv(12736, 64, 512)
            ai_blk = pv(13248, 1, 512, I32)
            ab_blk = pv(14272, 1, 512)
            rowblk = pv(14784, 1, 512, F32)

            fns = [(lambda en, k=k: en.matmul(
                pb[7][0:65, 0:1], cwot[:, k * 65:(k + 1) * 65], onesc[:, :],
                start=(k == 0), stop=(k == NK - 1))) for k in range(NK)]
            p.group("t", fns, reads=["cwot", "onesc"], writes=["pb7"])
            p.emit("s", lambda en: en.copy(Scol[0:65, 0:1], pb[7][0:65, 0:1]),
                   reads=["pb7"], writes=["Scol"])
            p.emit("t", lambda en: en.transpose(
                pb[6][0:1, 0:65], Scol[0:65, 0:1], I_f[0:65, 0:65]),
                reads=["Scol", "I_f"], writes=["pb6"])
            p.emit("s", lambda en: en.copy(Scbf[0:1, 0:65], pb[6][0:1, 0:65]),
                   reads=["pb6"], writes=["Scbf"])
            p.emit("t", lambda en: en.matmul(
                pb[6][:, 0:64], onesr[0:1, :], Scbf[0:1, 0:64],
                start=True, stop=True),
                reads=["onesr", "Scbf"], writes=["pb6"])
            p.emit("s", lambda en: en.copy(ScB[:, :], pb[6][:, 0:64]),
                   reads=["pb6"], writes=["ScB"])
            yield

            for tg in range(4):
                for k in range(NK):
                    par = k % 3
                    pq = k % 2
                    bx, bq = f"p1x{par}", f"p1q{pq}"
                    dma("g", xt[par][:, :],
                        dxTall[k * 128:(k + 1) * 128, tg * 2048:(tg + 1) * 2048],
                        writes=[bx])
                    p.emit("v", lambda en, par=par, pq=pq: en.tensor_mul(
                        xq[pq][:, :], xt[par][:, :], xt[par][:, :]),
                        reads=[bx], writes=[bq])
                    fns = []
                    for n in range(4):
                        fns.append(lambda en, k=k, n=n, par=par: en.matmul(
                            pb[n][0:65, :], cwot[:, k * 65:(k + 1) * 65],
                            xt[par][:, n * 512:(n + 1) * 512],
                            start=(k == 0), stop=(k == NK - 1)))
                    for n in range(4):
                        fns.append(lambda en, k=k, n=n, pq=pq: en.matmul(
                            pb[4 + n][0:65, :], cwot[:, k * 65:(k + 1) * 65],
                            xq[pq][:, n * 512:(n + 1) * 512],
                            start=(k == 0), stop=(k == NK - 1)))
                    p.group("t", fns, reads=[bx, bq, "cwot"], writes=list(PB))
                    if k % 8 == 7:
                        yield
                for n in range(4):
                    p.emit("s", lambda en, n=n: en.copy(dots[:, :], pb[n][0:65, :]),
                           reads=[PB[n]], writes=["p1d"])
                    p.emit("s", lambda en, n=n: en.copy(dotsB[:, :],
                                                        pb[4 + n][0:65, :]),
                           reads=[PB[4 + n]], writes=["p1dB"])
                    for j in range(4):
                        tile = tg * 16 + n * 4 + j
                        p.emit("t", lambda en, j=j, n=n: en.transpose(
                            pb[n][:, 0:65], dots[:, j * 128:(j + 1) * 128],
                            I_f[0:65, 0:65]),
                            reads=["p1d", "I_f"], writes=[PB[n]])
                        p.emit("t", lambda en, j=j, n=n: en.transpose(
                            pb[4 + n][:, 0:65], dotsB[:, j * 128:(j + 1) * 128],
                            I_f[0:65, 0:65]),
                            reads=["p1dB", "I_f"], writes=[PB[4 + n]])
                        p.emit("v", lambda en, n=n: en.tensor_scalar(
                            mcol, pb[n][:, 64:65], 1.0 / HID, None, ALU.mult),
                            reads=[PB[n]], writes=["p1m"])
                        p.emit("v", lambda en: en.tensor_tensor(
                            m2c, mcol, mcol, ALU.mult),
                            reads=["p1m"], writes=["p1m2"])
                        p.emit("v", lambda en, n=n: en.scalar_tensor_tensor(
                            varc, pb[4 + n][:, 64:65], 1.0 / HID, m2c,
                            ALU.mult, ALU.subtract),
                            reads=[PB[4 + n], "p1m2"], writes=["p1var"])
                        p.emit("s", lambda en: en.activation(
                            rstdc, varc, AF.Ln, bias=c_eps[:, :]),
                            reads=["p1var", "c_eps"], writes=["p1rstd"])
                        p.emit("s", lambda en: en.activation(
                            rstdc, rstdc, AF.Exp, scale=-0.5),
                            reads=["p1rstd"], writes=["p1rstd"])
                        p.emit("v", lambda en: en.tensor_scalar(
                            m2c, mcol, -1.0, None, ALU.mult),
                            reads=["p1m"], writes=["p1m2"])
                        p.emit("v", lambda en, n=n: en.scalar_tensor_tensor(
                            a_sb, ScB[:, :], m2c, pb[n][:, 0:64],
                            ALU.mult, ALU.add),
                            reads=["ScB", "p1m2", PB[n]], writes=["p1a"])
                        p.emit("v", lambda en: en.tensor_scalar(
                            a_sb, a_sb, rstdc, 1.0 / FFN_T, ALU.mult, ALU.mult),
                            reads=["p1a", "p1rstd"], writes=["p1a"])
                        p.emit("v", lambda en: en.reduce_max(zmaxc, a_sb, axis=AX.X),
                               reads=["p1a"], writes=["p1zm"])
                        p.emit("v", lambda en: en.tensor_scalar(
                            nzmaxc, zmaxc, -1.0, None, ALU.mult),
                            reads=["p1zm"], writes=["p1nzm"])
                        p.emit("s", lambda en: en.activation(
                            e_sb, a_sb, AF.Exp, bias=nzmaxc),
                            reads=["p1a", "p1nzm"], writes=["p1e"])
                        p.emit("v", lambda en: en.reduce_sum(Zc, e_sb, axis=AX.X),
                               reads=["p1e"], writes=["p1Z"])
                        p.emit("v", lambda en: en.reciprocal(rZc, Zc),
                               reads=["p1Z"], writes=["p1rZ"])
                        p.emit("v", lambda en: en.tensor_scalar(
                            dots_bf, e_sb, rZc, None, ALU.mult),
                            reads=["p1e", "p1rZ"], writes=["p1dbf"])
                        p.emit("v", lambda en: en.tensor_scalar(
                            qse, rZc, FFN_TH, None, ALU.is_gt),
                            reads=["p1rZ"], writes=["p1qse"])
                        dma("g", bass.AP(sqsel, tile, [[64, 128], [1, 1]]), qse,
                            reads=["p1qse"], writes=["qseld"], slow=True)
                        p.emit("t", lambda en, n=n: en.transpose(
                            pb[4 + n][:, :].bitcast(BF16)[0:64, 0:128],
                            dots_bf, I_bf[:, :]),
                            reads=["p1dbf", "I_bf"], writes=[PB[4 + n]])
                        p.emit("v", lambda en, n=n: en.reduce_max(
                            tmax[:, :], pb[4 + n][:, :].bitcast(BF16)[0:64, 0:128],
                            axis=AX.X),
                            reads=[PB[4 + n]], writes=["tmax"])
                        p.emit("v", lambda en: en.tensor_tensor(
                            dmax[:, :], dmax[:, :], tmax[:, :], ALU.max),
                            reads=["tmax", "dmax"], writes=["dmax"])
                        yield

            p.emit("v", lambda en: en.tensor_scalar(
                selc[:, :], dmax[:, :], FFN_TH, None, ALU.is_gt),
                reads=["dmax"], writes=["selc"])
            p.emit("v", lambda en: en.tensor_copy(selbf[:, :], selc[:, :]),
                   reads=["selc"], writes=["selbf"])
            for och in range(8):
                dma("q", ai_blk, dassign[0:1, och * 512:(och + 1) * 512],
                    writes=["p1ai"])
                p.emit("v", lambda en: en.tensor_copy(ab_blk, ai_blk),
                       reads=["p1ai"], writes=["p1ab"])
                p.emit("t", lambda en: en.matmul(
                    pb[6][0:64, :], onesr[0:1, 0:64], ab_blk,
                    start=True, stop=True),
                    reads=["onesr", "p1ab"], writes=["pb6"])
                p.emit("v", lambda en: en.tensor_scalar(
                    Mblk, pb[6][0:64, :], iota64[:, :], None, ALU.is_equal),
                    reads=["pb6", "iota64"], writes=["p1M"])
                p.emit("t", lambda en: en.matmul(
                    pb[7][0:1, :], selbf[:, :], Mblk, start=True, stop=True),
                    reads=["selbf", "p1M"], writes=["pb7"])
                p.emit("v", lambda en: en.tensor_scalar(
                    rowblk, pb[7][0:1, :], 0.5, None, ALU.is_gt),
                    reads=["pb7"], writes=["p1row"])
                dma("q", srowd.ap()[0:1, och * 512:(och + 1) * 512], rowblk,
                    reads=["p1row"], writes=["rowd"])
                yield
            dma("g", rowselc[:, :],
                bass.AP(srowd, 0, [[1, 128], [128, 32]]),
                reads=["rowd"], writes=["rowselc"], slow=True)
            p.emit("v", lambda en: en.tensor_tensor(
                biasrs[:, :], ffnb[:, :], rowselc[:, :], ALU.mult),
                reads=["ffnb", "rowselc"], writes=["biasrs"])

            def rd_qrow(en):
                return en.dma_start(
                    out=qrowf[0:1, :].rearrange("o (tt pp) -> o tt pp", tt=8),
                    in_=bass.AP(sqsel, roff, [[0, 1], [1, 8], [64, 128]]),
                    allow_slow_non_contiguous=True)
            p.emit("q", rd_qrow, reads=["qseld", "roff"], writes=["qrowf"],
                   sem="qrowf", inc=16)
            p.emit("t", lambda en: en.matmul(
                pb[6][:, :], onesr[0:1, :], qrowf[0:1, 0:512],
                start=True, stop=True),
                reads=["onesr", "qrowf"], writes=["pb6"])
            p.emit("t", lambda en: en.matmul(
                pb[7][:, :], onesr[0:1, :], qrowf[0:1, 512:1024],
                start=True, stop=True),
                reads=["onesr", "qrowf"], writes=["pb7"])
            p.emit("s", lambda en: en.copy(qselB[:, 0:512], pb[6][:, :]),
                   reads=["pb6"], writes=["qselB"])
            p.emit("s", lambda en: en.copy(qselB[:, 512:1024], pb[7][:, :]),
                   reads=["pb7"], writes=["qselB"])
            yield

        def phase_ln(xdram, dst_h, dstname, tag):
            xt = pv(0, 128, 4096)
            xnt = pv(4096, 128, 4096)
            xsq = pv(8192, 128, 4096)
            col = lambda i: pv(12544 + i * 2, 128, 1, F32)
            scol, mcol, sscol, varc, rstdc, nmr = (col(i) for i in range(6))
            for tt in range(NT):
                dma("g", xt[:, :], xdram[tt * 128:(tt + 1) * 128, :],
                    writes=[f"{tag}x"])
                p.emit("v", lambda en: en.reduce_sum(scol, xt[:, :], axis=AX.X),
                       reads=[f"{tag}x"], writes=[f"{tag}s"])
                p.emit("v", lambda en: en.tensor_scalar(
                    mcol, scol, 1.0 / HID, None, ALU.mult),
                    reads=[f"{tag}s"], writes=[f"{tag}m"])
                p.emit("s", lambda en: en.activation(xsq[:, :], xt[:, :], AF.Square),
                       reads=[f"{tag}x"], writes=[f"{tag}xsq"])
                p.emit("v", lambda en: en.reduce_sum(sscol, xsq[:, :], axis=AX.X),
                       reads=[f"{tag}xsq"], writes=[f"{tag}ss"])
                p.emit("v", lambda en: en.tensor_tensor(varc, mcol, mcol, ALU.mult),
                       reads=[f"{tag}m"], writes=[f"{tag}var"])
                p.emit("v", lambda en: en.scalar_tensor_tensor(
                    varc, sscol, 1.0 / HID, varc, ALU.mult, ALU.subtract),
                    reads=[f"{tag}ss", f"{tag}var"], writes=[f"{tag}var"])
                p.emit("s", lambda en: en.activation(rstdc, varc, AF.Ln, bias=c_eps[:, :]),
                       reads=[f"{tag}var", "c_eps"], writes=[f"{tag}r"])
                p.emit("s", lambda en: en.activation(rstdc, rstdc, AF.Exp, scale=-0.5),
                       reads=[f"{tag}r"], writes=[f"{tag}r"])
                p.emit("v", lambda en: en.scalar_tensor_tensor(
                    nmr, mcol, -1.0, rstdc, ALU.mult, ALU.mult),
                    reads=[f"{tag}m", f"{tag}r"], writes=[f"{tag}nmr"])
                p.emit("v", lambda en: en.tensor_scalar(
                    xnt[:, :], xt[:, :], rstdc, nmr, ALU.mult, ALU.add),
                    reads=[f"{tag}x", f"{tag}r", f"{tag}nmr"], writes=[f"{tag}xn"])
                for kq in range(0, NK, 8):
                    par = (kq // 8) % 2
                    bank = pb[6 + par]
                    pbf = bank[:, :].bitcast(BF16)
                    fns = []
                    for j in range(8):
                        fns.append(lambda en, k=kq + j, j=j, pbf=pbf: en.transpose(
                            pbf[:, j * 128:(j + 1) * 128],
                            xnt[:, k * 128:(k + 1) * 128], I_bf[:, :]))
                    p.group("t", fns, reads=[f"{tag}xn", "I_bf"],
                            writes=[PB[6 + par]])
                    def evac(en, kq=kq, tt=tt, pbf=pbf):
                        dstap = bass.AP(dst_h, kq * T + tt * 128,
                                        [[NK * T, 128], [T, 8], [1, 128]])
                        return en.tensor_copy(
                            dstap, pbf[:, :].rearrange("p (j t) -> p j t", j=8))
                    p.emit("v", evac, reads=[PB[6 + par]], writes=[dstname])

        def phase_kvdt(xn_src, xnname, cosT, cname, sinT, sname, tag,
                       kdst, kdstn, vdst, vdstn, dtsbX, dtsbn, dteX, dten,
                       decBX, decBn):
            wstA = pv(0, 128, 4096)
            wstB = pv(4096, 128, 4096)
            sc1 = pv(8192, 128, 1024)
            sc2 = pv(9216, 128, 1024)
            sc3 = pv(10240, 128, 1024)
            wintv = wint
            GtT = pv(12288, 32, 1024, F32)
            dtT = pv(14336, 32, 1024, F32)
            gTt = pv(16384, 32, 1024, F32)
            eGtT = pv(18432, 32, 1024)
            zb = pv(19456, 32, 512, F32)
            absz = pv(20480, 32, 512, F32)
            easb = pv(21504, 32, 512, F32)
            rel = pv(22528, 32, 512, F32)
            ek_sb = pv(23552, 32, 128, F32)

            def proj8(wblk, dst, dstname, is_k):
                for m in range(KVH):
                    par = m % 2
                    stripe = wstA if par == 0 else wstB
                    wb = "wstA" if par == 0 else "wstB"
                    dma("g", stripe[:, :], wblk[m, :, :], writes=[wb], sem="wsA" if par == 0 else "wsB")
                    pjn = [PB[par * 2], PB[par * 2 + 1]]
                    fns = []
                    for k in range(NK):
                        for nh in range(2):
                            fns.append(lambda en, k=k, nh=nh, par=par,
                                       stripe=stripe: en.matmul(
                                pb[par * 2 + nh][:, :],
                                stripe[:, k * 128:(k + 1) * 128],
                                xn_src[:, k * T + nh * 512:k * T + (nh + 1) * 512],
                                start=(k == 0), stop=(k == NK - 1)))
                    p.group("t", fns, reads=[wb, xnname], writes=pjn)
                    p.emit("s", lambda en, par=par: en.copy(
                        sc1[:, 0:512], pb[par * 2][:, :]),
                        reads=[pjn[0]], writes=["prs1"])
                    p.emit("s", lambda en, par=par: en.copy(
                        sc1[:, 512:1024], pb[par * 2 + 1][:, :]),
                        reads=[pjn[1]], writes=["prs1"])
                    if is_k:
                        fns = [(lambda en, nh=nh: en.matmul(
                            pb[4 + nh][:, :], S_Tc[:, :],
                            sc1[:, nh * 512:(nh + 1) * 512],
                            start=True, stop=True)) for nh in range(2)]
                        p.group("t", fns, reads=["prs1", "S_T"],
                                writes=["pb4", "pb5"])
                        for nh in range(2):
                            p.emit("v", lambda en, nh=nh: en.tensor_tensor(
                                sc2[:, nh * 512:(nh + 1) * 512], pb[4 + nh][:, :],
                                sinT[:, nh * 512:(nh + 1) * 512], ALU.mult),
                                reads=[PB[4 + nh], sname], writes=["prs2"])
                            p.emit("v", lambda en, nh=nh, par=par: en.tensor_tensor(
                                sc3[:, nh * 512:(nh + 1) * 512],
                                pb[par * 2 + nh][:, :],
                                cosT[:, nh * 512:(nh + 1) * 512], ALU.mult),
                                reads=[pjn[nh], cname], writes=["prs3"])
                        p.emit("v", lambda en: en.tensor_tensor(
                            sc3[:, :], sc3[:, :], sc2[:, :], ALU.add),
                            reads=["prs3", "prs2"], writes=["prs3"])
                    else:
                        pbf4 = pb[4][:, :].bitcast(BF16)
                        pbf5 = pb[5][:, :].bitcast(BF16)
                        fns = []
                        for j in range(8):
                            tgt = pbf4 if j < 4 else pbf5
                            fns.append(lambda en, j=j, tgt=tgt: en.transpose(
                                tgt[:, (j % 4) * 128:(j % 4 + 1) * 128],
                                sc1[:, j * 128:(j + 1) * 128], I_bf[:, :]))
                        p.group("t", fns, reads=["prs1", "I_bf"],
                                writes=["pb4", "pb5"])
                        p.emit("v", lambda en, pbf4=pbf4: en.tensor_copy(
                            sc3[:, 0:512], pbf4[:, 0:512]),
                            reads=["pb4"], writes=["prs3"])
                        p.emit("v", lambda en, pbf5=pbf5: en.tensor_copy(
                            sc3[:, 512:1024], pbf5[:, 0:512]),
                            reads=["pb5"], writes=["prs3"])
                    def sp(en, m=m, dst=dst):
                        return en.dma_start(out=bass.AP(
                            dst, m * 128,
                            [[KVH * 128, 128], [128 * KVH * 128, NCH], [1, 128]]),
                            in_=sc3[:, :].rearrange("p (c t) -> p c t", c=NCH))
                    p.emit("q", sp, reads=["prs3"], writes=[dstname], sem=dstname, inc=16)

            proj8(dWk, kdst, kdstn, True)
            proj8(dWv, vdst, vdstn, False)

            fns = []
            for k in range(NK):
                for nh in range(2):
                    fns.append(lambda en, k=k, nh=nh: en.matmul(
                        pb[nh][0:32, :], wint[:, k * 32:(k + 1) * 32],
                        xn_src[:, k * T + nh * 512:k * T + (nh + 1) * 512],
                        start=(k == 0), stop=(k == NK - 1)))
            p.group("t", fns, reads=["wint", xnname], writes=["pb0", "pb1"])
            for nh in range(2):
                sl = slice(nh * 512, (nh + 1) * 512)
                p.emit("v", lambda en, nh=nh: en.tensor_scalar(
                    zb, pb[nh][0:32, :], dtbias[:, :], None, ALU.add),
                    reads=[PB[nh], "dtbias"], writes=[f"{tag}zb"])
                p.emit("v", lambda en: en.tensor_scalar(
                    rel, zb, -1.0, None, ALU.mult),
                    reads=[f"{tag}zb"], writes=[f"{tag}rel"])
                p.emit("v", lambda en: en.tensor_tensor(
                    absz, zb, rel, ALU.max),
                    reads=[f"{tag}zb", f"{tag}rel"], writes=[f"{tag}ab"])
                p.emit("s", lambda en: en.activation(easb, absz, AF.Exp, scale=-1.0),
                       reads=[f"{tag}ab"], writes=[f"{tag}ea"])
                p.emit("s", lambda en: en.activation(easb, easb, AF.Ln, bias=c_one[0:32, :]),
                       reads=[f"{tag}ea", "c_one"], writes=[f"{tag}ea"])
                p.emit("v", lambda en: en.tensor_relu(rel, zb),
                       reads=[f"{tag}zb"], writes=[f"{tag}rel"])
                p.emit("v", lambda en, sl=sl: en.tensor_tensor(
                    dtT[:, sl], rel, easb, ALU.add),
                    reads=[f"{tag}rel", f"{tag}ea"], writes=["dtT"])
            p.emit("v", lambda en: en.tensor_scalar(
                gTt[:, :], dtT[:, :], expA[:, :], -1.0, ALU.mult, ALU.mult),
                reads=["dtT", "expA"], writes=["gTt"])
            for c in range(NCH):
                p.emit("v", lambda en, c=c: en.tensor_tensor_scan(
                    GtT[:, c * 128:(c + 1) * 128], gTt[:, c * 128:(c + 1) * 128],
                    zeros32[:, :], 0.0, ALU.add, ALU.add),
                    reads=["gTt", "zeros32"], writes=["GtT"])
            p.emit("s", lambda en: en.activation(eGtT[:, :], GtT[:, :], AF.Exp),
                   reads=["GtT"], writes=["eGtT"])
            for c in range(NCH):
                sl = slice(c * 128, (c + 1) * 128)
                cs = slice(c * 32, (c + 1) * 32)
                p.emit("t", lambda en, sl=sl: en.transpose(
                    pb[6][:, 0:32], GtT[:, sl], I_f[0:32, 0:32]),
                    reads=["GtT", "I_f"], writes=["pb6"])
                p.emit("s", lambda en, cs=cs: en.activation(
                    negG[:, cs], pb[6][:, 0:32], AF.Copy, scale=-1.0),
                    reads=["pb6"], writes=["negG"])
                p.emit("s", lambda en, sl=sl, c=c: en.activation(
                    ek_sb, GtT[:, sl], AF.Exp, scale=-1.0,
                    bias=GtT[:, c * 128 + 127:c * 128 + 128]),
                    reads=["GtT"], writes=[f"{tag}ek"])
                p.emit("t", lambda en: en.transpose(
                    pb[7][:, 0:32], ek_sb, I_f[0:32, 0:32]),
                    reads=[f"{tag}ek", "I_f"], writes=["pb7"])
                p.emit("s", lambda en, cs=cs: en.copy(ekd[:, cs], pb[7][:, 0:32]),
                       reads=["pb7"], writes=["ekd"])
                p.emit("t", lambda en, sl=sl: en.transpose(
                    pb[6][:, 0:32], dtT[:, sl], I_f[0:32, 0:32]),
                    reads=["dtT", "I_f"], writes=["pb6"])
                p.emit("s", lambda en, cs=cs: en.copy(dtsbX[:, cs], pb[6][:, 0:32]),
                       reads=["pb6"], writes=[dtsbn])
            p.emit("v", lambda en: en.tensor_tensor(
                dteX[:, :], dtsbX[:, :], ekd[:, :], ALU.mult),
                reads=[dtsbn, "ekd"], writes=[dten])
            dma("q", sGd.ap(), GtT[:, :], reads=["GtT"], writes=["Gd"])
            dma("q", sEGb.ap(), eGtT[:, :], reads=["eGtT"], writes=["EGd"])
            dma("q", sdec.ap(),
                bass.AP(pool, eGtT.offset + 127, [[POOLN, 32], [128, 8]]),
                reads=["eGtT"], writes=["decd"], slow=True)
            dma("g", decBX[:, :],
                bass.AP(sdec, 0, [[0, 128], [8, 32], [1, 8]]),
                reads=["decd"], writes=[decBn], sem=decBn)

        def phase_attn(with_y, tag, kspill, vspill, kspn, vspn,
                       scb, csb, ybanks, ktb, dtsbX, dtsbn, dteX, dten,
                       decBX, decBn):
            ktr = [pv(11520, 128, 128), pv(11648, 128, 128)]
            xc4 = [pv(11776, 128, 512), pv(12288, 128, 512)]
            xc24 = [pv(12800, 128, 512), pv(13312, 128, 512)]
            stbf = pv(13824, 128, 512)
            qhat = [pv(14336, 128, 512), pv(14848, 128, 512)]
            Gb = [pv(15360, 128, 512, F32), pv(16384, 128, 512, F32)]
            E_sb = [pv(17408, 128, 512), pv(17920, 128, 512)]
            dct = [pv(18432, 128, 512), pv(18944, 128, 512)]
            STs = [pv(19456, 128, 512), pv(19968, 128, 512)]
            ysq = pv(20480, 128, 512)
            z_sb = pv(20480, 128, 512)
            ysb = [pv(20992, 128, 512), pv(21504, 128, 512)]
            r1 = pv(22016, 1, 512, F32)
            rsr = pv(23040, 1, 512)

            def load_chunk(c):
                par = c % 2
                dma("q", kcb[par][:, :].rearrange("p (m t) -> p m t", m=KVH),
                    bass.AP(kspill, c * 128 * KVH * 128,
                            [[KVH * 128, 128], [128, KVH], [1, 128]]),
                    reads=[kspn], writes=[f"kc{par}"], sem=f"kc{par}")
                dma("q", vcb[par][:, :].rearrange("p (m t) -> p m t", m=KVH),
                    bass.AP(vspill, c * 128 * KVH * 128,
                            [[KVH * 128, 128], [128, KVH], [1, 128]]),
                    reads=[vspn], writes=[f"vc{par}"], sem=f"vc{par}")
                if with_y:
                    dma("q", qcb[par][:, :].rearrange("p (m t) -> p m t", m=H),
                        bass.AP(sqsp, c * 128 * H * 128,
                                [[H * 128, 128], [128, H], [1, 128]]),
                        reads=["qspd"], writes=[f"qc{par}"], sem=f"qc{par}")
                    dma("q", sgcb[par][:, :].rearrange("p (m t) -> p m t", m=H),
                        bass.AP(ssgsp, c * 128 * H * 128,
                                [[H * 128, 128], [128, H], [1, 128]]),
                        reads=["sgspd"], writes=[f"sgc{par}"], sem=f"sgc{par}")

            def load_ge(c, g):
                if not with_y:
                    return
                gp = g % 2
                h0 = 4 * g
                def gb_dma(en, c=c, h0=h0, gp=gp):
                    return en.dma_start(
                        out=Gb[gp][:, :].rearrange("p (j t) -> p j t", j=4),
                        in_=bass.AP(sGd, h0 * T + c * 128,
                                    [[0, 128], [T, 4], [1, 128]]))
                p.emit("g", gb_dma, reads=["Gd"], writes=[f"Gb{gp}"],
                       sem=f"Gb{gp}", inc=16)
                def e_dma(en, c=c, h0=h0, gp=gp):
                    return en.dma_start(
                        out=E_sb[gp][:, :].rearrange("p (j t) -> p j t", j=4),
                        in_=bass.AP(sEGb, h0 * T + c * 128,
                                    [[0, 128], [T, 4], [1, 128]]))
                p.emit("g", e_dma, reads=["EGd"], writes=[f"E{gp}"],
                       sem=f"E{gp}", inc=16)

            load_chunk(0)
            load_ge(0, 0)
            for c in range(NCH):
                par = c % 2
                if c + 1 < NCH:
                    load_chunk(c + 1)
                k_c, v_c = kcb[par], vcb[par]
                q_c, sg_c = qcb[par], sgcb[par]
                kcn, vcn = f"kc{par}", f"vc{par}"
                qcn, sgcn = f"qc{par}", f"sgc{par}"
                for g in range(KVH):
                    if g + 1 < KVH:
                        load_ge(c, g + 1)
                    elif c + 1 < NCH:
                        load_ge(c + 1, 0)
                    gp = g % 2
                    h0 = 4 * g
                    sb = pb[scb[gp]]
                    cbk = pb[csb[gp]]
                    yb = pb[ybanks[gp]]
                    sbn = PB[scb[gp]]
                    cbn = PB[csb[gp]]
                    ybn = PB[ybanks[gp]]
                    ktpb = pb[ktb]
                    ktpbn = PB[ktb]
                    p.emit("t", lambda en, g=g, k_c=k_c, gp=gp, ktpb=ktpb:
                           en.transpose(
                        ktpb[:, :].bitcast(BF16)[:, gp * 128:gp * 128 + 128],
                        k_c[:, g * 128:(g + 1) * 128], I_bf[:, :]),
                        reads=[kcn, "I_bf"], writes=[ktpbn])
                    p.emit("s", lambda en, gp=gp, ktpb=ktpb: en.copy(
                        ktr[gp][:, :],
                        ktpb[:, :].bitcast(BF16)[:, gp * 128:gp * 128 + 128]),
                        reads=[ktpbn], writes=[f"ktr{gp}"])
                    vblk = bass.AP(vcb[par], g * 128,
                                   [[KVH * 128, 128], [0, 4], [1, 128]])
                    dtc = bass.AP(dtsbX, c * 32 + h0, [[256, 128], [1, 4], [0, 128]])
                    dtec = bass.AP(dteX, c * 32 + h0, [[256, 128], [1, 4], [0, 128]])
                    if with_y:
                        p.emit("v", lambda en, vblk=vblk, dtc=dtc, gp=gp:
                               en.tensor_tensor(
                            xc4[gp][:, :].rearrange("p (j t) -> p j t", j=4),
                            vblk, dtc, ALU.mult),
                            reads=[vcn, dtsbn], writes=[f"xc4{gp}"])
                    p.emit("v", lambda en, vblk=vblk, dtec=dtec, gp=gp:
                           en.tensor_tensor(
                        xc24[gp][:, :].rearrange("p (j t) -> p j t", j=4),
                        vblk, dtec, ALU.mult),
                        reads=[vcn, dten], writes=[f"xc24{gp}"])
                    if with_y:
                        fns = [(lambda en, hh=hh, g=g, sb=sb, h0=h0, k_c=k_c,
                                q_c=q_c: en.matmul(
                            sb[:, hh * 128:(hh + 1) * 128],
                            k_c[:, g * 128:(g + 1) * 128],
                            q_c[:, (h0 + hh) * 128:(h0 + hh + 1) * 128],
                            start=True, stop=True)) for hh in range(4)]
                        p.group("t", fns, reads=[kcn, qcn], writes=[sbn])
                        p.emit("v", lambda en, gp=gp: en.tensor_tensor(
                            Gb[gp][:, :], Gb[gp][:, :], maskc[:, :], ALU.add),
                            reads=[f"Gb{gp}", "maskc"], writes=[f"Gb{gp}"])
                        for hh in range(4):
                            p.emit("s", lambda en, hh=hh, c=c, h0=h0, gp=gp:
                                   en.activation(
                                dct[gp][:, hh * 128:(hh + 1) * 128],
                                Gb[gp][:, hh * 128:(hh + 1) * 128], AF.Exp,
                                bias=negG[:, c * 32 + h0 + hh:c * 32 + h0 + hh + 1]),
                                reads=[f"Gb{gp}", "negG"], writes=[f"dct{gp}"])
                        p.emit("v", lambda en, sb=sb, gp=gp: en.tensor_tensor(
                            STs[gp][:, :], sb[:, :], dct[gp][:, :], ALU.mult),
                            reads=[sbn, f"dct{gp}"], writes=[f"ST{gp}"])
                        p.emit("v", lambda en, h0=h0, gp=gp, q_c=q_c:
                               en.tensor_tensor(
                            qhat[gp][:, :], q_c[:, h0 * 128:(h0 + 4) * 128],
                            E_sb[gp][:, :], ALU.mult),
                            reads=[qcn, f"E{gp}"], writes=[f"qh{gp}"])
                    fns = [(lambda en, hh=hh, cbk=cbk, gp=gp: en.matmul(
                        cbk[:, hh * 128:(hh + 1) * 128], ktr[gp][:, :],
                        xc24[gp][:, hh * 128:(hh + 1) * 128],
                        start=True, stop=True)) for hh in range(4)]
                    p.group("t", fns, reads=[f"ktr{gp}", f"xc24{gp}"],
                            writes=[cbn])
                    if with_y:
                        p.emit("v", lambda en, h0=h0: en.tensor_copy(
                            stbf[:, :], states[:, h0 * 128:(h0 + 4) * 128]),
                            reads=["states"], writes=["stbf"])
                        for hh in range(4):
                            h = h0 + hh
                            fns = [
                                (lambda en, hh=hh, yb=yb, gp=gp: en.matmul(
                                    yb[:, hh * 128:(hh + 1) * 128],
                                    xc4[gp][:, hh * 128:(hh + 1) * 128],
                                    STs[gp][:, hh * 128:(hh + 1) * 128],
                                    start=True, stop=False)),
                                (lambda en, hh=hh, yb=yb, gp=gp: en.matmul(
                                    yb[:, hh * 128:(hh + 1) * 128],
                                    stbf[:, hh * 128:(hh + 1) * 128],
                                    qhat[gp][:, hh * 128:(hh + 1) * 128],
                                    start=False, stop=True)),
                            ]
                            p.group("t", fns,
                                    reads=[f"xc4{gp}", f"ST{gp}",
                                           "stbf", f"qh{gp}"],
                                    writes=[ybn])
                            p.emit("v", lambda en, h=h, c=c, cbk=cbk, hh=hh:
                                   en.scalar_tensor_tensor(
                                states[:, h * 128:(h + 1) * 128],
                                states[:, h * 128:(h + 1) * 128],
                                decBX[:, h * 8 + c:h * 8 + c + 1],
                                cbk[:, hh * 128:(hh + 1) * 128],
                                ALU.mult, ALU.add),
                                reads=[decBn, cbn, "stbf"],
                                writes=["states"])
                        p.emit("s", lambda en, yb=yb: en.activation(
                            ysq[:, :], yb[:, :], AF.Square),
                            reads=[ybn], writes=["ysq"])
                        p.emit("s", lambda en, yb=yb, gp=gp: en.activation(
                            ysb[gp][:, :], yb[:, :], AF.Copy, scale=gnormc[:, :]),
                            reads=[ybn, "gnormc"], writes=[f"ysb{gp}"])
                        p.emit("t", lambda en, sb=sb: en.matmul(
                            sb[0:1, :], onesc[:, :], ysq[:, :],
                            start=True, stop=True),
                            reads=["ysq", "onesc"], writes=[sbn])
                        p.emit("s", lambda en, sb=sb: en.activation(
                            r1, sb[0:1, :], AF.Ln, scale=1.0 / 128.0,
                            bias=c_eps[0:1, :]),
                            reads=[sbn, "c_eps"], writes=["r1"])
                        p.emit("s", lambda en: en.activation(
                            rsr, r1, AF.Exp, scale=-0.5),
                            reads=["r1"], writes=["rsr"])
                        p.emit("t", lambda en, sb=sb: en.matmul(
                            sb[:, :], onesr[0:1, :], rsr[0:1, :],
                            start=True, stop=True),
                            reads=["rsr", "onesr"], writes=[sbn])
                        p.emit("v", lambda en, sb=sb, gp=gp: en.tensor_tensor(
                            z_sb[:, :], ysb[gp][:, :], sb[:, :], ALU.mult),
                            reads=[f"ysb{gp}", sbn, "ysq"], writes=["ysq"])
                        def yfin(en, c=c, h0=h0, sg_c=sg_c):
                            dstap = bass.AP(bigA, h0 * T + c * 128,
                                            [[NK * T, 128], [T, 4], [1, 128]])
                            return en.tensor_tensor(
                                dstap,
                                z_sb[:, :].rearrange("p (j t) -> p j t", j=4),
                                sg_c[:, h0 * 128:(h0 + 4) * 128].rearrange(
                                    "p (j t) -> p j t", j=4), ALU.mult)
                        p.emit("v", yfin, reads=["ysq", sgcn],
                               writes=["yf"])
                    yield

        def phase_qgate(xn_src, xnname):
            wstA = pv(0, 128, 4096)
            wstB = pv(4096, 128, 4096)
            sc1 = pv(8192, 128, 1024)
            sc2 = pv(9216, 128, 1024)
            sc3 = pv(10240, 128, 1024)
            for m in range(H):
                par = m % 2
                stripe = wstA if par == 0 else wstB
                wb = "wstA" if par == 0 else "wstB"
                dma("g", stripe[:, :], dWq[m, :, :], writes=[wb], sem="wsA" if par == 0 else "wsB")
                pjn = [PB[par * 2], PB[par * 2 + 1]]
                fns = []
                for k in range(NK):
                    for nh in range(2):
                        fns.append(lambda en, k=k, nh=nh, stripe=stripe, par=par:
                                   en.matmul(
                            pb[par * 2 + nh][:, :],
                            stripe[:, k * 128:(k + 1) * 128],
                            xn_src[:, k * T + nh * 512:k * T + (nh + 1) * 512],
                            start=(k == 0), stop=(k == NK - 1)))
                p.group("t", fns, reads=[wb, xnname], writes=pjn)
                p.emit("s", lambda en, par=par: en.copy(
                    sc1[:, 0:512], pb[par * 2][:, :]), reads=[pjn[0]],
                    writes=["prs1"])
                p.emit("s", lambda en, par=par: en.copy(
                    sc1[:, 512:1024], pb[par * 2 + 1][:, :]), reads=[pjn[1]],
                    writes=["prs1"])
                fns = [(lambda en, nh=nh: en.matmul(
                    pb[4 + nh][:, :], S_Tc[:, :],
                    sc1[:, nh * 512:(nh + 1) * 512], start=True, stop=True))
                    for nh in range(2)]
                p.group("t", fns, reads=["prs1", "S_T"], writes=["pb4", "pb5"])
                for nh in range(2):
                    p.emit("v", lambda en, nh=nh: en.tensor_tensor(
                        sc2[:, nh * 512:(nh + 1) * 512], pb[4 + nh][:, :],
                        sinO[:, nh * 512:(nh + 1) * 512], ALU.mult),
                        reads=[PB[4 + nh], "sinO"], writes=["prs2"])
                    p.emit("v", lambda en, nh=nh, par=par: en.tensor_tensor(
                        sc3[:, nh * 512:(nh + 1) * 512], pb[par * 2 + nh][:, :],
                        cosO[:, nh * 512:(nh + 1) * 512], ALU.mult),
                        reads=[pjn[nh], "cosO"], writes=["prs3"])
                p.emit("v", lambda en: en.tensor_tensor(
                    sc3[:, :], sc3[:, :], sc2[:, :], ALU.add),
                    reads=["prs3", "prs2"], writes=["prs3"])
                def spq(en, m=m):
                    return en.dma_start(out=bass.AP(
                        sqsp, m * 128,
                        [[H * 128, 128], [128 * H * 128, NCH], [1, 128]]),
                        in_=sc3[:, :].rearrange("p (c t) -> p c t", c=NCH))
                p.emit("q", spq, reads=["prs3"], writes=["qspd"], sem="qspd", inc=16)
                yield
            for m in range(H):
                par = m % 2
                stripe = wstA if par == 0 else wstB
                wb = "wstA" if par == 0 else "wstB"
                dma("g", stripe[:, :], dWg[m, :, :], writes=[wb], sem="wsA" if par == 0 else "wsB")
                pjn = [PB[par * 2], PB[par * 2 + 1]]
                fns = []
                for k in range(NK):
                    for nh in range(2):
                        fns.append(lambda en, k=k, nh=nh, stripe=stripe, par=par:
                                   en.matmul(
                            pb[par * 2 + nh][:, :],
                            stripe[:, k * 128:(k + 1) * 128],
                            xn_src[:, k * T + nh * 512:k * T + (nh + 1) * 512],
                            start=(k == 0), stop=(k == NK - 1)))
                p.group("t", fns, reads=[wb, xnname], writes=pjn)
                p.emit("s", lambda en, par=par: en.activation(
                    sc3[:, 0:512], pb[par * 2][:, :], AF.Silu),
                    reads=[pjn[0]], writes=["prs3"])
                p.emit("s", lambda en, par=par: en.activation(
                    sc3[:, 512:1024], pb[par * 2 + 1][:, :], AF.Silu),
                    reads=[pjn[1]], writes=["prs3"])
                def spg(en, m=m):
                    return en.dma_start(out=bass.AP(
                        ssgsp, m * 128,
                        [[H * 128, 128], [128 * H * 128, NCH], [1, 128]]),
                        in_=sc3[:, :].rearrange("p (c t) -> p c t", c=NCH))
                p.emit("q", spg, reads=["prs3"], writes=["sgspd"], sem="sgspd", inc=16)
                yield

        def phase_ffnpart(xn_src, xnname):
            stf = [pv(0, 128, 4096), pv(4096, 128, 4096)]
            xTm = pv(8192, 128, 1024, F32)
            t2 = pv(10240, 128, 512, F32)
            outst = pv(11264, 128, 512, F32)
            for m in range(NK):
                par = m % 2
                fb_ = "wstA" if par == 0 else "wstB"
                dma("g", stf[par][:, :], dWf[m, :, :], writes=[fb_],
                    sem="wsA" if par == 0 else "wsB")
                dma("q", xTm[:, :], dxT[m * 128:(m + 1) * 128, :],
                    writes=["o_xT"], sem="o_xT")
                pjf = [PB[par * 2], PB[par * 2 + 1]]
                fns = []
                for k in range(NK):
                    for nh in range(2):
                        fns.append(lambda en, k=k, nh=nh, par=par: en.matmul(
                            pb[par * 2 + nh][:, :],
                            stf[par][:, k * 128:(k + 1) * 128],
                            xn_src[:, k * T + nh * 512:k * T + (nh + 1) * 512],
                            start=(k == 0), stop=(k == NK - 1)))
                p.group("t", fns, reads=[fb_, xnname], writes=pjf)
                for nh in range(2):
                    p.emit("v", lambda en, m=m, nh=nh, par=par:
                           en.scalar_tensor_tensor(
                        t2, pb[par * 2 + nh][:, :], rowselc[:, m:m + 1],
                        qselB[:, nh * 512:(nh + 1) * 512],
                        ALU.mult, ALU.mult),
                        reads=[pjf[nh], "rowselc", "qselB"], writes=["o_t2"])
                    p.emit("v", lambda en, m=m, nh=nh: en.scalar_tensor_tensor(
                        t2, qselB[:, nh * 512:(nh + 1) * 512],
                        biasrs[:, m:m + 1], t2, ALU.mult, ALU.add),
                        reads=["qselB", "biasrs", "o_t2"], writes=["o_t2"])
                    p.emit("v", lambda en, nh=nh: en.tensor_tensor(
                        outst, t2, xTm[:, nh * 512:(nh + 1) * 512], ALU.add),
                        reads=["o_t2", "o_xT"], writes=["o_out"])
                    dma("q", spart.ap()[m * 128:(m + 1) * 128,
                                        nh * 512:(nh + 1) * 512],
                        outst, reads=["o_out"], writes=["partd"], sem="partd")
                yield

        def phase_wo():
            stw = [pv(0, 128, 4096), pv(4096, 128, 4096)]
            prt = pv(16384, 128, 1024, F32)
            t4 = pv(20480, 128, 512, F32)
            outst = pv(21504, 128, 512, F32)
            for m in range(NK):
                par = m % 2
                wb_ = "wstA" if par == 0 else "wstB"
                dma("g", stw[par][:, :], dWo[m, :, :], writes=[wb_], sem="wsA" if par == 0 else "wsB")
                dma("q", prt[:, :], spart.ap()[m * 128:(m + 1) * 128, :],
                    reads=["partd"], writes=["o_prt"])
                pjw = [PB[par * 2], PB[par * 2 + 1]]
                fns = []
                for k in range(NK):
                    for nh in range(2):
                        fns.append(lambda en, k=k, nh=nh, par=par: en.matmul(
                            pb[par * 2 + nh][:, :],
                            stw[par][:, k * 128:(k + 1) * 128],
                            bigA[:, k * T + nh * 512:k * T + (nh + 1) * 512],
                            start=(k == 0), stop=(k == NK - 1)))
                p.group("t", fns, reads=[wb_, "yf"], writes=pjw)
                for nh in range(2):
                    p.emit("v", lambda en, nh=nh, par=par: en.scalar_tensor_tensor(
                        t4, pb[par * 2 + nh][:, :], 1.0,
                        prt[:, nh * 512:(nh + 1) * 512], ALU.mult, ALU.add),
                        reads=[pjw[nh], "o_prt"], writes=["o_t4"])
                    p.emit("v", lambda en: en.tensor_copy(outst, t4),
                           reads=["o_t4"], writes=["o_out"])
                    dma("q", dout[m * 128:(m + 1) * 128, nh * 512:(nh + 1) * 512],
                        outst, reads=["o_out"], writes=["outd"])

        def phase_debug():
            if not DEBUG:
                return
            for n, t_ in [("dmax", dmax), ("selc", selc), ("rowselc", rowselc),
                          ("qselB", qselB), ("biasrs", biasrs), ("qrowf", qrowf),
                          ("states", states), ("dtsb", dtsb), ("decB", decB),
                          ("negG", negG), ("ekd", ekd)]:
                if t_.dtype if False else True:
                    pass
                def mk(en, n=n, t_=t_):
                    return en.dma_start(out=ddbg[n], in_=t_[:, :])
                p.emit("g", mk, reads=[n if n in p.bufs else "outd"],
                       extra_deps=[("v", p.cnt["v"]), ("s", p.cnt["s"]),
                                   ("t", p.cnt["t"])], sem="dbgd", inc=16)

        def run2(ga, gb):
            ga, gb = iter(ga), iter(gb)
            da = db = False
            while not (da and db):
                if not da:
                    try:
                        next(ga)
                    except StopIteration:
                        da = True
                if not db:
                    try:
                        next(gb)
                    except StopIteration:
                        db = True

        def drain(g):
            for _ in g:
                pass

        phase_ln(dxw, bigA, "xnA", "lw")
        p.fence()
        phase_kvdt(bigA[:, :], "xnA", cosW, "cosW", sinW, "sinW", "w",
                   sksp_w, "kspw", svsp_w, "vspw",
                   dtsbW, "dtsbW", dteW, "dteW", decBW, "decBW")
        p.fence()
        phase_ln(dx, bigA, "xnA", "lo")
        p.fence()
        phase_kvdt(bigA[:, :], "xnA", cosO, "cosO", sinO, "sinO", "o",
                   sksp, "ksp", svsp, "vsp",
                   dtsb, "dtsb", dte, "dte", decB, "decB")
        p.fence()
        drain(phase_attn(False, "wa", sksp_w, svsp_w, "kspw", "vspw",
                         [0, 1], [0, 1], [0, 1], 2,
                         dtsbW, "dtsbW", dteW, "dteW", decBW, "decBW"))
        p.fence()
        drain(phase_qgate(bigA[:, :], "xnA"))
        p.fence()
        drain(phase_p1())
        p.fence()
        drain(phase_ffnpart(bigA[:, :], "xnA"))
        p.fence()
        drain(phase_attn(True, "oa", sksp, svsp, "ksp", "vsp",
                         [0, 1], [2, 3], [4, 5], 6,
                         dtsb, "dtsb", dte, "dte", decB, "decB"))
        p.fence()
        phase_wo()
        phase_debug()
        with nc.Block() as block:
            p.replay(nc, es, block)

    return nc


_CACHE = {}


def _blk(w, n_m):
    return np.ascontiguousarray(
        w.reshape(n_m, 128, NK, 128).transpose(0, 3, 2, 1).reshape(n_m, 128, HID))


def _rope_tables(pos):
    inv = 1.0 / ROPE_THETA ** (np.arange(0, D, 2, dtype=np.float64) / D)
    ang = np.concatenate([inv, inv])[:, None] * pos[None, :].astype(np.float64)
    return (np.ascontiguousarray(np.cos(ang).astype(np.float32)),
            np.ascontiguousarray(np.sin(ang).astype(np.float32)))


def kernel(L0_attn_hidden_states, L0_attn_Wq, L0_attn_Wk, L0_attn_Wv,
           L0_attn_Wo, L0_attn_Wg, L0_attn_Win, L0_attn_dt_bias,
           L0_attn_A_log_bias, L0_attn_gnorm_w, L0_ffn_weight, L0_ffn_bias,
           L0_ffn_centroids, L0_ffn_assignments):
    x = np.ascontiguousarray(np.asarray(L0_attn_hidden_states, dtype=np.float32))
    Wq = np.asarray(L0_attn_Wq, np.float32)
    Wk = np.asarray(L0_attn_Wk, np.float32)
    Wv = np.asarray(L0_attn_Wv, np.float32)
    Wo = np.asarray(L0_attn_Wo, np.float32)
    Wg = np.asarray(L0_attn_Wg, np.float32)
    Win = np.asarray(L0_attn_Win, np.float32)
    dtb = np.asarray(L0_attn_dt_bias, np.float32)
    Alog = np.asarray(L0_attn_A_log_bias, np.float32)
    gnw = np.asarray(L0_attn_gnorm_w, np.float32)
    Wf = np.asarray(L0_ffn_weight, np.float32)
    fb = np.asarray(L0_ffn_bias, np.float32)
    cent = np.asarray(L0_ffn_centroids, np.float32)
    assign = np.asarray(L0_ffn_assignments, np.int32)

    if "nc" not in _CACHE:
        _CACHE["nc"] = build_program()
    nc = _CACHE["nc"]

    xTall = np.ascontiguousarray(x.reshape(ALLT, HID).T)
    Wq_b = _blk(Wq, 32)
    Wk_b = _blk(Wk, 8)
    Wv_b = _blk(Wv, 8)
    Wg_b = _blk(Wg, 32)
    Wo_b = _blk(Wo, 32)
    Wf_b = _blk(Wf, 32)
    Win_t = np.ascontiguousarray(Win.T)
    CWO = np.ascontiguousarray(
        np.concatenate([cent.T, np.ones((HID, 1), np.float32)], axis=1))
    fb_c = np.ascontiguousarray(fb.reshape(32, 128).T)
    ident = np.eye(128, dtype=np.float32)
    S = np.zeros((128, 128), np.float32)
    S[np.arange(64), np.arange(64) + 64] = -1.0
    S[np.arange(64) + 64, np.arange(64)] = 1.0
    S_T = np.ascontiguousarray(S.T)
    mask1 = np.where(np.arange(128)[None, :] >= np.arange(128)[:, None],
                     0.0, NEG_BIG).astype(np.float32)
    mask4 = np.ascontiguousarray(np.tile(mask1, (1, 4)))
    iota = np.arange(64, dtype=np.float32).reshape(64, 1)

    in_maps = []
    for c in range(NCORES):
        b, s = divmod(c, 4)
        t0 = s * T
        cos_o, sin_o = _rope_tables(np.arange(t0, t0 + T))
        w0 = max(s - 1, 0) * T
        cos_w, sin_w = _rope_tables(np.arange(w0, w0 + T))
        xw = (np.zeros((T, HID), np.float32) if s == 0
              else np.ascontiguousarray(x[b, t0 - T:t0, :]))
        in_maps.append({
            "x_own": np.ascontiguousarray(x[b, t0:t0 + T, :]),
            "x_win": xw,
            "xT_own": np.ascontiguousarray(x[b, t0:t0 + T, :].T),
            "xT_all": xTall,
            "Wq_blk": Wq_b, "Wk_blk": Wk_b, "Wv_blk": Wv_b,
            "Wg_blk": Wg_b, "Wo_blk": Wo_b, "Wf_blk": Wf_b,
            "Win_t": Win_t, "CWO": CWO,
            "ffn_b": fb_c,
            "dt_b": dtb.reshape(32, 1),
            "A_log": Alog.reshape(32, 1),
            "gnorm": gnw.reshape(128, 1),
            "assign": assign.reshape(1, HID),
            "ident": ident, "S_T": S_T, "maskneg4": mask4,
            "cos_own": cos_o, "sin_own": sin_o,
            "cos_win": cos_w, "sin_win": sin_w,
            "iota64_in": iota,
            "own_off": np.array([[b * 32 + s * 8]], np.int32),
        })

    res = run_bass_kernel_spmd(nc, in_maps, core_ids=list(range(NCORES)))
    out = np.empty((B, L, HID), np.float32)
    for c in range(NCORES):
        b, s = divmod(c, 4)
        out[b, s * T:(s + 1) * T, :] = res.results[c]["out"].T
    return out


# version 09
# speedup vs baseline: 1.4640x; 1.0365x over previous
import numpy as np
import concourse.bass as bass
import concourse.mybir as mybir
from concourse.bass_utils import run_bass_kernel_spmd

F32 = mybir.dt.float32
BF16 = mybir.dt.bfloat16
I32 = mybir.dt.int32
AF = mybir.ActivationFunctionType
ALU = mybir.AluOpType
AX = mybir.AxisListType

B, L, HID = 2, 4096, 4096
NCORES = 8
T = 1024
NT = 8
NCH = 8
NK = 32
H, KVH, D = 32, 8, 128
ROPE_THETA = 10000.0
RMS_EPS = 1e-5
LN_EPS = 1e-5
FFN_T = 0.1
FFN_TH = 0.01
ALLT = B * L
NEG_BIG = -1.0e30
DEBUG = False


class Prog:
    ENGINES = ("t", "v", "s", "g", "q")
    WAIT_ALL = ("CONSTg", "CONSTq")

    def __init__(self):
        self.ops = {e: [] for e in self.ENGINES}
        self.cnt = {e: 0 for e in self.ENGINES}
        self.bufs = {}

    def _bump(self, key, inc):
        self.cnt[key] = self.cnt.get(key, 0) + inc

    def _deps(self, reads, writes, extra):
        deps = []
        for b in reads:
            st = self.bufs.get(b)
            if st is not None and st[0] is not None:
                deps.append(st[0])
        for b in writes:
            st = self.bufs.get(b)
            if st is not None:
                if st[0] is not None:
                    deps.append(st[0])
                deps.extend(st[1])
        for d in extra:
            if d is not None:
                deps.append(d)
        best = {}
        for (e, v) in deps:
            if best.get(e, -1) < v:
                best[e] = v
        return list(best.items())

    def emit(self, eng, fn, reads=(), writes=(), extra_deps=(), inc=None,
             sem=None):
        key = sem if sem is not None else eng
        if inc is None:
            inc = 16 if key not in self.ENGINES else (
                0 if eng in ("g", "q") else 1)
            if eng not in ("g", "q") and key in self.ENGINES:
                inc = 1
        deps = self._deps(reads, writes, extra_deps)
        self._bump(key, inc)
        tok = (key, self.cnt[key])
        self.ops[eng].append((fn, deps, inc, key))
        for b in writes:
            self.bufs[b] = [tok, []]
        for b in reads:
            self.bufs.setdefault(b, [None, []])[1].append(tok)
        return tok

    def group(self, eng, fns, reads=(), writes=(), extra_deps=()):
        deps = self._deps(reads, writes, extra_deps)
        inc = 1
        for i, fn in enumerate(fns):
            self.ops[eng].append((fn, deps if i == 0 else [],
                                  inc if i == len(fns) - 1 else 0, eng))
        self._bump(eng, inc)
        tok = (eng, self.cnt[eng])
        for b in writes:
            self.bufs[b] = [tok, []]
        for b in reads:
            self.bufs.setdefault(b, [None, []])[1].append(tok)
        return tok

    def fence(self):
        snap = dict(self.cnt)
        for e in self.ENGINES:
            deps = [(k, v) for k, v in snap.items() if k != e and v > 0]
            self.emit(e, lambda en: en.nop(), extra_deps=deps, inc=0)

    def replay(self, nc, es, block):
        sems = {}
        for k in self.cnt:
            sems[k] = es.enter_context(nc.semaphore(f"sm_{k}"))
        totals = dict(self.cnt)

        def body(eng_name):
            def run(en):
                waited = {}
                for (fn, deps, inc, key) in self.ops[eng_name]:
                    for (dk, dv) in deps:
                        if dk == eng_name and eng_name == "t":
                            continue
                        if dk in self.WAIT_ALL:
                            dv = totals[dk]
                        if waited.get(dk, 0) >= dv:
                            continue
                        en.wait_ge(sems[dk], dv)
                        waited[dk] = dv
                    inst = fn(en)
                    if inc > 0:
                        inst.then_inc(sems[key], inc)
                if eng_name == "q":
                    for k, tv in totals.items():
                        if k not in self.ENGINES and tv > 0 and \
                                waited.get(k, 0) < tv:
                            en.wait_ge(sems[k], tv)
                elif totals.get(eng_name, 0) > 0:
                    en.wait_ge(sems[eng_name], totals[eng_name])
            return run

        block.tensor(body("t"))
        block.vector(body("v"))
        block.scalar(body("s"))
        block.gpsimd(body("g"))
        block.sync(body("q"))


PB = [f"pb{i}" for i in range(8)]


def build_program():
    nc = bass.Bass("TRN2", target_bir_lowering=False)

    dx = nc.dram_tensor("x_own", [T, HID], F32, kind="ExternalInput").ap()
    dxw = nc.dram_tensor("x_win", [T, HID], F32, kind="ExternalInput").ap()
    dxT = nc.dram_tensor("xT_own", [HID, T], F32, kind="ExternalInput").ap()
    dxTall = nc.dram_tensor("xT_all", [HID, ALLT], F32, kind="ExternalInput").ap()
    dWq = nc.dram_tensor("Wq_blk", [32, 128, HID], F32, kind="ExternalInput").ap()
    dWk = nc.dram_tensor("Wk_blk", [8, 128, HID], F32, kind="ExternalInput").ap()
    dWv = nc.dram_tensor("Wv_blk", [8, 128, HID], F32, kind="ExternalInput").ap()
    dWg = nc.dram_tensor("Wg_blk", [32, 128, HID], F32, kind="ExternalInput").ap()
    dWo = nc.dram_tensor("Wo_blk", [32, 128, HID], F32, kind="ExternalInput").ap()
    dWf = nc.dram_tensor("Wf_blk", [32, 128, HID], F32, kind="ExternalInput").ap()
    dWin = nc.dram_tensor("Win_t", [HID, 32], F32, kind="ExternalInput").ap()
    dCWO = nc.dram_tensor("CWO", [HID, 65], F32, kind="ExternalInput").ap()
    dfb = nc.dram_tensor("ffn_b", [128, 32], F32, kind="ExternalInput").ap()
    ddtb = nc.dram_tensor("dt_b", [32, 1], F32, kind="ExternalInput").ap()
    dAlog = nc.dram_tensor("A_log", [32, 1], F32, kind="ExternalInput").ap()
    dgn = nc.dram_tensor("gnorm", [128, 1], F32, kind="ExternalInput").ap()
    dassign = nc.dram_tensor("assign", [1, HID], I32, kind="ExternalInput").ap()
    dI = nc.dram_tensor("ident", [128, 128], F32, kind="ExternalInput").ap()
    dS_T = nc.dram_tensor("S_T", [128, 128], F32, kind="ExternalInput").ap()
    dmask = nc.dram_tensor("maskneg4", [128, 512], F32, kind="ExternalInput").ap()
    dcos_o = nc.dram_tensor("cos_own", [128, T], F32, kind="ExternalInput").ap()
    dsin_o = nc.dram_tensor("sin_own", [128, T], F32, kind="ExternalInput").ap()
    dcos_w = nc.dram_tensor("cos_win", [128, T], F32, kind="ExternalInput").ap()
    dsin_w = nc.dram_tensor("sin_win", [128, T], F32, kind="ExternalInput").ap()
    diota = nc.dram_tensor("iota64_in", [64, 1], F32, kind="ExternalInput").ap()
    doff = nc.dram_tensor("own_off", [1, 1], I32, kind="ExternalInput").ap()

    dout = nc.dram_tensor("out", [HID, T], F32, kind="ExternalOutput").ap()
    if DEBUG:
        ddbg = {n: nc.dram_tensor(f"dbg_{n}", sh, F32, kind="ExternalOutput").ap()
                for n, sh in [("dmax", [64, 1]), ("selc", [64, 1]),
                              ("rowselc", [128, 32]), ("qselB", [128, T]),
                              ("biasrs", [128, 32]), ("qrowf", [1, T]),
                              ("states", [128, H * 128]),
                              ("dtsb", [128, 256]), ("decB", [128, 256]),
                              ("negG", [128, 256]), ("ekd", [128, 256])]}

    sGd = nc.dram_tensor("Gdram", [32, T], F32)
    sEGb = nc.dram_tensor("EGbf", [32, T], BF16)
    sdec = nc.dram_tensor("decdram", [32, 8], BF16)
    sqsp = nc.dram_tensor("qspill", [NCH, 128, H, 128], BF16)
    ssgsp = nc.dram_tensor("sgspill", [NCH, 128, H, 128], BF16)
    sksp = nc.dram_tensor("kspill", [NCH, 128, KVH, 128], BF16)
    svsp = nc.dram_tensor("vspill", [NCH, 128, KVH, 128], BF16)
    sksp_w = nc.dram_tensor("kspill_w", [NCH, 128, KVH, 128], BF16)
    svsp_w = nc.dram_tensor("vspill_w", [NCH, 128, KVH, 128], BF16)
    spart = nc.dram_tensor("spartial", [HID, T], F32)
    sqsel = nc.dram_tensor("qseld", [128, 64], BF16)
    srowd = nc.dram_tensor("rowseld", [1, HID], F32)

    p = Prog()

    from contextlib import ExitStack
    es = ExitStack()
    SBt = lambda name, shape, dt: es.enter_context(nc.sbuf_tensor(name, shape, dt))
    PSt = lambda name, shape, dt: es.enter_context(nc.psum_tensor(name, shape, dt))

    with es:
        bigA = SBt("bigA", [128, NK * T], BF16)
        states = SBt("states", [128, H * 128], F32)
        dtsb = SBt("dtsb", [128, 256], F32)
        ekd = SBt("ekd", [128, 256], F32)
        dte = SBt("dte", [128, 256], F32)
        decB = SBt("decB", [128, 256], F32)
        dtsbW = SBt("dtsbW", [128, 256], F32)
        dteW = SBt("dteW", [128, 256], F32)
        decBW = SBt("decBW", [128, 256], F32)
        negG = SBt("negG", [128, 256], F32)
        wint = SBt("wint", [128, NK * 32], BF16)
        cwot = SBt("cwot", [128, NK * 65], BF16)
        I_bf = SBt("I_bf", [128, 128], BF16)
        I_f = SBt("I_f", [128, 128], F32)
        S_Tc = SBt("S_Tc", [128, 128], BF16)
        maskc = SBt("maskc", [128, 512], F32)
        cosO = SBt("cosO", [128, T], BF16)
        sinO = SBt("sinO", [128, T], BF16)
        cosW = SBt("cosW", [128, T], BF16)
        sinW = SBt("sinW", [128, T], BF16)
        onesr = SBt("onesr", [1, 128], BF16)
        onesc = SBt("onesc", [128, 1], BF16)
        c_eps = SBt("c_eps", [128, 1], F32)
        c_one = SBt("c_one", [128, 1], F32)
        iota64 = SBt("iota64", [64, 1], F32)
        alog = SBt("alog", [32, 1], F32)
        expA = SBt("expA", [32, 1], F32)
        dtbias = SBt("dtbias", [32, 1], F32)
        gnormc = SBt("gnormc", [128, 1], F32)
        ffnb = SBt("ffnb", [128, 32], F32)
        zeros32 = SBt("zeros32", [32, 128], F32)
        qselB = SBt("qselB", [128, T], BF16)
        rowselc = SBt("rowselc", [128, 32], F32)
        biasrs = SBt("biasrs", [128, 32], F32)
        dmax = SBt("dmax", [64, 1], F32)
        tmax = SBt("tmax", [64, 1], F32)
        selc = SBt("selc", [64, 1], F32)
        selbf = SBt("selbf", [64, 1], BF16)
        qrowf = SBt("qrowf", [1, T], BF16)
        Scol = SBt("Scol", [128, 1], F32)
        Scbf = SBt("Scbf", [1, 65], BF16)
        ScB = SBt("ScB", [128, 64], F32)
        offsb = SBt("offsb", [1, 1], I32)
        POOLN = 23808
        pool = SBt("pool", [128, POOLN], BF16)
        qcb = [SBt(f"qcb{i}", [128, H * 128], BF16) for i in range(2)]
        sgcb = [SBt(f"sgcb{i}", [128, H * 128], BF16) for i in range(2)]
        kcb = [SBt(f"kcb{i}", [128, KVH * 128], BF16) for i in range(2)]
        vcb = [SBt(f"vcb{i}", [128, KVH * 128], BF16) for i in range(2)]
        pb = [PSt(f"psb{i}", [128, 512], F32) for i in range(8)]

        roff = es.enter_context(nc.sync.register("roff"))

        def pv(off, part, free, dt=BF16):
            if dt == BF16:
                return pool[0:part, off:off + free]
            return pool[:, off:off + 2 * free].bitcast(dt)[0:part, 0:free]

        def dma(qe, out_ap, in_ap, reads=(), writes=(), deps=(), slow=False,
                sem=None):
            def fn(en, out_ap=out_ap, in_ap=in_ap, slow=slow):
                return en.dma_start(out=out_ap, in_=in_ap,
                                    allow_slow_non_contiguous=slow)
            if sem is None:
                sem = writes[0] if writes else "misc_dma"
            return p.emit(qe, fn, reads=reads, writes=writes, extra_deps=deps,
                          sem=sem, inc=16)
        
        dma("g", I_bf[:, :], dI, writes=["I_bf"], sem="CONSTg")
        dma("q", I_f[:, :], dI, writes=["I_f"], sem="CONSTq")
        dma("g", S_Tc[:, :], dS_T, writes=["S_T"], sem="CONSTg")
        dma("q", maskc[:, :], dmask, writes=["maskc"], sem="CONSTq")
        dma("g", cosO[:, :], dcos_o, writes=["cosO"], sem="CONSTg")
        dma("g", sinO[:, :], dsin_o, writes=["sinO"], sem="CONSTg")
        dma("g", cosW[:, :], dcos_w, writes=["cosW"], sem="CONSTg")
        dma("g", sinW[:, :], dsin_w, writes=["sinW"], sem="CONSTg")
        dma("q", iota64[:, :], diota, writes=["iota64"], sem="CONSTq")
        dma("q", alog[:, :], dAlog, writes=["alog"], sem="CONSTq")
        dma("q", dtbias[:, :], ddtb, writes=["dtbias"], sem="CONSTq")
        dma("q", gnormc[:, :], dgn, writes=["gnormc"], sem="CONSTq")
        dma("q", ffnb[:, :], dfb, writes=["ffnb"], sem="CONSTq")
        dma("q", offsb[:, :], doff, writes=["offsb"], sem="CONSTq")
        dma("g", wint[:, :].rearrange("p (k h) -> p k h", k=NK),
            dWin.rearrange("(k p) h -> p k h", p=128), writes=["wint"], sem="CONSTg")
        dma("g", cwot[:, :].rearrange("p (k c) -> p k c", k=NK),
            dCWO.rearrange("(k p) c -> p k c", p=128), writes=["cwot"], sem="CONSTg")
        p.emit("v", lambda en: en.memset(onesr[:, :], 1.0), writes=["onesr"])
        p.emit("v", lambda en: en.memset(c_eps[:, :], LN_EPS), writes=["c_eps"])
        p.emit("v", lambda en: en.memset(c_one[:, :], 1.0), writes=["c_one"])
        p.emit("v", lambda en: en.memset(onesc[:, :], 1.0), writes=["onesc"])
        p.emit("v", lambda en: en.memset(zeros32[:, :], 0.0), writes=["zeros32"])
        p.emit("v", lambda en: en.memset(states[:, :], 0.0), writes=["states"])
        p.emit("v", lambda en: en.memset(dmax[:, :], NEG_BIG), writes=["dmax"])
        p.emit("s", lambda en: en.activation(expA[:, :], alog[:, :], AF.Exp),
               reads=["alog"], writes=["expA"])
        p.emit("q", lambda en: en.reg_load(roff, offsb[0:1, 0:1]),
               reads=["offsb"], writes=["roff"], inc=0)

        def phase_p1():
            xt = [pv(0, 128, 2048), pv(2048, 128, 2048), pv(4096, 128, 2048)]
            xq = [pv(6144, 128, 2048), pv(8192, 128, 2048)]
            dots = pv(10240, 65, 512, F32)
            dotsB = pv(11264, 65, 512, F32)
            a_sb = pv(12288, 128, 64, F32)
            e_sb = pv(12416, 128, 64, F32)
            col = lambda i: pv(12544 + i * 2, 128, 1, F32)
            mcol, m2c, varc, rstdc, zmaxc, nzmaxc, Zc, rZc, qmx = (
                col(i) for i in range(9))
            dots_bf = pv(12608, 128, 64)
            qse = pv(12672, 128, 64)
            Mblk = pv(12736, 64, 512)
            ai_blk = pv(13248, 1, 512, I32)
            ab_blk = pv(14272, 1, 512)
            rowblk = pv(14784, 1, 512, F32)

            fns = [(lambda en, k=k: en.matmul(
                pb[7][0:65, 0:1], cwot[:, k * 65:(k + 1) * 65], onesc[:, :],
                start=(k == 0), stop=(k == NK - 1))) for k in range(NK)]
            p.group("t", fns, reads=["cwot", "onesc"], writes=["pb7"])
            p.emit("s", lambda en: en.copy(Scol[0:65, 0:1], pb[7][0:65, 0:1]),
                   reads=["pb7"], writes=["Scol"])
            p.emit("t", lambda en: en.transpose(
                pb[6][0:1, 0:65], Scol[0:65, 0:1], I_f[0:65, 0:65]),
                reads=["Scol", "I_f"], writes=["pb6"])
            p.emit("s", lambda en: en.copy(Scbf[0:1, 0:65], pb[6][0:1, 0:65]),
                   reads=["pb6"], writes=["Scbf"])
            p.emit("t", lambda en: en.matmul(
                pb[6][:, 0:64], onesr[0:1, :], Scbf[0:1, 0:64],
                start=True, stop=True),
                reads=["onesr", "Scbf"], writes=["pb6"])
            p.emit("s", lambda en: en.copy(ScB[:, :], pb[6][:, 0:64]),
                   reads=["pb6"], writes=["ScB"])
            yield

            for tg in range(4):
                for k in range(NK):
                    par = k % 3
                    pq = k % 2
                    bx, bq = f"p1x{par}", f"p1q{pq}"
                    dma("g", xt[par][:, :],
                        dxTall[k * 128:(k + 1) * 128, tg * 2048:(tg + 1) * 2048],
                        writes=[bx])
                    p.emit("v", lambda en, par=par, pq=pq: en.tensor_mul(
                        xq[pq][:, :], xt[par][:, :], xt[par][:, :]),
                        reads=[bx], writes=[bq])
                    fns = []
                    for n in range(4):
                        fns.append(lambda en, k=k, n=n, par=par: en.matmul(
                            pb[n][0:65, :], cwot[:, k * 65:(k + 1) * 65],
                            xt[par][:, n * 512:(n + 1) * 512],
                            start=(k == 0), stop=(k == NK - 1)))
                    for n in range(4):
                        fns.append(lambda en, k=k, n=n, pq=pq: en.matmul(
                            pb[4 + n][0:65, :], cwot[:, k * 65:(k + 1) * 65],
                            xq[pq][:, n * 512:(n + 1) * 512],
                            start=(k == 0), stop=(k == NK - 1)))
                    p.group("t", fns, reads=[bx, bq, "cwot"], writes=list(PB))
                    if k % 8 == 7:
                        yield
                for n in range(4):
                    p.emit("s", lambda en, n=n: en.copy(dots[:, :], pb[n][0:65, :]),
                           reads=[PB[n]], writes=["p1d"])
                    p.emit("s", lambda en, n=n: en.copy(dotsB[:, :],
                                                        pb[4 + n][0:65, :]),
                           reads=[PB[4 + n]], writes=["p1dB"])
                    for j in range(4):
                        tile = tg * 16 + n * 4 + j
                        p.emit("t", lambda en, j=j, n=n: en.transpose(
                            pb[n][:, 0:65], dots[:, j * 128:(j + 1) * 128],
                            I_f[0:65, 0:65]),
                            reads=["p1d", "I_f"], writes=[PB[n]])
                        p.emit("t", lambda en, j=j, n=n: en.transpose(
                            pb[4 + n][:, 0:65], dotsB[:, j * 128:(j + 1) * 128],
                            I_f[0:65, 0:65]),
                            reads=["p1dB", "I_f"], writes=[PB[4 + n]])
                        p.emit("v", lambda en, n=n: en.tensor_scalar(
                            mcol, pb[n][:, 64:65], 1.0 / HID, None, ALU.mult),
                            reads=[PB[n]], writes=["p1m"])
                        p.emit("v", lambda en: en.tensor_tensor(
                            m2c, mcol, mcol, ALU.mult),
                            reads=["p1m"], writes=["p1m2"])
                        p.emit("v", lambda en, n=n: en.scalar_tensor_tensor(
                            varc, pb[4 + n][:, 64:65], 1.0 / HID, m2c,
                            ALU.mult, ALU.subtract),
                            reads=[PB[4 + n], "p1m2"], writes=["p1var"])
                        p.emit("s", lambda en: en.activation(
                            rstdc, varc, AF.Ln, bias=c_eps[:, :]),
                            reads=["p1var", "c_eps"], writes=["p1rstd"])
                        p.emit("s", lambda en: en.activation(
                            rstdc, rstdc, AF.Exp, scale=-0.5),
                            reads=["p1rstd"], writes=["p1rstd"])
                        p.emit("v", lambda en: en.tensor_scalar(
                            m2c, mcol, -1.0, None, ALU.mult),
                            reads=["p1m"], writes=["p1m2"])
                        p.emit("v", lambda en, n=n: en.scalar_tensor_tensor(
                            a_sb, ScB[:, :], m2c, pb[n][:, 0:64],
                            ALU.mult, ALU.add),
                            reads=["ScB", "p1m2", PB[n]], writes=["p1a"])
                        p.emit("v", lambda en: en.tensor_scalar(
                            a_sb, a_sb, rstdc, 1.0 / FFN_T, ALU.mult, ALU.mult),
                            reads=["p1a", "p1rstd"], writes=["p1a"])
                        p.emit("v", lambda en: en.reduce_max(zmaxc, a_sb, axis=AX.X),
                               reads=["p1a"], writes=["p1zm"])
                        p.emit("v", lambda en: en.tensor_scalar(
                            nzmaxc, zmaxc, -1.0, None, ALU.mult),
                            reads=["p1zm"], writes=["p1nzm"])
                        p.emit("s", lambda en: en.activation(
                            e_sb, a_sb, AF.Exp, bias=nzmaxc),
                            reads=["p1a", "p1nzm"], writes=["p1e"])
                        p.emit("v", lambda en: en.reduce_sum(Zc, e_sb, axis=AX.X),
                               reads=["p1e"], writes=["p1Z"])
                        p.emit("v", lambda en: en.reciprocal(rZc, Zc),
                               reads=["p1Z"], writes=["p1rZ"])
                        p.emit("v", lambda en: en.tensor_scalar(
                            dots_bf, e_sb, rZc, None, ALU.mult),
                            reads=["p1e", "p1rZ"], writes=["p1dbf"])
                        p.emit("v", lambda en, tile=tile: en.tensor_scalar(
                            qse[:, tile:tile + 1], rZc, FFN_TH, None, ALU.is_gt),
                            reads=["p1rZ"], writes=["p1qse"])
                        p.emit("t", lambda en, n=n: en.transpose(
                            pb[4 + n][:, :].bitcast(BF16)[0:64, 0:128],
                            dots_bf, I_bf[:, :]),
                            reads=["p1dbf", "I_bf"], writes=[PB[4 + n]])
                        p.emit("v", lambda en, n=n: en.reduce_max(
                            tmax[:, :], pb[4 + n][:, :].bitcast(BF16)[0:64, 0:128],
                            axis=AX.X),
                            reads=[PB[4 + n]], writes=["tmax"])
                        p.emit("v", lambda en: en.tensor_tensor(
                            dmax[:, :], dmax[:, :], tmax[:, :], ALU.max),
                            reads=["tmax", "dmax"], writes=["dmax"])
                        yield

            dma("g", sqsel.ap(), qse, reads=["p1qse"], writes=["qseld"])
            p.emit("v", lambda en: en.tensor_scalar(
                selc[:, :], dmax[:, :], FFN_TH, None, ALU.is_gt),
                reads=["dmax"], writes=["selc"])
            p.emit("v", lambda en: en.tensor_copy(selbf[:, :], selc[:, :]),
                   reads=["selc"], writes=["selbf"])
            for och in range(8):
                dma("q", ai_blk, dassign[0:1, och * 512:(och + 1) * 512],
                    writes=["p1ai"])
                p.emit("v", lambda en: en.tensor_copy(ab_blk, ai_blk),
                       reads=["p1ai"], writes=["p1ab"])
                p.emit("t", lambda en: en.matmul(
                    pb[6][0:64, :], onesr[0:1, 0:64], ab_blk,
                    start=True, stop=True),
                    reads=["onesr", "p1ab"], writes=["pb6"])
                p.emit("v", lambda en: en.tensor_scalar(
                    Mblk, pb[6][0:64, :], iota64[:, :], None, ALU.is_equal),
                    reads=["pb6", "iota64"], writes=["p1M"])
                p.emit("t", lambda en: en.matmul(
                    pb[7][0:1, :], selbf[:, :], Mblk, start=True, stop=True),
                    reads=["selbf", "p1M"], writes=["pb7"])
                p.emit("v", lambda en: en.tensor_scalar(
                    rowblk, pb[7][0:1, :], 0.5, None, ALU.is_gt),
                    reads=["pb7"], writes=["p1row"])
                dma("q", srowd.ap()[0:1, och * 512:(och + 1) * 512], rowblk,
                    reads=["p1row"], writes=["rowd"])
                yield
            dma("g", rowselc[:, :],
                bass.AP(srowd, 0, [[1, 128], [128, 32]]),
                reads=["rowd"], writes=["rowselc"], slow=True)
            p.emit("v", lambda en: en.tensor_tensor(
                biasrs[:, :], ffnb[:, :], rowselc[:, :], ALU.mult),
                reads=["ffnb", "rowselc"], writes=["biasrs"])

            def rd_qrow(en):
                return en.dma_start(
                    out=qrowf[0:1, :].rearrange("o (tt pp) -> o tt pp", tt=8),
                    in_=bass.AP(sqsel, roff, [[0, 1], [1, 8], [64, 128]]),
                    allow_slow_non_contiguous=True)
            p.emit("q", rd_qrow, reads=["qseld", "roff"], writes=["qrowf"],
                   sem="qrowf", inc=16)
            p.emit("t", lambda en: en.matmul(
                pb[6][:, :], onesr[0:1, :], qrowf[0:1, 0:512],
                start=True, stop=True),
                reads=["onesr", "qrowf"], writes=["pb6"])
            p.emit("t", lambda en: en.matmul(
                pb[7][:, :], onesr[0:1, :], qrowf[0:1, 512:1024],
                start=True, stop=True),
                reads=["onesr", "qrowf"], writes=["pb7"])
            p.emit("s", lambda en: en.copy(qselB[:, 0:512], pb[6][:, :]),
                   reads=["pb6"], writes=["qselB"])
            p.emit("s", lambda en: en.copy(qselB[:, 512:1024], pb[7][:, :]),
                   reads=["pb7"], writes=["qselB"])
            yield

        def phase_ln(xdram, dst_h, dstname, tag):
            xt = pv(0, 128, 4096)
            xnt = pv(4096, 128, 4096)
            xsq = pv(8192, 128, 4096)
            col = lambda i: pv(12544 + i * 2, 128, 1, F32)
            scol, mcol, sscol, varc, rstdc, nmr = (col(i) for i in range(6))
            for tt in range(NT):
                dma("g", xt[:, :], xdram[tt * 128:(tt + 1) * 128, :],
                    writes=[f"{tag}x"])
                p.emit("v", lambda en: en.reduce_sum(scol, xt[:, :], axis=AX.X),
                       reads=[f"{tag}x"], writes=[f"{tag}s"])
                p.emit("v", lambda en: en.tensor_scalar(
                    mcol, scol, 1.0 / HID, None, ALU.mult),
                    reads=[f"{tag}s"], writes=[f"{tag}m"])
                p.emit("s", lambda en: en.activation(xsq[:, :], xt[:, :], AF.Square),
                       reads=[f"{tag}x"], writes=[f"{tag}xsq"])
                p.emit("v", lambda en: en.reduce_sum(sscol, xsq[:, :], axis=AX.X),
                       reads=[f"{tag}xsq"], writes=[f"{tag}ss"])
                p.emit("v", lambda en: en.tensor_tensor(varc, mcol, mcol, ALU.mult),
                       reads=[f"{tag}m"], writes=[f"{tag}var"])
                p.emit("v", lambda en: en.scalar_tensor_tensor(
                    varc, sscol, 1.0 / HID, varc, ALU.mult, ALU.subtract),
                    reads=[f"{tag}ss", f"{tag}var"], writes=[f"{tag}var"])
                p.emit("s", lambda en: en.activation(rstdc, varc, AF.Ln, bias=c_eps[:, :]),
                       reads=[f"{tag}var", "c_eps"], writes=[f"{tag}r"])
                p.emit("s", lambda en: en.activation(rstdc, rstdc, AF.Exp, scale=-0.5),
                       reads=[f"{tag}r"], writes=[f"{tag}r"])
                p.emit("v", lambda en: en.scalar_tensor_tensor(
                    nmr, mcol, -1.0, rstdc, ALU.mult, ALU.mult),
                    reads=[f"{tag}m", f"{tag}r"], writes=[f"{tag}nmr"])
                p.emit("v", lambda en: en.tensor_scalar(
                    xnt[:, :], xt[:, :], rstdc, nmr, ALU.mult, ALU.add),
                    reads=[f"{tag}x", f"{tag}r", f"{tag}nmr"], writes=[f"{tag}xn"])
                for kq in range(0, NK, 8):
                    par = (kq // 8) % 2
                    bank = pb[6 + par]
                    pbf = bank[:, :].bitcast(BF16)
                    fns = []
                    for j in range(8):
                        fns.append(lambda en, k=kq + j, j=j, pbf=pbf: en.transpose(
                            pbf[:, j * 128:(j + 1) * 128],
                            xnt[:, k * 128:(k + 1) * 128], I_bf[:, :]))
                    p.group("t", fns, reads=[f"{tag}xn", "I_bf"],
                            writes=[PB[6 + par]])
                    def evac(en, kq=kq, tt=tt, pbf=pbf):
                        dstap = bass.AP(dst_h, kq * T + tt * 128,
                                        [[NK * T, 128], [T, 8], [1, 128]])
                        return en.tensor_copy(
                            dstap, pbf[:, :].rearrange("p (j t) -> p j t", j=8))
                    p.emit("v", evac, reads=[PB[6 + par]], writes=[dstname])

        def phase_kvdt(xn_src, xnname, cosT, cname, sinT, sname, tag,
                       kdst, kdstn, vdst, vdstn, dtsbX, dtsbn, dteX, dten,
                       decBX, decBn):
            wstA = pv(0, 128, 4096)
            wstB = pv(4096, 128, 4096)
            sc1 = pv(8192, 128, 1024)
            sc2 = pv(9216, 128, 1024)
            sc3 = pv(10240, 128, 1024)
            wintv = wint
            GtT = pv(12288, 32, 1024, F32)
            dtT = pv(14336, 32, 1024, F32)
            gTt = pv(16384, 32, 1024, F32)
            eGtT = pv(18432, 32, 1024)
            zb = pv(19456, 32, 512, F32)
            absz = pv(20480, 32, 512, F32)
            easb = pv(21504, 32, 512, F32)
            rel = pv(22528, 32, 512, F32)
            ek_sb = pv(23552, 32, 128, F32)

            def proj8(wblk, dst, dstname, is_k):
                for m in range(KVH):
                    par = m % 2
                    stripe = wstA if par == 0 else wstB
                    wb = "wstA" if par == 0 else "wstB"
                    dma("g", stripe[:, :], wblk[m, :, :], writes=[wb], sem="wsA" if par == 0 else "wsB")
                    pjn = [PB[par * 2], PB[par * 2 + 1]]
                    fns = []
                    for k in range(NK):
                        for nh in range(2):
                            fns.append(lambda en, k=k, nh=nh, par=par,
                                       stripe=stripe: en.matmul(
                                pb[par * 2 + nh][:, :],
                                stripe[:, k * 128:(k + 1) * 128],
                                xn_src[:, k * T + nh * 512:k * T + (nh + 1) * 512],
                                start=(k == 0), stop=(k == NK - 1)))
                    p.group("t", fns, reads=[wb, xnname], writes=pjn)
                    p.emit("s", lambda en, par=par: en.copy(
                        sc1[:, 0:512], pb[par * 2][:, :]),
                        reads=[pjn[0]], writes=["prs1"])
                    p.emit("s", lambda en, par=par: en.copy(
                        sc1[:, 512:1024], pb[par * 2 + 1][:, :]),
                        reads=[pjn[1]], writes=["prs1"])
                    if is_k:
                        fns = [(lambda en, nh=nh: en.matmul(
                            pb[4 + nh][:, :], S_Tc[:, :],
                            sc1[:, nh * 512:(nh + 1) * 512],
                            start=True, stop=True)) for nh in range(2)]
                        p.group("t", fns, reads=["prs1", "S_T"],
                                writes=["pb4", "pb5"])
                        for nh in range(2):
                            p.emit("v", lambda en, nh=nh: en.tensor_tensor(
                                sc2[:, nh * 512:(nh + 1) * 512], pb[4 + nh][:, :],
                                sinT[:, nh * 512:(nh + 1) * 512], ALU.mult),
                                reads=[PB[4 + nh], sname], writes=["prs2"])
                            p.emit("v", lambda en, nh=nh, par=par: en.tensor_tensor(
                                sc3[:, nh * 512:(nh + 1) * 512],
                                pb[par * 2 + nh][:, :],
                                cosT[:, nh * 512:(nh + 1) * 512], ALU.mult),
                                reads=[pjn[nh], cname], writes=["prs3"])
                        p.emit("v", lambda en: en.tensor_tensor(
                            sc3[:, :], sc3[:, :], sc2[:, :], ALU.add),
                            reads=["prs3", "prs2"], writes=["prs3"])
                    else:
                        pbf4 = pb[4][:, :].bitcast(BF16)
                        pbf5 = pb[5][:, :].bitcast(BF16)
                        fns = []
                        for j in range(8):
                            tgt = pbf4 if j < 4 else pbf5
                            fns.append(lambda en, j=j, tgt=tgt: en.transpose(
                                tgt[:, (j % 4) * 128:(j % 4 + 1) * 128],
                                sc1[:, j * 128:(j + 1) * 128], I_bf[:, :]))
                        p.group("t", fns, reads=["prs1", "I_bf"],
                                writes=["pb4", "pb5"])
                        p.emit("v", lambda en, pbf4=pbf4: en.tensor_copy(
                            sc3[:, 0:512], pbf4[:, 0:512]),
                            reads=["pb4"], writes=["prs3"])
                        p.emit("v", lambda en, pbf5=pbf5: en.tensor_copy(
                            sc3[:, 512:1024], pbf5[:, 0:512]),
                            reads=["pb5"], writes=["prs3"])
                    def sp(en, m=m, dst=dst):
                        return en.dma_start(out=bass.AP(
                            dst, m * 128,
                            [[KVH * 128, 128], [128 * KVH * 128, NCH], [1, 128]]),
                            in_=sc3[:, :].rearrange("p (c t) -> p c t", c=NCH))
                    p.emit("q", sp, reads=["prs3"], writes=[dstname], sem=dstname, inc=16)

            proj8(dWk, kdst, kdstn, True)
            proj8(dWv, vdst, vdstn, False)

            fns = []
            for k in range(NK):
                for nh in range(2):
                    fns.append(lambda en, k=k, nh=nh: en.matmul(
                        pb[nh][0:32, :], wint[:, k * 32:(k + 1) * 32],
                        xn_src[:, k * T + nh * 512:k * T + (nh + 1) * 512],
                        start=(k == 0), stop=(k == NK - 1)))
            p.group("t", fns, reads=["wint", xnname], writes=["pb0", "pb1"])
            for nh in range(2):
                sl = slice(nh * 512, (nh + 1) * 512)
                p.emit("v", lambda en, nh=nh: en.tensor_scalar(
                    zb, pb[nh][0:32, :], dtbias[:, :], None, ALU.add),
                    reads=[PB[nh], "dtbias"], writes=[f"{tag}zb"])
                p.emit("v", lambda en: en.tensor_scalar(
                    rel, zb, -1.0, None, ALU.mult),
                    reads=[f"{tag}zb"], writes=[f"{tag}rel"])
                p.emit("v", lambda en: en.tensor_tensor(
                    absz, zb, rel, ALU.max),
                    reads=[f"{tag}zb", f"{tag}rel"], writes=[f"{tag}ab"])
                p.emit("s", lambda en: en.activation(easb, absz, AF.Exp, scale=-1.0),
                       reads=[f"{tag}ab"], writes=[f"{tag}ea"])
                p.emit("s", lambda en: en.activation(easb, easb, AF.Ln, bias=c_one[0:32, :]),
                       reads=[f"{tag}ea", "c_one"], writes=[f"{tag}ea"])
                p.emit("v", lambda en: en.tensor_relu(rel, zb),
                       reads=[f"{tag}zb"], writes=[f"{tag}rel"])
                p.emit("v", lambda en, sl=sl: en.tensor_tensor(
                    dtT[:, sl], rel, easb, ALU.add),
                    reads=[f"{tag}rel", f"{tag}ea"], writes=["dtT"])
            p.emit("v", lambda en: en.tensor_scalar(
                gTt[:, :], dtT[:, :], expA[:, :], -1.0, ALU.mult, ALU.mult),
                reads=["dtT", "expA"], writes=["gTt"])
            for c in range(NCH):
                p.emit("v", lambda en, c=c: en.tensor_tensor_scan(
                    GtT[:, c * 128:(c + 1) * 128], gTt[:, c * 128:(c + 1) * 128],
                    zeros32[:, :], 0.0, ALU.add, ALU.add),
                    reads=["gTt", "zeros32"], writes=["GtT"])
            p.emit("s", lambda en: en.activation(eGtT[:, :], GtT[:, :], AF.Exp),
                   reads=["GtT"], writes=["eGtT"])
            for c in range(NCH):
                sl = slice(c * 128, (c + 1) * 128)
                cs = slice(c * 32, (c + 1) * 32)
                p.emit("t", lambda en, sl=sl: en.transpose(
                    pb[6][:, 0:32], GtT[:, sl], I_f[0:32, 0:32]),
                    reads=["GtT", "I_f"], writes=["pb6"])
                p.emit("s", lambda en, cs=cs: en.activation(
                    negG[:, cs], pb[6][:, 0:32], AF.Copy, scale=-1.0),
                    reads=["pb6"], writes=["negG"])
                p.emit("s", lambda en, sl=sl, c=c: en.activation(
                    ek_sb, GtT[:, sl], AF.Exp, scale=-1.0,
                    bias=GtT[:, c * 128 + 127:c * 128 + 128]),
                    reads=["GtT"], writes=[f"{tag}ek"])
                p.emit("t", lambda en: en.transpose(
                    pb[7][:, 0:32], ek_sb, I_f[0:32, 0:32]),
                    reads=[f"{tag}ek", "I_f"], writes=["pb7"])
                p.emit("s", lambda en, cs=cs: en.copy(ekd[:, cs], pb[7][:, 0:32]),
                       reads=["pb7"], writes=["ekd"])
                p.emit("t", lambda en, sl=sl: en.transpose(
                    pb[6][:, 0:32], dtT[:, sl], I_f[0:32, 0:32]),
                    reads=["dtT", "I_f"], writes=["pb6"])
                p.emit("s", lambda en, cs=cs: en.copy(dtsbX[:, cs], pb[6][:, 0:32]),
                       reads=["pb6"], writes=[dtsbn])
            p.emit("v", lambda en: en.tensor_tensor(
                dteX[:, :], dtsbX[:, :], ekd[:, :], ALU.mult),
                reads=[dtsbn, "ekd"], writes=[dten])
            dma("q", sGd.ap(), GtT[:, :], reads=["GtT"], writes=["Gd"])
            dma("q", sEGb.ap(), eGtT[:, :], reads=["eGtT"], writes=["EGd"])
            dma("q", sdec.ap(),
                bass.AP(pool, eGtT.offset + 127, [[POOLN, 32], [128, 8]]),
                reads=["eGtT"], writes=["decd"], slow=True)
            dma("g", decBX[:, :],
                bass.AP(sdec, 0, [[0, 128], [8, 32], [1, 8]]),
                reads=["decd"], writes=[decBn], sem=decBn)

        def phase_attn(with_y, tag, kspill, vspill, kspn, vspn,
                       scb, csb, ybanks, ktb, dtsbX, dtsbn, dteX, dten,
                       decBX, decBn):
            ktr = [pv(11520, 128, 128), pv(11648, 128, 128)]
            xc4 = [pv(11776, 128, 512), pv(12288, 128, 512)]
            xc24 = [pv(12800, 128, 512), pv(13312, 128, 512)]
            stbf_b = [pv(13824, 128, 512), pv(21504, 128, 512)]
            qhat = [pv(14336, 128, 512), pv(14848, 128, 512)]
            Gb = [pv(15360, 128, 512, F32), pv(16384, 128, 512, F32)]
            E_sb = [pv(17408, 128, 512), pv(17920, 128, 512)]
            dct = [pv(18432, 128, 512), pv(18944, 128, 512)]
            STs = [pv(19456, 128, 512), pv(19968, 128, 512)]
            ysq = pv(20480, 128, 512)
            z_sb = pv(20480, 128, 512)
            ysb = pv(20992, 128, 512)
            r1 = pv(22016, 1, 512, F32)
            rsr = pv(23040, 1, 512)

            def load_chunk(c):
                par = c % 2
                dma("q", kcb[par][:, :].rearrange("p (m t) -> p m t", m=KVH),
                    bass.AP(kspill, c * 128 * KVH * 128,
                            [[KVH * 128, 128], [128, KVH], [1, 128]]),
                    reads=[kspn], writes=[f"kc{par}"], sem=f"kc{par}")
                dma("q", vcb[par][:, :].rearrange("p (m t) -> p m t", m=KVH),
                    bass.AP(vspill, c * 128 * KVH * 128,
                            [[KVH * 128, 128], [128, KVH], [1, 128]]),
                    reads=[vspn], writes=[f"vc{par}"], sem=f"vc{par}")
                if with_y:
                    dma("q", qcb[par][:, :].rearrange("p (m t) -> p m t", m=H),
                        bass.AP(sqsp, c * 128 * H * 128,
                                [[H * 128, 128], [128, H], [1, 128]]),
                        reads=["qspd"], writes=[f"qc{par}"], sem=f"qc{par}")
                    dma("q", sgcb[par][:, :].rearrange("p (m t) -> p m t", m=H),
                        bass.AP(ssgsp, c * 128 * H * 128,
                                [[H * 128, 128], [128, H], [1, 128]]),
                        reads=["sgspd"], writes=[f"sgc{par}"], sem=f"sgc{par}")

            def load_ge(c, g):
                if not with_y:
                    return
                gp = g % 2
                h0 = 4 * g
                def gb_dma(en, c=c, h0=h0, gp=gp):
                    return en.dma_start(
                        out=Gb[gp][:, :].rearrange("p (j t) -> p j t", j=4),
                        in_=bass.AP(sGd, h0 * T + c * 128,
                                    [[0, 128], [T, 4], [1, 128]]))
                p.emit("g", gb_dma, reads=["Gd"], writes=[f"Gb{gp}"],
                       sem=f"Gb{gp}", inc=16)
                p.emit("v", lambda en, gp=gp: en.tensor_tensor(
                    Gb[gp][:, :], Gb[gp][:, :], maskc[:, :], ALU.add),
                    reads=[f"Gb{gp}", "maskc"], writes=[f"Gb{gp}"])
                def e_dma(en, c=c, h0=h0, gp=gp):
                    return en.dma_start(
                        out=E_sb[gp][:, :].rearrange("p (j t) -> p j t", j=4),
                        in_=bass.AP(sEGb, h0 * T + c * 128,
                                    [[0, 128], [T, 4], [1, 128]]))
                p.emit("g", e_dma, reads=["EGd"], writes=[f"E{gp}"],
                       sem=f"E{gp}", inc=16)

            load_chunk(0)
            load_ge(0, 0)
            for c in range(NCH):
                par = c % 2
                if c + 1 < NCH:
                    load_chunk(c + 1)
                k_c, v_c = kcb[par], vcb[par]
                q_c, sg_c = qcb[par], sgcb[par]
                kcn, vcn = f"kc{par}", f"vc{par}"
                qcn, sgcn = f"qc{par}", f"sgc{par}"
                for g in range(KVH):
                    if g + 1 < KVH:
                        load_ge(c, g + 1)
                    elif c + 1 < NCH:
                        load_ge(c + 1, 0)
                    gp = g % 2
                    h0 = 4 * g
                    sb = pb[scb[gp]]
                    cbk = pb[csb[gp]]
                    yb = pb[ybanks[gp]]
                    sbn = PB[scb[gp]]
                    cbn = PB[csb[gp]]
                    ybn = PB[ybanks[gp]]
                    ktpb = pb[ktb]
                    ktpbn = PB[ktb]
                    p.emit("t", lambda en, g=g, k_c=k_c, gp=gp, ktpb=ktpb:
                           en.transpose(
                        ktpb[:, :].bitcast(BF16)[:, gp * 128:gp * 128 + 128],
                        k_c[:, g * 128:(g + 1) * 128], I_bf[:, :]),
                        reads=[kcn, "I_bf"], writes=[ktpbn])
                    p.emit("s", lambda en, gp=gp, ktpb=ktpb: en.copy(
                        ktr[gp][:, :],
                        ktpb[:, :].bitcast(BF16)[:, gp * 128:gp * 128 + 128]),
                        reads=[ktpbn], writes=[f"ktr{gp}"])
                    vblk = bass.AP(vcb[par], g * 128,
                                   [[KVH * 128, 128], [0, 4], [1, 128]])
                    dtc = bass.AP(dtsbX, c * 32 + h0, [[256, 128], [1, 4], [0, 128]])
                    dtec = bass.AP(dteX, c * 32 + h0, [[256, 128], [1, 4], [0, 128]])
                    if with_y:
                        p.emit("v", lambda en, vblk=vblk, dtc=dtc, gp=gp:
                               en.tensor_tensor(
                            xc4[gp][:, :].rearrange("p (j t) -> p j t", j=4),
                            vblk, dtc, ALU.mult),
                            reads=[vcn, dtsbn], writes=[f"xc4{gp}"])
                    p.emit("v", lambda en, vblk=vblk, dtec=dtec, gp=gp:
                           en.tensor_tensor(
                        xc24[gp][:, :].rearrange("p (j t) -> p j t", j=4),
                        vblk, dtec, ALU.mult),
                        reads=[vcn, dten], writes=[f"xc24{gp}"])
                    if with_y:
                        fns = [(lambda en, hh=hh, g=g, sb=sb, h0=h0, k_c=k_c,
                                q_c=q_c: en.matmul(
                            sb[:, hh * 128:(hh + 1) * 128],
                            k_c[:, g * 128:(g + 1) * 128],
                            q_c[:, (h0 + hh) * 128:(h0 + hh + 1) * 128],
                            start=True, stop=True)) for hh in range(4)]
                        p.group("t", fns, reads=[kcn, qcn], writes=[sbn])
                        for hh in range(4):
                            p.emit("s", lambda en, hh=hh, c=c, h0=h0, gp=gp:
                                   en.activation(
                                dct[gp][:, hh * 128:(hh + 1) * 128],
                                Gb[gp][:, hh * 128:(hh + 1) * 128], AF.Exp,
                                bias=negG[:, c * 32 + h0 + hh:c * 32 + h0 + hh + 1]),
                                reads=[f"Gb{gp}", "negG"], writes=[f"dct{gp}"])
                        p.emit("v", lambda en, sb=sb, gp=gp: en.tensor_tensor(
                            STs[gp][:, :], sb[:, :], dct[gp][:, :], ALU.mult),
                            reads=[sbn, f"dct{gp}"], writes=[f"ST{gp}"])
                        p.emit("v", lambda en, h0=h0, gp=gp, q_c=q_c:
                               en.tensor_tensor(
                            qhat[gp][:, :], q_c[:, h0 * 128:(h0 + 4) * 128],
                            E_sb[gp][:, :], ALU.mult),
                            reads=[qcn, f"E{gp}"], writes=[f"qh{gp}"])
                    fns = [(lambda en, hh=hh, cbk=cbk, gp=gp: en.matmul(
                        cbk[:, hh * 128:(hh + 1) * 128], ktr[gp][:, :],
                        xc24[gp][:, hh * 128:(hh + 1) * 128],
                        start=True, stop=True)) for hh in range(4)]
                    p.group("t", fns, reads=[f"ktr{gp}", f"xc24{gp}"],
                            writes=[cbn])
                    if with_y:
                        p.emit("v", lambda en, h0=h0, gp=gp: en.tensor_copy(
                            stbf_b[gp][:, :], states[:, h0 * 128:(h0 + 4) * 128]),
                            reads=["states"], writes=[f"stbf{gp}"])
                        for hh in range(4):
                            h = h0 + hh
                            fns = [
                                (lambda en, hh=hh, yb=yb, gp=gp: en.matmul(
                                    yb[:, hh * 128:(hh + 1) * 128],
                                    xc4[gp][:, hh * 128:(hh + 1) * 128],
                                    STs[gp][:, hh * 128:(hh + 1) * 128],
                                    start=True, stop=False)),
                                (lambda en, hh=hh, yb=yb, gp=gp: en.matmul(
                                    yb[:, hh * 128:(hh + 1) * 128],
                                    stbf_b[gp][:, hh * 128:(hh + 1) * 128],
                                    qhat[gp][:, hh * 128:(hh + 1) * 128],
                                    start=False, stop=True)),
                            ]
                            p.group("t", fns,
                                    reads=[f"xc4{gp}", f"ST{gp}",
                                           f"stbf{gp}", f"qh{gp}"],
                                    writes=[ybn])
                            p.emit("v", lambda en, h=h, c=c, cbk=cbk, hh=hh:
                                   en.scalar_tensor_tensor(
                                states[:, h * 128:(h + 1) * 128],
                                states[:, h * 128:(h + 1) * 128],
                                decBX[:, h * 8 + c:h * 8 + c + 1],
                                cbk[:, hh * 128:(hh + 1) * 128],
                                ALU.mult, ALU.add),
                                reads=[decBn, cbn, f"stbf{gp}"],
                                writes=["states"])
                        p.emit("s", lambda en, yb=yb: en.activation(
                            ysq[:, :], yb[:, :], AF.Square),
                            reads=[ybn], writes=["ysq"])
                        p.emit("s", lambda en, yb=yb: en.activation(
                            ysb[:, :], yb[:, :], AF.Copy, scale=gnormc[:, :]),
                            reads=[ybn, "gnormc"], writes=["ysb"])
                        p.emit("t", lambda en, sb=sb: en.matmul(
                            sb[0:1, :], onesc[:, :], ysq[:, :],
                            start=True, stop=True),
                            reads=["ysq", "onesc"], writes=[sbn])
                        p.emit("s", lambda en, sb=sb: en.activation(
                            r1, sb[0:1, :], AF.Ln, scale=1.0 / 128.0,
                            bias=c_eps[0:1, :]),
                            reads=[sbn, "c_eps"], writes=["r1"])
                        p.emit("s", lambda en: en.activation(
                            rsr, r1, AF.Exp, scale=-0.5),
                            reads=["r1"], writes=["rsr"])
                        p.emit("t", lambda en, sb=sb: en.matmul(
                            sb[:, :], onesr[0:1, :], rsr[0:1, :],
                            start=True, stop=True),
                            reads=["rsr", "onesr"], writes=[sbn])
                        p.emit("v", lambda en, sb=sb: en.tensor_tensor(
                            z_sb[:, :], ysb[:, :], sb[:, :], ALU.mult),
                            reads=["ysb", sbn, "ysq"], writes=["ysq"])
                        def yfin(en, c=c, h0=h0, sg_c=sg_c):
                            dstap = bass.AP(bigA, h0 * T + c * 128,
                                            [[NK * T, 128], [T, 4], [1, 128]])
                            return en.tensor_tensor(
                                dstap,
                                z_sb[:, :].rearrange("p (j t) -> p j t", j=4),
                                sg_c[:, h0 * 128:(h0 + 4) * 128].rearrange(
                                    "p (j t) -> p j t", j=4), ALU.mult)
                        p.emit("v", yfin, reads=["ysq", sgcn],
                               writes=["yf"])
                    yield

        def phase_qgate(xn_src, xnname):
            wstA = pv(0, 128, 4096)
            wstB = pv(4096, 128, 4096)
            sc1 = pv(8192, 128, 1024)
            sc2 = pv(9216, 128, 1024)
            sc3 = pv(10240, 128, 1024)
            for m in range(H):
                par = m % 2
                stripe = wstA if par == 0 else wstB
                wb = "wstA" if par == 0 else "wstB"
                dma("g", stripe[:, :], dWq[m, :, :], writes=[wb], sem="wsA" if par == 0 else "wsB")
                pjn = [PB[par * 2], PB[par * 2 + 1]]
                fns = []
                for k in range(NK):
                    for nh in range(2):
                        fns.append(lambda en, k=k, nh=nh, stripe=stripe, par=par:
                                   en.matmul(
                            pb[par * 2 + nh][:, :],
                            stripe[:, k * 128:(k + 1) * 128],
                            xn_src[:, k * T + nh * 512:k * T + (nh + 1) * 512],
                            start=(k == 0), stop=(k == NK - 1)))
                p.group("t", fns, reads=[wb, xnname], writes=pjn)
                p.emit("s", lambda en, par=par: en.copy(
                    sc1[:, 0:512], pb[par * 2][:, :]), reads=[pjn[0]],
                    writes=["prs1"])
                p.emit("s", lambda en, par=par: en.copy(
                    sc1[:, 512:1024], pb[par * 2 + 1][:, :]), reads=[pjn[1]],
                    writes=["prs1"])
                fns = [(lambda en, nh=nh: en.matmul(
                    pb[4 + nh][:, :], S_Tc[:, :],
                    sc1[:, nh * 512:(nh + 1) * 512], start=True, stop=True))
                    for nh in range(2)]
                p.group("t", fns, reads=["prs1", "S_T"], writes=["pb4", "pb5"])
                for nh in range(2):
                    p.emit("v", lambda en, nh=nh: en.tensor_tensor(
                        sc2[:, nh * 512:(nh + 1) * 512], pb[4 + nh][:, :],
                        sinO[:, nh * 512:(nh + 1) * 512], ALU.mult),
                        reads=[PB[4 + nh], "sinO"], writes=["prs2"])
                    p.emit("v", lambda en, nh=nh, par=par: en.tensor_tensor(
                        sc3[:, nh * 512:(nh + 1) * 512], pb[par * 2 + nh][:, :],
                        cosO[:, nh * 512:(nh + 1) * 512], ALU.mult),
                        reads=[pjn[nh], "cosO"], writes=["prs3"])
                p.emit("v", lambda en: en.tensor_tensor(
                    sc3[:, :], sc3[:, :], sc2[:, :], ALU.add),
                    reads=["prs3", "prs2"], writes=["prs3"])
                def spq(en, m=m):
                    return en.dma_start(out=bass.AP(
                        sqsp, m * 128,
                        [[H * 128, 128], [128 * H * 128, NCH], [1, 128]]),
                        in_=sc3[:, :].rearrange("p (c t) -> p c t", c=NCH))
                p.emit("q", spq, reads=["prs3"], writes=["qspd"], sem="qspd", inc=16)
                yield
            for m in range(H):
                par = m % 2
                stripe = wstA if par == 0 else wstB
                wb = "wstA" if par == 0 else "wstB"
                dma("g", stripe[:, :], dWg[m, :, :], writes=[wb], sem="wsA" if par == 0 else "wsB")
                pjn = [PB[par * 2], PB[par * 2 + 1]]
                fns = []
                for k in range(NK):
                    for nh in range(2):
                        fns.append(lambda en, k=k, nh=nh, stripe=stripe, par=par:
                                   en.matmul(
                            pb[par * 2 + nh][:, :],
                            stripe[:, k * 128:(k + 1) * 128],
                            xn_src[:, k * T + nh * 512:k * T + (nh + 1) * 512],
                            start=(k == 0), stop=(k == NK - 1)))
                p.group("t", fns, reads=[wb, xnname], writes=pjn)
                p.emit("s", lambda en, par=par: en.activation(
                    sc3[:, 0:512], pb[par * 2][:, :], AF.Silu),
                    reads=[pjn[0]], writes=["prs3"])
                p.emit("s", lambda en, par=par: en.activation(
                    sc3[:, 512:1024], pb[par * 2 + 1][:, :], AF.Silu),
                    reads=[pjn[1]], writes=["prs3"])
                def spg(en, m=m):
                    return en.dma_start(out=bass.AP(
                        ssgsp, m * 128,
                        [[H * 128, 128], [128 * H * 128, NCH], [1, 128]]),
                        in_=sc3[:, :].rearrange("p (c t) -> p c t", c=NCH))
                p.emit("q", spg, reads=["prs3"], writes=["sgspd"], sem="sgspd", inc=16)
                yield

        def phase_ffnpart(xn_src, xnname):
            stf = [pv(0, 128, 4096), pv(4096, 128, 4096)]
            xTm = pv(8192, 128, 1024, F32)
            t2 = pv(10240, 128, 512, F32)
            outst = pv(11264, 128, 512, F32)
            for m in range(NK):
                par = m % 2
                fb_ = "wstA" if par == 0 else "wstB"
                dma("g", stf[par][:, :], dWf[m, :, :], writes=[fb_],
                    sem="wsA" if par == 0 else "wsB")
                dma("q", xTm[:, :], dxT[m * 128:(m + 1) * 128, :],
                    writes=["o_xT"], sem="o_xT")
                pjf = [PB[par * 2], PB[par * 2 + 1]]
                fns = []
                for k in range(NK):
                    for nh in range(2):
                        fns.append(lambda en, k=k, nh=nh, par=par: en.matmul(
                            pb[par * 2 + nh][:, :],
                            stf[par][:, k * 128:(k + 1) * 128],
                            xn_src[:, k * T + nh * 512:k * T + (nh + 1) * 512],
                            start=(k == 0), stop=(k == NK - 1)))
                p.group("t", fns, reads=[fb_, xnname], writes=pjf)
                for nh in range(2):
                    p.emit("v", lambda en, m=m, nh=nh, par=par:
                           en.scalar_tensor_tensor(
                        t2, pb[par * 2 + nh][:, :], rowselc[:, m:m + 1],
                        qselB[:, nh * 512:(nh + 1) * 512],
                        ALU.mult, ALU.mult),
                        reads=[pjf[nh], "rowselc", "qselB"], writes=["o_t2"])
                    p.emit("v", lambda en, m=m, nh=nh: en.scalar_tensor_tensor(
                        t2, qselB[:, nh * 512:(nh + 1) * 512],
                        biasrs[:, m:m + 1], t2, ALU.mult, ALU.add),
                        reads=["qselB", "biasrs", "o_t2"], writes=["o_t2"])
                    p.emit("v", lambda en, nh=nh: en.tensor_tensor(
                        outst, t2, xTm[:, nh * 512:(nh + 1) * 512], ALU.add),
                        reads=["o_t2", "o_xT"], writes=["o_out"])
                    dma("q", spart.ap()[m * 128:(m + 1) * 128,
                                        nh * 512:(nh + 1) * 512],
                        outst, reads=["o_out"], writes=["partd"], sem="partd")
                yield

        def phase_wo():
            stw = [pv(0, 128, 4096), pv(4096, 128, 4096)]
            prt = pv(16384, 128, 1024, F32)
            t4 = pv(20480, 128, 512, F32)
            outst = pv(21504, 128, 512, F32)
            for m in range(NK):
                par = m % 2
                wb_ = "wstA" if par == 0 else "wstB"
                dma("g", stw[par][:, :], dWo[m, :, :], writes=[wb_], sem="wsA" if par == 0 else "wsB")
                dma("q", prt[:, :], spart.ap()[m * 128:(m + 1) * 128, :],
                    reads=["partd"], writes=["o_prt"])
                pjw = [PB[par * 2], PB[par * 2 + 1]]
                fns = []
                for k in range(NK):
                    for nh in range(2):
                        fns.append(lambda en, k=k, nh=nh, par=par: en.matmul(
                            pb[par * 2 + nh][:, :],
                            stw[par][:, k * 128:(k + 1) * 128],
                            bigA[:, k * T + nh * 512:k * T + (nh + 1) * 512],
                            start=(k == 0), stop=(k == NK - 1)))
                p.group("t", fns, reads=[wb_, "yf"], writes=pjw)
                for nh in range(2):
                    p.emit("v", lambda en, nh=nh, par=par: en.scalar_tensor_tensor(
                        t4, pb[par * 2 + nh][:, :], 1.0,
                        prt[:, nh * 512:(nh + 1) * 512], ALU.mult, ALU.add),
                        reads=[pjw[nh], "o_prt"], writes=["o_t4"])
                    p.emit("v", lambda en: en.tensor_copy(outst, t4),
                           reads=["o_t4"], writes=["o_out"])
                    dma("q", dout[m * 128:(m + 1) * 128, nh * 512:(nh + 1) * 512],
                        outst, reads=["o_out"], writes=["outd"])

        def phase_debug():
            if not DEBUG:
                return
            for n, t_ in [("dmax", dmax), ("selc", selc), ("rowselc", rowselc),
                          ("qselB", qselB), ("biasrs", biasrs), ("qrowf", qrowf),
                          ("states", states), ("dtsb", dtsb), ("decB", decB),
                          ("negG", negG), ("ekd", ekd)]:
                if t_.dtype if False else True:
                    pass
                def mk(en, n=n, t_=t_):
                    return en.dma_start(out=ddbg[n], in_=t_[:, :])
                p.emit("g", mk, reads=[n if n in p.bufs else "outd"],
                       extra_deps=[("v", p.cnt["v"]), ("s", p.cnt["s"]),
                                   ("t", p.cnt["t"])], sem="dbgd", inc=16)

        def run2(ga, gb):
            ga, gb = iter(ga), iter(gb)
            da = db = False
            while not (da and db):
                if not da:
                    try:
                        next(ga)
                    except StopIteration:
                        da = True
                if not db:
                    try:
                        next(gb)
                    except StopIteration:
                        db = True

        def drain(g):
            for _ in g:
                pass

        phase_ln(dxw, bigA, "xnA", "lw")
        p.fence()
        phase_kvdt(bigA[:, :], "xnA", cosW, "cosW", sinW, "sinW", "w",
                   sksp_w, "kspw", svsp_w, "vspw",
                   dtsbW, "dtsbW", dteW, "dteW", decBW, "decBW")
        p.fence()
        phase_ln(dx, bigA, "xnA", "lo")
        p.fence()
        phase_kvdt(bigA[:, :], "xnA", cosO, "cosO", sinO, "sinO", "o",
                   sksp, "ksp", svsp, "vsp",
                   dtsb, "dtsb", dte, "dte", decB, "decB")
        p.fence()
        drain(phase_attn(False, "wa", sksp_w, svsp_w, "kspw", "vspw",
                         [0, 1], [0, 1], [0, 1], 2,
                         dtsbW, "dtsbW", dteW, "dteW", decBW, "decBW"))
        p.fence()
        drain(phase_qgate(bigA[:, :], "xnA"))
        p.fence()
        drain(phase_p1())
        p.fence()
        drain(phase_ffnpart(bigA[:, :], "xnA"))
        p.fence()
        drain(phase_attn(True, "oa", sksp, svsp, "ksp", "vsp",
                         [0, 1], [2, 3], [4, 5], 6,
                         dtsb, "dtsb", dte, "dte", decB, "decB"))
        p.fence()
        phase_wo()
        phase_debug()
        with nc.Block() as block:
            p.replay(nc, es, block)

    return nc


_CACHE = {}


def _blk(w, n_m):
    return np.ascontiguousarray(
        w.reshape(n_m, 128, NK, 128).transpose(0, 3, 2, 1).reshape(n_m, 128, HID))


def _rope_tables(pos):
    inv = 1.0 / ROPE_THETA ** (np.arange(0, D, 2, dtype=np.float64) / D)
    ang = np.concatenate([inv, inv])[:, None] * pos[None, :].astype(np.float64)
    return (np.ascontiguousarray(np.cos(ang).astype(np.float32)),
            np.ascontiguousarray(np.sin(ang).astype(np.float32)))


def kernel(L0_attn_hidden_states, L0_attn_Wq, L0_attn_Wk, L0_attn_Wv,
           L0_attn_Wo, L0_attn_Wg, L0_attn_Win, L0_attn_dt_bias,
           L0_attn_A_log_bias, L0_attn_gnorm_w, L0_ffn_weight, L0_ffn_bias,
           L0_ffn_centroids, L0_ffn_assignments):
    x = np.ascontiguousarray(np.asarray(L0_attn_hidden_states, dtype=np.float32))
    Wq = np.asarray(L0_attn_Wq, np.float32)
    Wk = np.asarray(L0_attn_Wk, np.float32)
    Wv = np.asarray(L0_attn_Wv, np.float32)
    Wo = np.asarray(L0_attn_Wo, np.float32)
    Wg = np.asarray(L0_attn_Wg, np.float32)
    Win = np.asarray(L0_attn_Win, np.float32)
    dtb = np.asarray(L0_attn_dt_bias, np.float32)
    Alog = np.asarray(L0_attn_A_log_bias, np.float32)
    gnw = np.asarray(L0_attn_gnorm_w, np.float32)
    Wf = np.asarray(L0_ffn_weight, np.float32)
    fb = np.asarray(L0_ffn_bias, np.float32)
    cent = np.asarray(L0_ffn_centroids, np.float32)
    assign = np.asarray(L0_ffn_assignments, np.int32)

    if "nc" not in _CACHE:
        _CACHE["nc"] = build_program()
    nc = _CACHE["nc"]

    xTall = np.ascontiguousarray(x.reshape(ALLT, HID).T)
    Wq_b = _blk(Wq, 32)
    Wk_b = _blk(Wk, 8)
    Wv_b = _blk(Wv, 8)
    Wg_b = _blk(Wg, 32)
    Wo_b = _blk(Wo, 32)
    Wf_b = _blk(Wf, 32)
    Win_t = np.ascontiguousarray(Win.T)
    CWO = np.ascontiguousarray(
        np.concatenate([cent.T, np.ones((HID, 1), np.float32)], axis=1))
    fb_c = np.ascontiguousarray(fb.reshape(32, 128).T)
    ident = np.eye(128, dtype=np.float32)
    S = np.zeros((128, 128), np.float32)
    S[np.arange(64), np.arange(64) + 64] = -1.0
    S[np.arange(64) + 64, np.arange(64)] = 1.0
    S_T = np.ascontiguousarray(S.T)
    mask1 = np.where(np.arange(128)[None, :] >= np.arange(128)[:, None],
                     0.0, NEG_BIG).astype(np.float32)
    mask4 = np.ascontiguousarray(np.tile(mask1, (1, 4)))
    iota = np.arange(64, dtype=np.float32).reshape(64, 1)

    in_maps = []
    for c in range(NCORES):
        b, s = divmod(c, 4)
        t0 = s * T
        cos_o, sin_o = _rope_tables(np.arange(t0, t0 + T))
        w0 = max(s - 1, 0) * T
        cos_w, sin_w = _rope_tables(np.arange(w0, w0 + T))
        xw = (np.zeros((T, HID), np.float32) if s == 0
              else np.ascontiguousarray(x[b, t0 - T:t0, :]))
        in_maps.append({
            "x_own": np.ascontiguousarray(x[b, t0:t0 + T, :]),
            "x_win": xw,
            "xT_own": np.ascontiguousarray(x[b, t0:t0 + T, :].T),
            "xT_all": xTall,
            "Wq_blk": Wq_b, "Wk_blk": Wk_b, "Wv_blk": Wv_b,
            "Wg_blk": Wg_b, "Wo_blk": Wo_b, "Wf_blk": Wf_b,
            "Win_t": Win_t, "CWO": CWO,
            "ffn_b": fb_c,
            "dt_b": dtb.reshape(32, 1),
            "A_log": Alog.reshape(32, 1),
            "gnorm": gnw.reshape(128, 1),
            "assign": assign.reshape(1, HID),
            "ident": ident, "S_T": S_T, "maskneg4": mask4,
            "cos_own": cos_o, "sin_own": sin_o,
            "cos_win": cos_w, "sin_win": sin_w,
            "iota64_in": iota,
            "own_off": np.array([[b * 32 + s * 8]], np.int32),
        })

    res = run_bass_kernel_spmd(nc, in_maps, core_ids=list(range(NCORES)))
    out = np.empty((B, L, HID), np.float32)
    for c in range(NCORES):
        b, s = divmod(c, 4)
        out[b, s * T:(s + 1) * T, :] = res.results[c]["out"].T
    return out
